# version 05
# baseline (speedup 1.0000x reference)
import os
import numpy as np
import ml_dtypes
from contextlib import ExitStack

_PARANOID = os.environ.get("K_PARANOID", "") == "1"

import concourse.bass as bass
import concourse.mybir as mybir
from concourse.bass_utils import run_bass_kernel_spmd

F32 = mybir.dt.float32
BF16 = mybir.dt.bfloat16
FP8 = mybir.dt.float8e5
AF = mybir.ActivationFunctionType
ALU = mybir.AluOpType

H = 512
CH = 4096
NXC = 5120
TA = 512
NT = 10
TB = 256
TC = 256
EPS = 1e-5
KCLAMP = 60.0
NCORES = 8
C514 = 514

TM, CM, TMF, CMF, G1, B1, G2, B2 = (i * 8 for i in range(8))
D1M, D1MC, D2M, D2MC, ED, EU = (i * 8 for i in range(6))


def _threefry2x32(k1, k2, x1, x2):
    x = [x1.astype(np.uint32).copy(), x2.astype(np.uint32).copy()]
    rot = [(13, 15, 26, 6), (17, 29, 16, 24)]
    ks = [np.uint32(k1), np.uint32(k2),
          np.uint32(k1) ^ np.uint32(k2) ^ np.uint32(0x1BD11BDA)]

    def rotl(v, r):
        return (v << np.uint32(r)) | (v >> np.uint32(32 - r))

    with np.errstate(over="ignore"):
        x[0] = x[0] + ks[0]
        x[1] = x[1] + ks[1]
        for i in range(5):
            for r in rot[i % 2]:
                x[0] = x[0] + x[1]
                x[1] = x[0] ^ rotl(x[1], r)
            x[0] = x[0] + ks[(i + 1) % 3]
            x[1] = x[1] + ks[(i + 2) % 3] + np.uint32(i + 1)
    return x[0], x[1]


def _bernoulli_mask(seed, p, shape):
    n = int(np.prod(shape))
    idx = np.arange(n, dtype=np.uint64)
    c1 = (idx >> np.uint64(32)).astype(np.uint32)
    c2 = (idx & np.uint64(0xFFFFFFFF)).astype(np.uint32)
    b1, b2 = _threefry2x32(np.uint32(seed >> 32), np.uint32(seed & 0xFFFFFFFF),
                           c1, c2)
    bits = b1 ^ b2
    fb = (bits >> np.uint32(9)) | np.uint32(0x3F800000)
    u = fb.view(np.float32) - np.float32(1.0)
    return (u < np.float32(p)).reshape(shape)


def _tname(t):
    if hasattr(t, "tensor"):
        t = t.tensor
    return t.name


class Sched:
    ENGS = ("sync", "tensor", "vector", "scalar", "gpsimd")
    COMPUTE = {"tensor", "vector", "scalar"}
    SEM_OF = {"sync": "LX", "tensor": "T", "vector": "V", "scalar": "S",
              "gpsimd": "G"}
    NROT = {"sync": 6, "gpsimd": 8}

    def __init__(self):
        self.ops = []
        self.state = {}
        self.epos = {e: 0 for e in self.ENGS}
        self.fence_deps = []

    def op(self, eng, fn, reads=(), writes=(), sreads=(), dma=False):
        idx = len(self.ops)
        deps = set(self.fence_deps)
        if _PARANOID and idx > 0:
            deps.add(idx - 1)
        sdeps = set()
        for t in reads:
            st = self.state.get(_tname(t))
            if st and st["w"] is not None:
                deps.add(st["w"])
        for t in sreads:
            st = self.state.get(_tname(t))
            if st and st["w"] is not None:
                sdeps.add(st["w"])
        for t in writes:
            st = self.state.get(_tname(t))
            if st:
                if st["w"] is not None:
                    deps.add(st["w"])
                deps.update(st["r"])
        o = dict(idx=idx, eng=eng, fn=fn, deps=deps, sdeps=sdeps, dma=dma,
                 pos=self.epos[eng], inc_needed=dma, lm=None)
        self.epos[eng] += 1
        self.ops.append(o)
        for t in list(reads) + list(sreads):
            self.state.setdefault(_tname(t), {"w": None, "r": []})["r"].append(idx)
        for t in writes:
            self.state[_tname(t)] = {"w": idx, "r": []}
        return idx

    def fence(self):
        last = {}
        for o in self.ops:
            last[o["eng"]] = o["idx"]
        self.fence_deps = list(last.values())

    def finalize(self):
        rot = {"sync": 0, "gpsimd": 0}
        for o in self.ops:
            if o["dma"]:
                i = rot[o["eng"]]
                o["sem"] = self.SEM_OF[o["eng"]] + str(i)
                rot[o["eng"]] = (i + 1) % self.NROT[o["eng"]]
            else:
                o["sem"] = self.SEM_OF[o["eng"]]
        for o in self.ops:
            waits = {}
            for d in o["deps"]:
                dop = self.ops[d]
                if (dop["eng"] == o["eng"] and o["eng"] in self.COMPUTE
                        and not dop["dma"]):
                    continue
                waits[d] = True
            for d in o["sdeps"]:
                dop = self.ops[d]
                same = (dop["eng"] == o["eng"] and o["eng"] in self.COMPUTE
                        and not dop["dma"])
                if same and (o["pos"] - dop["pos"]) >= 8:
                    continue
                waits[d] = True
            o["waitset"] = list(waits)
            for d in waits:
                self.ops[d]["inc_needed"] = True
        cnt = {}
        for o in self.ops:
            if o["dma"]:
                o["inc_needed"] = True
            if o["inc_needed"]:
                cnt[o["sem"]] = cnt.get(o["sem"], 0) + (16 if o["dma"] else 1)
                o["lm"] = cnt[o["sem"]]
        self.final_cnt = cnt

    def flush(self, block, sems):
        lastw = {e: {} for e in self.ENGS}
        seqs = {e: [] for e in self.ENGS}
        for o in self.ops:
            seqs[o["eng"]].append(o)
        for ename in self.ENGS:
            ops = seqs[ename]
            if not ops:
                continue

            def body(e, ops=ops, ename=ename):
                lw = lastw[ename]
                for o in ops:
                    wmax = {}
                    if o["dma"] and o["lm"] > 16:
                        wmax[o["sem"]] = o["lm"] - 16
                    for d in o["waitset"]:
                        dop = self.ops[d]
                        assert dop["lm"] is not None
                        wmax[dop["sem"]] = max(wmax.get(dop["sem"], 0),
                                               dop["lm"])
                    twait = False
                    for sn, v in wmax.items():
                        if lw.get(sn, 0) < v:
                            e.wait_ge(sems[sn], v)
                            lw[sn] = v
                            if sn == "T":
                                twait = True
                    ins = o["fn"](e)
                    if o["inc_needed"]:
                        ins.then_inc(sems[o["sem"]], 16 if o["dma"] else 1)

            getattr(block, ename)(body)


def _build():
    nc = bass.Bass("TRN2", target_bir_lowering=False,
                   detect_race_conditions=False)

    xh = nc.dram_tensor("xh", [1024, NXC], F32, kind="ExternalInput").ap()
    wkt = nc.dram_tensor("wkt", [1024, 1024], F32, kind="ExternalInput").ap()
    wvt = nc.dram_tensor("wvt", [1024, 1024], F32, kind="ExternalInput").ap()
    wrt = nc.dram_tensor("wrt", [1024, 1024], F32, kind="ExternalInput").ap()
    wot = nc.dram_tensor("wot", [1024, 1024], F32, kind="ExternalInput").ap()
    frt = nc.dram_tensor("frt", [1024, 1024], F32, kind="ExternalInput").ap()
    fkt = nc.dram_tensor("fkt", [1024, 4096], F32, kind="ExternalInput").ap()
    fvt = nc.dram_tensor("fvt", [4096, 1024], F32, kind="ExternalInput").ap()
    w1t = nc.dram_tensor("w1t", [1024, 4096], F32, kind="ExternalInput").ap()
    w2t = nc.dram_tensor("w2t", [4096, 1024], F32, kind="ExternalInput").ap()
    craw_d = nc.dram_tensor("craw", [128, 64], F32, kind="ExternalInput").ap()
    ctd_d = nc.dram_tensor("ctd", [128, 16], F32, kind="ExternalInput").ap()
    b1r_d = nc.dram_tensor("b1r", [128, 32], F32, kind="ExternalInput").ap()
    b2r_d = nc.dram_tensor("b2r", [128, 8], F32, kind="ExternalInput").ap()
    flg_d = nc.dram_tensor("flg", [128, 2], F32, kind="ExternalInput").ap()
    mask_d = nc.dram_tensor("maskv", [1024, CH], BF16, kind="ExternalInput").ap()
    out_d = nc.dram_tensor("out", [1024, CH], F32, kind="ExternalOutput").ap()
    xmf_s = nc.dram_tensor("xmf_s", [1024, CH], BF16).ap()
    x2_s = nc.dram_tensor("x2_s", [1024, CH], BF16).ap()
    attn_s = nc.dram_tensor("attn_s", [1024, CH], BF16).ap()
    stat_s = nc.dram_tensor("stat_s", [2, CH], F32).ap()

    def hbm8(ap, nchunk):
        return ap.rearrange("(c p) m -> p c m", p=128)

    def ch8(buf, nc_=8):
        return buf[:].rearrange("p (c m) -> p c m", c=nc_)

    sch = Sched()
    top = ExitStack()
    with top:
        semnames = (["V", "S", "T"]
                    + ["LX%d" % i for i in range(Sched.NROT["sync"])]
                    + ["G%d" % i for i in range(Sched.NROT["gpsimd"])])
        sems = {nm: top.enter_context(nc.semaphore(nm)) for nm in semnames}
        pb = [top.enter_context(nc.psum_tensor(f"pb{i}", [128, 512], F32))
              for i in range(8)]

        craw = top.enter_context(nc.sbuf_tensor("craw_s", [128, 64], F32))
        cder = top.enter_context(nc.sbuf_tensor("cder_s", [128, 64], F32))
        ctd = top.enter_context(nc.sbuf_tensor("ctd_s", [128, 16], F32))
        b1q = top.enter_context(nc.sbuf_tensor("b1q", [128, 32], F32))
        b2q = top.enter_context(nc.sbuf_tensor("b2q", [128, 8], F32))
        q8t = top.enter_context(nc.sbuf_tensor("q8t", [128, 40], FP8))
        flg = top.enter_context(nc.sbuf_tensor("flg_s", [128, 2], F32))
        onesF = top.enter_context(nc.sbuf_tensor("onesF", [128, 1], F32))
        onesB = top.enter_context(nc.sbuf_tensor("onesB", [128, 1], BF16))
        onesR = top.enter_context(nc.sbuf_tensor("onesR", [64, 128], F32))
        cnst = top.enter_context(nc.sbuf_tensor("cnst", [128, 2], F32))

        sch.op("sync", lambda e: e.dma_start(out=craw[:], in_=craw_d),
               writes=[craw], dma=True)
        sch.op("sync", lambda e: e.dma_start(out=ctd[:], in_=ctd_d),
               writes=[ctd], dma=True)
        sch.op("sync", lambda e: e.dma_start(out=b1q[:], in_=b1r_d),
               writes=[b1q], dma=True)
        sch.op("sync", lambda e: e.dma_start(out=b2q[:], in_=b2r_d),
               writes=[b2q], dma=True)
        sch.op("sync", lambda e: e.dma_start(out=flg[:], in_=flg_d),
               writes=[flg], dma=True)
        sch.op("vector", lambda e: e.memset(cnst[:, 0:1], EPS), writes=[cnst])
        sch.op("vector", lambda e: e.memset(cnst[:, 1:2], 1e-35), writes=[cnst])
        nc.const_aps.register(F32, EPS, cnst[:, 0:1])
        nc.const_aps.register(F32, 1e-35, cnst[:, 1:2])
        sch.op("vector", lambda e: e.memset(onesF[:], 1.0), writes=[onesF])
        sch.op("vector", lambda e: e.memset(onesB[:], 1.0), writes=[onesB])
        sch.op("vector", lambda e: e.memset(onesR[:], 1.0), writes=[onesR])
        def _derive(final):
            if final:
                sch.op("vector", lambda e: e.tensor_copy(q8t[:, 0:32], b1q[:]),
                       reads=[b1q], writes=[q8t])
                sch.op("vector", lambda e: e.tensor_copy(q8t[:, 32:40], b2q[:]),
                       reads=[b2q], writes=[q8t])
            sch.op("vector", lambda e: e.tensor_scalar(
                cder[:, D1M:D1M + 8], craw[:, TM:TM + 8], -1.0, 1.0,
                ALU.mult, ALU.add), reads=[craw], writes=[cder])
            sch.op("vector", lambda e: e.tensor_tensor(
                cder[:, D1MC:D1MC + 8], cder[:, D1M:D1M + 8],
                craw[:, CM:CM + 8], ALU.add), reads=[craw, cder], writes=[cder])
            sch.op("vector", lambda e: e.tensor_scalar(
                cder[:, D2M:D2M + 8], craw[:, TMF:TMF + 8], -1.0, 1.0,
                ALU.mult, ALU.add), reads=[craw], writes=[cder])
            sch.op("vector", lambda e: e.tensor_tensor(
                cder[:, D2MC:D2MC + 8], cder[:, D2M:D2M + 8],
                craw[:, CMF:CMF + 8], ALU.add), reads=[craw, cder], writes=[cder])
            if final:
                sch.op("vector", lambda e: e.memset(cnst[:, 0:1], EPS),
                       writes=[cnst])
                sch.op("vector", lambda e: e.memset(cnst[:, 1:2], 1e-35),
                       writes=[cnst])
                sch.op("vector", lambda e: e.memset(onesF[:], 1.0),
                       writes=[onesF])
                sch.op("vector", lambda e: e.memset(onesB[:], 1.0),
                       writes=[onesB])
                sch.op("vector", lambda e: e.memset(onesR[:], 1.0),
                       writes=[onesR])
                sch.op("vector", lambda e: e.tensor_copy(b1q[:], q8t[:, 0:32]),
                       reads=[q8t], writes=[b1q])
                sch.op("vector", lambda e: e.tensor_copy(b2q[:], q8t[:, 32:40]),
                       reads=[q8t], writes=[b2q])
            sch.op("scalar", lambda e: e.activation(
                cder[:, ED:ED + 8], ctd[:, 0:8], AF.Exp),
                reads=[ctd], writes=[cder])
            sch.op("scalar", lambda e: e.activation(
                cder[:, ED:ED + 8], cder[:, ED:ED + 8], AF.Exp, scale=-1.0),
                reads=[cder], writes=[cder])
            sch.op("scalar", lambda e: e.activation(
                cder[:, EU:EU + 8], ctd[:, 8:16], AF.Exp),
                reads=[ctd], writes=[cder])

        _derive(final=False)
        sch.fence()
        _derive(final=True)
        sch.fence()

        with ExitStack() as ea:
            sb = lambda nme, sh, dt: ea.enter_context(
                nc.sbuf_tensor(nme, sh, dt))
            wk_b = sb("wk_b", [128, 8192], BF16)
            wv_b = sb("wv_b", [128, 8192], BF16)
            wr_b = sb("wr_b", [128, 8192], BF16)
            wo_b = sb("wo_b", [128, 8192], BF16)
            xt0 = sb("xt0", [128, 8 * TA], F32)
            xt = [xt0, xt0]
            sqf = sb("sqf", [128, TA], F32)
            sqb2 = sb("sqb2", [128, 2 * TA], BF16)
            x1b = [sb(f"x1b{i}", [128, 8 * C514], F32) for i in range(2)]
            x2b = [sb(f"x2b{i}", [128, 8 * C514], F32) for i in range(2)]
            xma = sb("xma", [128, 8 * TA], BF16)
            tmin = sb("tmin", [128, 8 * TA], BF16)
            xmfb = sb("xmfb", [128, 8 * TA], BF16)
            mixt = sb("mixt", [128, TA], F32)
            mixt2 = sb("mixt2", [128, TA], F32)
            tmpf = sb("tmpf", [128, TA], F32)
            tmpg = tmpf
            kbuf = sb("kbuf", [128, TA], F32)
            ebuf = sb("ebuf", [128, TA], F32)
            ubuf = sb("ubuf", [128, TA], F32)
            sigb = sb("sigb", [128, TA], F32)
            srnb = sb("srnb", [128, TA], F32)
            sab = sb("sab", [128, 513], F32)
            sbb = sb("sbb", [128, 513], F32)
            aCar = sb("aCar", [128, 8], F32)
            bCar = sb("bCar", [128, 8], F32)
            rowsc = sb("rowsc", [64, 5 * TA], F32)
            rsave = sb("rsave", [64, 2 * TA], F32)

            for t_ in (aCar, bCar, sab, sbb, x1b[0], x1b[1], x2b[0], x2b[1]):
                sch.op("vector", lambda e, t_=t_: e.memset(t_[:], 0.0),
                       writes=[t_])

            for dst, src in ((wk_b, wkt), (wv_b, wvt), (wr_b, wrt),
                             (wo_b, wot)):
                sch.op("gpsimd",
                       lambda e, dst=dst, src=src: e.dma_start(
                           out=ch8(dst), in_=hbm8(src, 8)),
                       writes=[dst], dma=True)

            def x1_stage(t):
                c0 = TA * t
                xbuf = xt[t % 2]
                sch.op("sync", lambda e, xbuf=xbuf, c0=c0: e.dma_start(
                    out=ch8(xbuf), in_=hbm8(xh, 8)[:, :, c0:c0 + TA]),
                    writes=[xbuf], dma=True)
                for c in range(8):
                    sl = 0
                    sch.op("scalar", lambda e, c=c, sl=sl, xbuf=xbuf:
                        e.activation(sqf[:, sl:sl + TA],
                                     xbuf[:, c * TA:(c + 1) * TA], AF.Square),
                        reads=[xbuf], writes=[sqf])
                    sch.op("tensor", lambda e, c=c, xbuf=xbuf: e.matmul(
                        pb[0].ap()[0:1, 0:TA], onesF[:, 0:1],
                        xbuf[:, c * TA:(c + 1) * TA],
                        start=(c == 0), stop=(c == 7)),
                        reads=[onesF, xbuf], writes=[pb[0]])
                    sch.op("tensor", lambda e, c=c, sl=sl: e.matmul(
                        pb[1].ap()[0:1, 0:TA], onesF[:, 0:1],
                        sqf[:, sl:sl + TA],
                        start=(c == 0), stop=(c == 7)),
                        reads=[onesF, sqf], writes=[pb[1]])
                P0 = 0
                R = lambda i: rowsc[P0:P0 + 1, i * TA:(i + 1) * TA]
                sch.op("vector", lambda e, R=R: e.tensor_scalar(
                    R(0), pb[0].ap()[0:1, 0:TA], 1.0 / 1024, None,
                    ALU.mult), reads=[pb[0]], writes=[rowsc])
                sch.op("vector", lambda e, R=R: e.tensor_scalar(
                    R(1), pb[1].ap()[0:1, 0:TA], 1.0 / 1024, None,
                    ALU.mult), reads=[pb[1]], writes=[rowsc])
                sch.op("vector", lambda e, R=R: e.tensor_tensor(
                    R(2), R(0), R(0), ALU.mult),
                    reads=[rowsc], writes=[rowsc])
                sch.op("vector", lambda e, R=R: e.tensor_tensor(
                    R(1), R(1), R(2), ALU.subtract),
                    reads=[rowsc], writes=[rowsc])
                sch.op("scalar", lambda e, R=R: e.activation(
                    R(2), R(1), AF.Ln, bias=EPS),
                    reads=[rowsc], sreads=[cnst], writes=[rowsc])
                sch.op("scalar", lambda e, R=R: e.activation(
                    R(3), R(2), AF.Exp, scale=-0.5),
                    reads=[rowsc], writes=[rowsc])
                if 1 <= t <= 8:
                    sch.op("vector", lambda e, R=R: e.tensor_copy(
                        rsave[0:1, 0:TA], R(3)),
                        reads=[rowsc], writes=[rsave])
                    sch.op("vector", lambda e, R=R: e.tensor_tensor(
                        rsave[0:1, TA:2 * TA], R(0), R(3), ALU.mult),
                        reads=[rowsc], writes=[rsave])
                    sch.op("gpsimd", lambda e, t=t: e.dma_start(
                        out=stat_s[0:1, (t - 1) * TA:t * TA],
                        in_=rsave[0:1, 0:TA]), reads=[rsave], dma=True)
                    sch.op("gpsimd", lambda e, t=t: e.dma_start(
                        out=stat_s[1:2, (t - 1) * TA:t * TA],
                        in_=rsave[0:1, TA:2 * TA]), reads=[rsave], dma=True)
                sch.op("vector", lambda e, R=R: e.tensor_tensor(
                    R(2), R(3), R(3), ALU.mult),
                    reads=[rowsc], writes=[rowsc])
                sch.op("vector", lambda e, R=R: e.tensor_tensor(
                    R(1), R(1), R(2), ALU.mult),
                    reads=[rowsc], writes=[rowsc])
                sch.op("scalar", lambda e, R=R: e.activation(
                    R(2), R(1), AF.Ln, bias=EPS),
                    reads=[rowsc], sreads=[cnst], writes=[rowsc])
                sch.op("scalar", lambda e, R=R: e.activation(
                    R(4), R(2), AF.Exp, scale=-0.5),
                    reads=[rowsc], writes=[rowsc])
                sch.op("vector", lambda e, R=R: e.tensor_tensor(
                    R(4), R(3), R(4), ALU.mult),
                    reads=[rowsc], writes=[rowsc])
                sch.op("vector", lambda e, R=R: e.tensor_tensor(
                    R(0), R(0), R(4), ALU.mult),
                    reads=[rowsc], writes=[rowsc])
                sch.op("tensor", lambda e, R=R: e.matmul(
                    pb[2].ap()[:, 0:TA], onesR[0:1, :], R(4),
                    start=True, stop=True),
                    reads=[onesR, rowsc], writes=[pb[2]])
                sch.op("tensor", lambda e, R=R: e.matmul(
                    pb[3].ap()[:, 0:TA], onesR[0:1, :], R(0),
                    start=True, stop=True),
                    reads=[onesR, rowsc], writes=[pb[3]])
                x1o = x1b[t % 2]
                for c in range(8):
                    sch.op("vector", lambda e, c=c, xbuf=xbuf: e.tensor_tensor(
                        tmpf[:], xbuf[:, c * TA:(c + 1) * TA],
                        pb[2].ap()[:, 0:TA], ALU.mult),
                        reads=[xbuf, pb[2]], writes=[tmpf])
                    sch.op("vector", lambda e: e.tensor_tensor(
                        tmpf[:], tmpf[:], pb[3].ap()[:, 0:TA], ALU.subtract),
                        reads=[tmpf, pb[3]], writes=[tmpf])
                    sch.op("scalar", lambda e, c=c, x1o=x1o: e.activation(
                        x1o[:, c * C514 + 1:c * C514 + 1 + TA], tmpf[:],
                        AF.Identity, bias=craw[:, B1 + c:B1 + c + 1],
                        scale=craw[:, G1 + c:G1 + c + 1]),
                        reads=[tmpf], sreads=[craw], writes=[x1o])
                if t == 0:
                    sch.op("vector", lambda e, x1o=x1o: e.tensor_scalar(
                        x1o[:], x1o[:], flg[:, 0:1], None, ALU.mult),
                        reads=[x1o], sreads=[flg], writes=[x1o])
                if t == NT - 1:
                    sch.op("vector", lambda e, x1o=x1o: e.tensor_scalar(
                        x1o[:], x1o[:], flg[:, 1:2], None, ALU.mult),
                        reads=[x1o], sreads=[flg], writes=[x1o])
                if t >= 1:
                    x1p = x1b[(t - 1) % 2]
                    sch.op("vector", lambda e, x1o=x1o, x1p=x1p: e.tensor_copy(
                        ch8(x1o, 8)[:, :, 0:1], ch8(x1p, 8)[:, :, 512:513]),
                        reads=[x1p], writes=[x1o])
                    sch.op("vector", lambda e, x1o=x1o, x1p=x1p: e.tensor_copy(
                        ch8(x1p, 8)[:, :, 513:514], ch8(x1o, 8)[:, :, 1:2]),
                        reads=[x1o], writes=[x1p])

            def mix(x_src, out_buf, tm_off, cm_off, d1m_off, d1mc_off):
                for c in range(8):
                    sl = 0
                    coef = d1mc_off if c < 4 else d1m_off
                    sch.op("scalar", lambda e, c=c, sl=sl, coef=coef,
                           x_src=x_src: e.activation(
                        mixt[:, sl:sl + TA],
                        x_src[:, c * C514:c * C514 + TA],
                        AF.Identity, bias=0.0,
                        scale=cder[:, coef + c:coef + c + 1]),
                        reads=[x_src], sreads=[cder], writes=[mixt])
                    if c >= 4:
                        sch.op("vector", lambda e, c=c, sl=sl, x_src=x_src:
                            e.scalar_tensor_tensor(
                                mixt2[:, 0:TA],
                                x_src[:, c * C514 + 2:c * C514 + 2 + TA],
                                craw[:, cm_off + c:cm_off + c + 1],
                                mixt[:, sl:sl + TA], ALU.mult, ALU.add),
                            reads=[x_src, mixt], sreads=[craw],
                            writes=[mixt2])
                    src, soff = (mixt, sl) if c < 4 else (mixt2, 0)
                    sch.op("vector", lambda e, c=c, soff=soff, src=src,
                           x_src=x_src: e.scalar_tensor_tensor(
                        out_buf[:, c * TA:(c + 1) * TA],
                        x_src[:, c * C514 + 1:c * C514 + 1 + TA],
                        craw[:, tm_off + c:tm_off + c + 1],
                        src[:, soff:soff + TA], ALU.mult, ALU.add),
                        reads=[x_src, src], sreads=[craw], writes=[out_buf])

            def chain_stage(t):
                x1cur = x1b[t % 2]
                mix(x1cur, xma, TM, CM, D1M, D1MC)
                for oc in range(8):
                    for i, wsb in ((4, wk_b), (5, wv_b), (6, wr_b)):
                        for kc in range(8):
                            sch.op("tensor", lambda e, i=i, wsb=wsb, kc=kc,
                                   oc=oc: e.matmul(
                                pb[i].ap()[:, 0:TA],
                                wsb[:, kc * 1024 + oc * 128:
                                    kc * 1024 + oc * 128 + 128],
                                xma[:, kc * TA:(kc + 1) * TA],
                                start=(kc == 0), stop=(kc == 7)),
                                reads=[wsb, xma], writes=[pb[i]])
                    sch.op("vector", lambda e: e.tensor_scalar(
                        kbuf[:], pb[4].ap()[:, 0:TA], KCLAMP, None, ALU.min),
                        reads=[pb[4]], writes=[kbuf])
                    sch.op("scalar", lambda e: e.activation(
                        ebuf[:], kbuf[:], AF.Exp),
                        reads=[kbuf], writes=[ebuf])
                    if t == 0:
                        sch.op("vector", lambda e: e.tensor_scalar(
                            ebuf[:], ebuf[:], flg[:, 0:1], None, ALU.mult),
                            reads=[ebuf], sreads=[flg], writes=[ebuf])
                    sch.op("vector", lambda e: e.tensor_tensor(
                        ubuf[:], ebuf[:], pb[5].ap()[:, 0:TA], ALU.mult),
                        reads=[ebuf, pb[5]], writes=[ubuf])
                    sch.op("scalar", lambda e: e.activation(
                        sigb[:], pb[6].ap()[:, 0:TA], AF.Sigmoid),
                        reads=[pb[6]], writes=[sigb])
                    edc = cder[:, ED + oc:ED + oc + 1].broadcast_to([128, TA])
                    sch.op("vector", lambda e, oc=oc: e.tensor_copy(
                        sab[:, 0:1], aCar[:, oc:oc + 1]),
                        reads=[aCar], writes=[sab])
                    sch.op("vector", lambda e, edc=edc, oc=oc:
                        e.tensor_tensor_scan(
                            sab[:, 1:1 + TA], edc, ubuf[:],
                            aCar[:, oc:oc + 1], ALU.mult, ALU.add),
                        reads=[cder, ubuf], sreads=[aCar], writes=[sab])
                    sch.op("vector", lambda e, oc=oc: e.tensor_copy(
                        sbb[:, 0:1], bCar[:, oc:oc + 1]),
                        reads=[bCar], writes=[sbb])
                    sch.op("vector", lambda e, edc=edc, oc=oc:
                        e.tensor_tensor_scan(
                            sbb[:, 1:1 + TA], edc, ebuf[:],
                            bCar[:, oc:oc + 1], ALU.mult, ALU.add),
                        reads=[cder, ebuf], sreads=[bCar], writes=[sbb])
                    sch.op("vector", lambda e, oc=oc: e.tensor_copy(
                        aCar[:, oc:oc + 1], sab[:, TA:TA + 1]),
                        reads=[sab], writes=[aCar])
                    sch.op("vector", lambda e, oc=oc: e.scalar_tensor_tensor(
                        ubuf[:], ubuf[:], cder[:, EU + oc:EU + oc + 1],
                        sab[:, 0:TA], ALU.mult, ALU.add),
                        reads=[ubuf, sab], sreads=[cder], writes=[ubuf])
                    sch.op("vector", lambda e, oc=oc: e.scalar_tensor_tensor(
                        ebuf[:], ebuf[:], cder[:, EU + oc:EU + oc + 1],
                        sbb[:, 0:TA], ALU.mult, ALU.add),
                        reads=[ebuf, sbb], sreads=[cder], writes=[ebuf])
                    sch.op("vector", lambda e, oc=oc: e.tensor_copy(
                        bCar[:, oc:oc + 1], sbb[:, TA:TA + 1]),
                        reads=[sbb], writes=[bCar])
                    sch.op("scalar", lambda e: e.activation(
                        kbuf[:], ebuf[:], AF.Ln, bias=1e-35),
                        reads=[ebuf], sreads=[cnst], writes=[kbuf])
                    sch.op("scalar", lambda e: e.activation(
                        srnb[:], kbuf[:], AF.Exp, scale=-1.0),
                        reads=[kbuf], writes=[srnb])
                    sch.op("vector", lambda e: e.tensor_tensor(
                        ubuf[:], ubuf[:], sigb[:], ALU.mult),
                        reads=[ubuf, sigb], writes=[ubuf])
                    sch.op("vector", lambda e, oc=oc: e.tensor_tensor(
                        tmin[:, oc * TA:(oc + 1) * TA], ubuf[:],
                        srnb[:], ALU.mult),
                        reads=[ubuf, srnb], writes=[tmin])
                for oc in range(8):
                    bk = 4 + (oc % 2)
                    for kc in range(8):
                        sch.op("tensor", lambda e, bk=bk, kc=kc, oc=oc:
                            e.matmul(
                                pb[bk].ap()[:, 0:TA],
                                wo_b[:, kc * 1024 + oc * 128:
                                     kc * 1024 + oc * 128 + 128],
                                tmin[:, kc * TA:(kc + 1) * TA],
                                start=(kc == 0), stop=(kc == 7)),
                            reads=[wo_b, tmin], writes=[pb[bk]])
                    sch.op("vector", lambda e, oc=oc, bk=bk, x1cur=x1cur,
                           x2o=x2b[t % 2]: e.tensor_tensor(
                            x2o[:, oc * C514 + 1:oc * C514 + 1 + TA],
                            x1cur[:, oc * C514 + 1:oc * C514 + 1 + TA],
                            pb[bk].ap()[:, 0:TA], ALU.add),
                        reads=[x1cur, pb[bk]], writes=[x2b[t % 2]])
                x2in = x2b[t % 2]
                for c in range(8):
                    sl = (c % 2) * TA
                    sch.op("scalar", lambda e, c=c, sl=sl, x2in=x2in:
                        e.activation(
                        sqb2[:, sl:sl + TA],
                        x2in[:, c * C514 + 1:c * C514 + 1 + TA],
                        AF.Square), reads=[x2in], writes=[sqb2])
                    sch.op("tensor", lambda e, c=c, x2in=x2in: e.matmul(
                        pb[0].ap()[32:33, 0:TA], onesF[:, 0:1],
                        x2in[:, c * C514 + 1:c * C514 + 1 + TA],
                        start=(c == 0), stop=(c == 7)),
                        reads=[onesF, x2in], writes=[pb[0]])
                    sch.op("tensor", lambda e, c=c, sl=sl: e.matmul(
                        pb[1].ap()[32:33, 0:TA], onesB[:, 0:1],
                        sqb2[:, sl:sl + TA],
                        start=(c == 0), stop=(c == 7)),
                        reads=[onesB, sqb2], writes=[pb[1]])
                P2 = 32
                R2 = lambda i: rowsc[P2:P2 + 1, i * TA:(i + 1) * TA]
                sch.op("vector", lambda e, R2=R2: e.tensor_scalar(
                    R2(0), pb[0].ap()[32:33, 0:TA], 1.0 / 1024, None,
                    ALU.mult), reads=[pb[0]], writes=[rowsc])
                sch.op("vector", lambda e, R2=R2: e.tensor_scalar(
                    R2(1), pb[1].ap()[32:33, 0:TA], 1.0 / 1024, None,
                    ALU.mult), reads=[pb[1]], writes=[rowsc])
                sch.op("vector", lambda e, R2=R2: e.tensor_tensor(
                    R2(2), R2(0), R2(0), ALU.mult),
                    reads=[rowsc], writes=[rowsc])
                sch.op("vector", lambda e, R2=R2: e.tensor_tensor(
                    R2(1), R2(1), R2(2), ALU.subtract),
                    reads=[rowsc], writes=[rowsc])
                sch.op("scalar", lambda e, R2=R2: e.activation(
                    R2(2), R2(1), AF.Ln, bias=EPS),
                    reads=[rowsc], sreads=[cnst], writes=[rowsc])
                sch.op("scalar", lambda e, R2=R2: e.activation(
                    R2(3), R2(2), AF.Exp, scale=-0.5),
                    reads=[rowsc], writes=[rowsc])
                sch.op("vector", lambda e, R2=R2: e.tensor_tensor(
                    R2(0), R2(0), R2(3), ALU.mult),
                    reads=[rowsc], writes=[rowsc])
                sch.op("tensor", lambda e, R2=R2: e.matmul(
                    pb[2].ap()[:, 0:TA], onesR[32:33, :], R2(3),
                    start=True, stop=True),
                    reads=[onesR, rowsc], writes=[pb[2]])
                sch.op("tensor", lambda e, R2=R2: e.matmul(
                    pb[3].ap()[:, 0:TA], onesR[32:33, :], R2(0),
                    start=True, stop=True),
                    reads=[onesR, rowsc], writes=[pb[3]])
                x2o = x2b[t % 2]
                for c in range(8):
                    sch.op("vector", lambda e, c=c, x2in=x2in: e.tensor_tensor(
                        tmpg[:], x2in[:, c * C514 + 1:c * C514 + 1 + TA],
                        pb[2].ap()[:, 0:TA], ALU.mult),
                        reads=[x2in, pb[2]], writes=[tmpg])
                    sch.op("vector", lambda e: e.tensor_tensor(
                        tmpg[:], tmpg[:], pb[3].ap()[:, 0:TA], ALU.subtract),
                        reads=[tmpg, pb[3]], writes=[tmpg])
                    sch.op("scalar", lambda e, c=c, x2o=x2o: e.activation(
                        x2o[:, c * C514 + 1:c * C514 + 1 + TA], tmpg[:],
                        AF.Identity, bias=craw[:, B2 + c:B2 + c + 1],
                        scale=craw[:, G2 + c:G2 + c + 1]),
                        reads=[tmpg], sreads=[craw], writes=[x2o])
                if t == 0:
                    sch.op("vector", lambda e, x2o=x2o: e.tensor_scalar(
                        x2o[:], x2o[:], flg[:, 0:1], None, ALU.mult),
                        reads=[x2o], sreads=[flg], writes=[x2o])
                if t == NT - 1:
                    sch.op("vector", lambda e, x2o=x2o: e.tensor_scalar(
                        x2o[:], x2o[:], flg[:, 1:2], None, ALU.mult),
                        reads=[x2o], sreads=[flg], writes=[x2o])
                if t >= 1:
                    x2p = x2b[(t - 1) % 2]
                    sch.op("vector", lambda e, x2o=x2o, x2p=x2p: e.tensor_copy(
                        ch8(x2o, 8)[:, :, 0:1], ch8(x2p, 8)[:, :, 512:513]),
                        reads=[x2p], writes=[x2o])
                    sch.op("vector", lambda e, x2o=x2o, x2p=x2p: e.tensor_copy(
                        ch8(x2p, 8)[:, :, 513:514], ch8(x2o, 8)[:, :, 1:2]),
                        reads=[x2o], writes=[x2p])
                if 1 <= t <= 8:
                    sch.op("gpsimd", lambda e, t=t, x2o=x2o: e.dma_start(
                        out=hbm8(x2_s, 8)[:, :, (t - 1) * TA:t * TA],
                        in_=ch8(x2o, 8)[:, :, 1:1 + TA]),
                        reads=[x2o], dma=True)

            def mix2_stage(t):
                x2cur = x2b[t % 2]
                mix(x2cur, xmfb, TMF, CMF, D2M, D2MC)
                sch.op("gpsimd", lambda e, t=t: e.dma_start(
                    out=hbm8(xmf_s, 8)[:, :, (t - 1) * TA:t * TA],
                    in_=ch8(xmfb)), reads=[xmfb], dma=True)

            for k in range(NT + 2):
                if k <= NT - 1:
                    x1_stage(k)
                if 1 <= k <= NT:
                    chain_stage(k - 1)
                if 1 <= k - 2 <= 8:
                    mix2_stage(k - 2)

            sch.fence()
        with ExitStack() as eb:
            sb = lambda nme, sh, dt: eb.enter_context(
                nc.sbuf_tensor(nme, sh, dt))
            fk_b = sb("fk_b", [128, 8 * 4096], BF16)
            fv_b = sb("fv_b", [128, 32 * 1024], BF16)
            fr_b = sb("fr_b", [128, 8192], BF16)
            xfb = [sb(f"xfb{i}", [128, 8 * TB], BF16) for i in range(2)]
            x2rb = [sb(f"x2rb{i}", [128, 8 * TB], F32) for i in range(2)]
            fkb = sb("fkb_t", [128, 32 * TB], BF16)
            sfb = sb("sfb", [128, 8 * TB], F32)
            atb = sb("atb", [128, 8 * TB], BF16)
            tmpq = sb("tmpq", [128, TB], F32)
            rlf = sb("rlf", [128, TB], F32)

            for dst, src, nck in ((fk_b, fkt, 8), (fv_b, fvt, 32),
                                  (fr_b, frt, 8)):
                sch.op("gpsimd", lambda e, dst=dst, src=src, nck=nck:
                    e.dma_start(out=ch8(dst, nck), in_=hbm8(src, nck)),
                    writes=[dst], dma=True)

            for t in range(16):
                c0 = TB * t
                xf = xfb[t % 2]
                x2r = x2rb[t % 2]
                sch.op("gpsimd", lambda e, xf=xf, c0=c0: e.dma_start(
                    out=ch8(xf), in_=hbm8(xmf_s, 8)[:, :, c0:c0 + TB]),
                    writes=[xf], dma=True)
                sch.op("gpsimd", lambda e, x2r=x2r, c0=c0: e.dma_start(
                    out=ch8(x2r), in_=hbm8(x2_s, 8)[:, :, c0:c0 + TB]),
                    writes=[x2r], dma=True)
                for oc in range(8):
                    bk = oc % 2
                    for kc in range(8):
                        sch.op("tensor", lambda e, bk=bk, kc=kc, oc=oc, xf=xf:
                            e.matmul(
                                pb[bk].ap()[:, 0:TB],
                                fr_b[:, kc * 1024 + oc * 128:
                                     kc * 1024 + oc * 128 + 128],
                                xf[:, kc * TB:(kc + 1) * TB],
                                start=(kc == 0), stop=(kc == 7)),
                            reads=[fr_b, xf], writes=[pb[bk]])
                    sch.op("scalar", lambda e, oc=oc, bk=bk: e.activation(
                        sfb[:, oc * TB:(oc + 1) * TB], pb[bk].ap()[:, 0:TB],
                        AF.Sigmoid), reads=[pb[bk]], writes=[sfb])
                for hc in range(32):
                    bk = 2 + hc % 2
                    for kc in range(8):
                        sch.op("tensor", lambda e, bk=bk, kc=kc, hc=hc, xf=xf:
                            e.matmul(
                                pb[bk].ap()[:, 0:TB],
                                fk_b[:, kc * 4096 + hc * 128:
                                     kc * 4096 + hc * 128 + 128],
                                xf[:, kc * TB:(kc + 1) * TB],
                                start=(kc == 0), stop=(kc == 7)),
                            reads=[fk_b, xf], writes=[pb[bk]])
                    sch.op("scalar", lambda e, bk=bk: e.activation(
                        rlf[:], pb[bk].ap()[:, 0:TB], AF.Relu),
                        reads=[pb[bk]], writes=[rlf])
                    sch.op("vector", lambda e, hc=hc: e.tensor_tensor(
                        fkb[:, hc * TB:(hc + 1) * TB], rlf[:], rlf[:],
                        ALU.mult), reads=[rlf], writes=[fkb])
                for oc in range(8):
                    bk = 4 + oc % 2
                    for kc in range(32):
                        sch.op("tensor", lambda e, bk=bk, kc=kc, oc=oc:
                            e.matmul(
                                pb[bk].ap()[:, 0:TB],
                                fv_b[:, kc * 1024 + oc * 128:
                                     kc * 1024 + oc * 128 + 128],
                                fkb[:, kc * TB:(kc + 1) * TB],
                                start=(kc == 0), stop=(kc == 31)),
                            reads=[fv_b, fkb], writes=[pb[bk]])
                    sch.op("vector", lambda e, oc=oc, bk=bk: e.tensor_tensor(
                        tmpq[:], sfb[:, oc * TB:(oc + 1) * TB],
                        pb[bk].ap()[:, 0:TB], ALU.mult),
                        reads=[sfb, pb[bk]], writes=[tmpq])
                    sch.op("vector", lambda e, oc=oc, x2r=x2r: e.tensor_tensor(
                        atb[:, oc * TB:(oc + 1) * TB], tmpq[:],
                        x2r[:, oc * TB:(oc + 1) * TB], ALU.add),
                        reads=[tmpq, x2r], writes=[atb])
                sch.op("gpsimd", lambda e, c0=c0: e.dma_start(
                    out=hbm8(attn_s, 8)[:, :, c0:c0 + TB], in_=ch8(atb)),
                    reads=[atb], dma=True)

            sch.fence()
        with ExitStack() as ec:
            sb = lambda nme, sh, dt: ec.enter_context(
                nc.sbuf_tensor(nme, sh, dt))
            w1_8 = sb("w1_8", [128, 8 * 4096], FP8)
            w2_8 = sb("w2_8", [128, 32 * 1024], FP8)
            wstg = sb("wstg", [128, 4096], F32)
            sstat = sb("sstat", [64, CH], F32)
            xcb = [sb(f"xcb{i}", [128, 8 * TC], F32) for i in range(2)]
            acb = [sb(f"acb{i}", [128, 8 * TC], F32) for i in range(2)]
            mkb = [sb(f"mkb{i}", [128, 8 * TC], BF16) for i in range(2)]
            xqb = sb("xqb", [128, 8 * TC], FP8)
            hbb = sb("hbb", [128, 32 * TC], BF16)
            h8b = sb("h8b", [128, 32 * TC], FP8)
            y2b = sb("y2b", [128, 8 * TC], BF16)
            outb = [sb(f"outb{i}", [128, 8 * TC], F32) for i in range(2)]
            tmpc = sb("tmpc", [128, TC], F32)

            sch.op("sync", lambda e: e.dma_start(
                out=sstat[0:1, :], in_=stat_s[0:1, :]),
                writes=[sstat], dma=True)
            sch.op("sync", lambda e: e.dma_start(
                out=sstat[32:33, :], in_=stat_s[1:2, :]),
                writes=[sstat], dma=True)
            for kc in range(8):
                sch.op("sync", lambda e, kc=kc: e.dma_start(
                    out=wstg[:], in_=hbm8(w1t, 8)[:, kc, :]),
                    writes=[wstg], dma=True)
                sch.op("vector", lambda e, kc=kc: e.tensor_copy(
                    w1_8[:, kc * 4096:(kc + 1) * 4096], wstg[:]),
                    reads=[wstg], writes=[w1_8])
            for g in range(8):
                sch.op("sync", lambda e, g=g: e.dma_start(
                    out=wstg[:].rearrange("p (c m) -> p c m", c=4),
                    in_=hbm8(w2t, 32)[:, 4 * g:4 * g + 4, :]),
                    writes=[wstg], dma=True)
                sch.op("vector", lambda e, g=g: e.tensor_copy(
                    w2_8[:, g * 4096:(g + 1) * 4096], wstg[:]),
                    reads=[wstg], writes=[w2_8])

            for t in range(16):
                c0 = TC * t
                xc = xcb[t % 2]
                ac = acb[t % 2]
                mk = mkb[t % 2]
                ob = outb[t % 2]
                sch.op("sync", lambda e, xc=xc, c0=c0: e.dma_start(
                    out=ch8(xc), in_=hbm8(xh, 8)[:, :, H + c0:H + c0 + TC]),
                    writes=[xc], dma=True)
                sch.op("gpsimd", lambda e, ac=ac, c0=c0: e.dma_start(
                    out=ch8(ac), in_=hbm8(attn_s, 8)[:, :, c0:c0 + TC]),
                    writes=[ac], dma=True)
                sch.op("gpsimd", lambda e, mk=mk, c0=c0: e.dma_start(
                    out=ch8(mk), in_=hbm8(mask_d, 8)[:, :, c0:c0 + TC]),
                    writes=[mk], dma=True)
                sch.op("tensor", lambda e, c0=c0: e.matmul(
                    pb[2].ap()[:, 0:TC], onesR[0:1, :],
                    sstat[0:1, c0:c0 + TC], start=True, stop=True),
                    reads=[onesR, sstat], writes=[pb[2]])
                sch.op("tensor", lambda e, c0=c0: e.matmul(
                    pb[3].ap()[:, 0:TC], onesR[32:33, :],
                    sstat[32:33, c0:c0 + TC], start=True, stop=True),
                    reads=[onesR, sstat], writes=[pb[3]])
                for c in range(8):
                    sch.op("vector", lambda e, c=c, xc=xc: e.tensor_tensor(
                        tmpc[:], xc[:, c * TC:(c + 1) * TC],
                        pb[2].ap()[:, 0:TC], ALU.mult),
                        reads=[xc, pb[2]], writes=[tmpc])
                    sch.op("vector", lambda e, c=c: e.tensor_tensor(
                        tmpc[:], tmpc[:],
                        pb[3].ap()[:, 0:TC], ALU.subtract),
                        reads=[tmpc, pb[3]], writes=[tmpc])
                    sch.op("vector", lambda e, c=c: e.tensor_copy(
                        xqb[:, c * TC:(c + 1) * TC], tmpc[:]),
                        reads=[tmpc], writes=[xqb])
                for hc in range(32):
                    bk = 4 + hc % 2
                    for kc in range(8):
                        sch.op("tensor", lambda e, bk=bk, kc=kc, hc=hc:
                            e.matmul(
                                pb[bk].ap()[:, 0:TC],
                                w1_8[:, kc * 4096 + hc * 128:
                                     kc * 4096 + hc * 128 + 128],
                                xqb[:, kc * TC:(kc + 1) * TC],
                                start=(kc == 0), stop=(kc == 7)),
                            reads=[w1_8, xqb], writes=[pb[bk]])
                    sch.op("scalar", lambda e, hc=hc, bk=bk: e.activation(
                        hbb[:, hc * TC:(hc + 1) * TC], pb[bk].ap()[:, 0:TC],
                        AF.Relu, bias=b1q[:, hc:hc + 1], scale=1.0),
                        reads=[pb[bk]], sreads=[b1q], writes=[hbb])
                    sch.op("vector", lambda e, hc=hc: e.tensor_copy(
                        h8b[:, hc * TC:(hc + 1) * TC],
                        hbb[:, hc * TC:(hc + 1) * TC]),
                        reads=[hbb], writes=[h8b])
                for oc in range(8):
                    bk = 6 + oc % 2
                    for kc in range(32):
                        sch.op("tensor", lambda e, bk=bk, kc=kc, oc=oc:
                            e.matmul(
                                pb[bk].ap()[:, 0:TC],
                                w2_8[:, kc * 1024 + oc * 128:
                                     kc * 1024 + oc * 128 + 128],
                                h8b[:, kc * TC:(kc + 1) * TC],
                                start=(kc == 0), stop=(kc == 31)),
                            reads=[w2_8, h8b], writes=[pb[bk]])
                    sch.op("scalar", lambda e, oc=oc, bk=bk: e.activation(
                        y2b[:, oc * TC:(oc + 1) * TC], pb[bk].ap()[:, 0:TC],
                        AF.Identity, bias=b2q[:, oc:oc + 1], scale=1.0),
                        reads=[pb[bk]], sreads=[b2q], writes=[y2b])
                    sch.op("vector", lambda e, oc=oc, xc=xc, ac=ac, ob=ob:
                        e.tensor_tensor(
                            ob[:, oc * TC:(oc + 1) * TC],
                            xc[:, oc * TC:(oc + 1) * TC],
                            ac[:, oc * TC:(oc + 1) * TC], ALU.add),
                        reads=[xc, ac], writes=[ob])
                    sch.op("vector", lambda e, oc=oc, mk=mk: e.tensor_tensor(
                        tmpc[:], y2b[:, oc * TC:(oc + 1) * TC],
                        mk[:, oc * TC:(oc + 1) * TC], ALU.mult),
                        reads=[y2b, mk], writes=[tmpc])
                    sch.op("vector", lambda e, oc=oc, ob=ob: e.tensor_tensor(
                        ob[:, oc * TC:(oc + 1) * TC],
                        ob[:, oc * TC:(oc + 1) * TC], tmpc[:], ALU.add),
                        reads=[ob, tmpc], writes=[ob])
                sch.op("gpsimd", lambda e, ob=ob, c0=c0: e.dma_start(
                    out=hbm8(out_d, 8)[:, :, c0:c0 + TC], in_=ch8(ob)),
                    reads=[ob], dma=True)

            sch.fence()
            sch.op("gpsimd", lambda e: e.nop())

        sch.finalize()
        with nc.Block() as block:
            sch.flush(block, sems)
    return nc


_NC_CACHE = [None]


def _stage(inputs):
    x = np.asarray(inputs["L0_attn_x"], dtype=np.float32)
    B, T, C = x.shape
    assert (B, T, C) == (2, 16384, 1024)

    def wt(a):
        return np.ascontiguousarray(np.asarray(a, np.float32).T)

    weights = dict(
        wkt=wt(inputs["L0_attn_Wk"]), wvt=wt(inputs["L0_attn_Wv"]),
        wrt=wt(inputs["L0_attn_Wr"]), wot=wt(inputs["L0_attn_Wo"]),
        frt=wt(inputs["L0_attn_Fr"]), fkt=wt(inputs["L0_attn_Fk"]),
        fvt=wt(inputs["L0_attn_Fv"]), w1t=wt(inputs["L0_ffn_W1"]),
        w2t=wt(inputs["L0_ffn_W2"]))

    def pack8(v, ncol):
        return np.ascontiguousarray(
            np.asarray(v, np.float32).reshape(ncol, 128).T)

    craw = np.zeros((128, 64), np.float32)
    for off, nme in ((TM, "L0_attn_att_time_mix"),
                     (CM, "L0_attn_att_combined_mix"),
                     (TMF, "L0_attn_ffn_time_mix"),
                     (CMF, "L0_attn_ffn_combined_mix"),
                     (G1, "L0_attn_ln1_g"), (B1, "L0_attn_ln1_b"),
                     (G2, "L0_attn_ln2_g"), (B2, "L0_attn_ln2_b")):
        craw[:, off:off + 8] = pack8(np.asarray(inputs[nme]).reshape(-1), 8)
    ctd = np.zeros((128, 16), np.float32)
    ctd[:, 0:8] = pack8(inputs["L0_attn_time_decay"], 8)
    ctd[:, 8:16] = pack8(inputs["L0_attn_time_first"], 8)
    b1r = pack8(inputs["L0_ffn_b1"], 32)
    b2r = pack8(inputs["L0_ffn_b2"], 8)

    mask = _bernoulli_mask(42, 0.8, (B, T, C))
    maskv = np.where(mask, np.float32(1.25),
                     np.float32(0.0)).astype(ml_dtypes.bfloat16)

    in_maps = []
    for core in range(NCORES):
        b, j = core // 4, core % 4
        t0 = j * CH
        lo, hi = t0 - H, t0 + CH + 2
        xr = np.zeros((NXC, 1024), np.float32)
        s0, s1 = max(lo, 0), min(hi, T)
        xr[s0 - lo:s1 - lo] = x[b, s0:s1]
        flgv = np.zeros((128, 2), np.float32)
        flgv[:, 0] = 0.0 if j == 0 else 1.0
        flgv[:, 1] = 0.0 if j == 3 else 1.0
        m = dict(weights)
        m.update(xh=np.ascontiguousarray(xr.T), craw=craw, ctd=ctd,
                 b1r=b1r, b2r=b2r, flg=flgv,
                 maskv=np.ascontiguousarray(maskv[b, t0:t0 + CH].T))
        in_maps.append(m)
    return in_maps


def kernel(**inputs):
    x = np.asarray(inputs["L0_attn_x"], dtype=np.float32)
    B, T, C = x.shape
    in_maps = _stage(inputs)
    if _NC_CACHE[0] is None:
        _NC_CACHE[0] = _build()
    nc = _NC_CACHE[0]
    res = run_bass_kernel_spmd(nc, in_maps, core_ids=list(range(NCORES)))

    out = np.empty((B, T, C), np.float32)
    for core in range(NCORES):
        b, j = core // 4, core % 4
        out[b, j * CH:(j + 1) * CH] = np.asarray(res.results[core]["out"]).T
    return out


# version 06
# speedup vs baseline: 1.0663x; 1.0663x over previous
import os
import numpy as np
import ml_dtypes
from contextlib import ExitStack

_PARANOID = os.environ.get("K_PARANOID", "") == "1"

import concourse.bass as bass
import concourse.mybir as mybir
from concourse.bass_utils import run_bass_kernel_spmd

F32 = mybir.dt.float32
BF16 = mybir.dt.bfloat16
FP8 = mybir.dt.float8e5
AF = mybir.ActivationFunctionType
ALU = mybir.AluOpType
PM = mybir.MatmulPerfMode

H = 512
CH = 4096
NXC = 5120
TA = 512
NT = 10
TB = 256
TC = 512
EPS = 1e-5
KCLAMP = 60.0
NCORES = 8
C514 = 514

TM, CM, TMF, CMF, G1, B1, G2, B2 = (i * 8 for i in range(8))
D1M, D1MC, D2M, D2MC, ED, EU = (i * 8 for i in range(6))


def _threefry2x32(k1, k2, x1, x2):
    x = [x1.astype(np.uint32).copy(), x2.astype(np.uint32).copy()]
    rot = [(13, 15, 26, 6), (17, 29, 16, 24)]
    ks = [np.uint32(k1), np.uint32(k2),
          np.uint32(k1) ^ np.uint32(k2) ^ np.uint32(0x1BD11BDA)]

    def rotl(v, r):
        return (v << np.uint32(r)) | (v >> np.uint32(32 - r))

    with np.errstate(over="ignore"):
        x[0] = x[0] + ks[0]
        x[1] = x[1] + ks[1]
        for i in range(5):
            for r in rot[i % 2]:
                x[0] = x[0] + x[1]
                x[1] = x[0] ^ rotl(x[1], r)
            x[0] = x[0] + ks[(i + 1) % 3]
            x[1] = x[1] + ks[(i + 2) % 3] + np.uint32(i + 1)
    return x[0], x[1]


def _bernoulli_mask(seed, p, shape):
    n = int(np.prod(shape))
    idx = np.arange(n, dtype=np.uint64)
    c1 = (idx >> np.uint64(32)).astype(np.uint32)
    c2 = (idx & np.uint64(0xFFFFFFFF)).astype(np.uint32)
    b1, b2 = _threefry2x32(np.uint32(seed >> 32), np.uint32(seed & 0xFFFFFFFF),
                           c1, c2)
    bits = b1 ^ b2
    fb = (bits >> np.uint32(9)) | np.uint32(0x3F800000)
    u = fb.view(np.float32) - np.float32(1.0)
    return (u < np.float32(p)).reshape(shape)


def _tname(t):
    if hasattr(t, "tensor"):
        t = t.tensor
    return t.name


class Sched:
    ENGS = ("sync", "tensor", "vector", "scalar", "gpsimd")
    COMPUTE = {"tensor", "vector", "scalar"}
    SEM_OF = {"sync": "LX", "tensor": "T", "vector": "V", "scalar": "S",
              "gpsimd": "G"}
    NROT = {"sync": 6, "gpsimd": 8}

    def __init__(self):
        self.ops = []
        self.state = {}
        self.epos = {e: 0 for e in self.ENGS}
        self.fence_deps = []

    def op(self, eng, fn, reads=(), writes=(), sreads=(), dma=False):
        idx = len(self.ops)
        deps = set(self.fence_deps)
        if _PARANOID and idx > 0:
            deps.add(idx - 1)
        sdeps = set()
        for t in reads:
            st = self.state.get(_tname(t))
            if st and st["w"] is not None:
                deps.add(st["w"])
        for t in sreads:
            st = self.state.get(_tname(t))
            if st and st["w"] is not None:
                sdeps.add(st["w"])
        for t in writes:
            st = self.state.get(_tname(t))
            if st:
                if st["w"] is not None:
                    deps.add(st["w"])
                deps.update(st["r"])
        o = dict(idx=idx, eng=eng, fn=fn, deps=deps, sdeps=sdeps, dma=dma,
                 pos=self.epos[eng], inc_needed=dma, lm=None)
        self.epos[eng] += 1
        self.ops.append(o)
        for t in list(reads) + list(sreads):
            self.state.setdefault(_tname(t), {"w": None, "r": []})["r"].append(idx)
        for t in writes:
            self.state[_tname(t)] = {"w": idx, "r": []}
        return idx

    def fence(self):
        last = {}
        for o in self.ops:
            last[o["eng"]] = o["idx"]
        self.fence_deps = list(last.values())

    def finalize(self):
        rot = {"sync": 0, "gpsimd": 0}
        for o in self.ops:
            if o["dma"]:
                i = rot[o["eng"]]
                o["sem"] = self.SEM_OF[o["eng"]] + str(i)
                rot[o["eng"]] = (i + 1) % self.NROT[o["eng"]]
            else:
                o["sem"] = self.SEM_OF[o["eng"]]
        for o in self.ops:
            waits = {}
            for d in o["deps"]:
                dop = self.ops[d]
                if (dop["eng"] == o["eng"] and o["eng"] in self.COMPUTE
                        and not dop["dma"]):
                    continue
                waits[d] = True
            for d in o["sdeps"]:
                dop = self.ops[d]
                same = (dop["eng"] == o["eng"] and o["eng"] in self.COMPUTE
                        and not dop["dma"])
                if same and (o["pos"] - dop["pos"]) >= 8:
                    continue
                waits[d] = True
            o["waitset"] = list(waits)
            for d in waits:
                self.ops[d]["inc_needed"] = True
        cnt = {}
        for o in self.ops:
            if o["dma"]:
                o["inc_needed"] = True
            if o["inc_needed"]:
                cnt[o["sem"]] = cnt.get(o["sem"], 0) + (16 if o["dma"] else 1)
                o["lm"] = cnt[o["sem"]]
        self.final_cnt = cnt

    def flush(self, block, sems):
        lastw = {e: {} for e in self.ENGS}
        seqs = {e: [] for e in self.ENGS}
        for o in self.ops:
            seqs[o["eng"]].append(o)
        for ename in self.ENGS:
            ops = seqs[ename]
            if not ops:
                continue

            def body(e, ops=ops, ename=ename):
                lw = lastw[ename]
                for o in ops:
                    wmax = {}
                    if o["dma"] and o["lm"] > 16:
                        wmax[o["sem"]] = o["lm"] - 16
                    for d in o["waitset"]:
                        dop = self.ops[d]
                        assert dop["lm"] is not None
                        wmax[dop["sem"]] = max(wmax.get(dop["sem"], 0),
                                               dop["lm"])
                    twait = False
                    for sn, v in wmax.items():
                        if lw.get(sn, 0) < v:
                            e.wait_ge(sems[sn], v)
                            lw[sn] = v
                            if sn == "T":
                                twait = True
                    ins = o["fn"](e)
                    if o["inc_needed"]:
                        ins.then_inc(sems[o["sem"]], 16 if o["dma"] else 1)

            getattr(block, ename)(body)


def _build():
    nc = bass.Bass("TRN2", target_bir_lowering=False,
                   detect_race_conditions=False)

    xh = nc.dram_tensor("xh", [1024, NXC], F32, kind="ExternalInput").ap()
    wkt = nc.dram_tensor("wkt", [1024, 1024], F32, kind="ExternalInput").ap()
    wvt = nc.dram_tensor("wvt", [1024, 1024], F32, kind="ExternalInput").ap()
    wrt = nc.dram_tensor("wrt", [1024, 1024], F32, kind="ExternalInput").ap()
    wot = nc.dram_tensor("wot", [1024, 1024], F32, kind="ExternalInput").ap()
    frt = nc.dram_tensor("frt", [1024, 1024], F32, kind="ExternalInput").ap()
    fkt = nc.dram_tensor("fkt", [1024, 4096], F32, kind="ExternalInput").ap()
    fvt = nc.dram_tensor("fvt", [4096, 1024], F32, kind="ExternalInput").ap()
    w1d = nc.dram_tensor("w1d", [128, 32768], F32, kind="ExternalInput").ap()
    w2d = nc.dram_tensor("w2d", [128, 32768], F32, kind="ExternalInput").ap()
    craw_d = nc.dram_tensor("craw", [128, 64], F32, kind="ExternalInput").ap()
    ctd_d = nc.dram_tensor("ctd", [128, 16], F32, kind="ExternalInput").ap()
    b1r_d = nc.dram_tensor("b1r", [128, 32], F32, kind="ExternalInput").ap()
    b2r_d = nc.dram_tensor("b2r", [128, 8], F32, kind="ExternalInput").ap()
    flg_d = nc.dram_tensor("flg", [128, 2], F32, kind="ExternalInput").ap()
    mask_d = nc.dram_tensor("maskv", [1024, CH], BF16, kind="ExternalInput").ap()
    out_d = nc.dram_tensor("out", [1024, CH], F32, kind="ExternalOutput").ap()
    xmf_s = nc.dram_tensor("xmf_s", [1024, CH], BF16).ap()
    x2_s = nc.dram_tensor("x2_s", [1024, CH], BF16).ap()
    attn_s = nc.dram_tensor("attn_s", [1024, CH], BF16).ap()
    stat_s = nc.dram_tensor("stat_s", [2, CH], F32).ap()
    fk16 = nc.dram_tensor("fk16", [1024, 4096], BF16).ap()
    fv16 = nc.dram_tensor("fv16", [4096, 1024], BF16).ap()
    fr16 = nc.dram_tensor("fr16", [1024, 1024], BF16).ap()
    w1q8 = nc.dram_tensor("w1q8", [128, 32768], FP8).ap()
    w2q8 = nc.dram_tensor("w2q8", [128, 32768], FP8).ap()

    def hbm8(ap, nchunk):
        return ap.rearrange("(c p) m -> p c m", p=128)

    def ch8(buf, nc_=8):
        return buf[:].rearrange("p (c m) -> p c m", c=nc_)

    sch = Sched()
    top = ExitStack()
    with top:
        semnames = (["V", "S", "T"]
                    + ["LX%d" % i for i in range(Sched.NROT["sync"])]
                    + ["G%d" % i for i in range(Sched.NROT["gpsimd"])])
        sems = {nm: top.enter_context(nc.semaphore(nm)) for nm in semnames}
        pb = [top.enter_context(nc.psum_tensor(f"pb{i}", [128, 512], F32))
              for i in range(8)]

        b1q = top.enter_context(nc.sbuf_tensor("b1q", [128, 32], F32))
        b2q = top.enter_context(nc.sbuf_tensor("b2q", [128, 8], F32))
        onesR = top.enter_context(nc.sbuf_tensor("onesR", [64, 128], F32))

        with ExitStack() as ea:
            sb = lambda nme, sh, dt: ea.enter_context(
                nc.sbuf_tensor(nme, sh, dt))
            craw = sb("craw_s", [128, 64], F32)
            cder = sb("cder_s", [128, 64], F32)
            ctd = sb("ctd_s", [128, 16], F32)
            q8t = sb("q8t", [128, 40], FP8)
            flg = sb("flg_s", [128, 2], F32)
            onesF = sb("onesF", [128, 1], F32)
            onesB = sb("onesB", [128, 1], BF16)
            cnst = sb("cnst", [128, 2], F32)
            sch.op("sync", lambda e: e.dma_start(out=craw[:], in_=craw_d),
                   writes=[craw], dma=True)
            sch.op("sync", lambda e: e.dma_start(out=ctd[:], in_=ctd_d),
                   writes=[ctd], dma=True)
            sch.op("sync", lambda e: e.dma_start(out=b1q[:], in_=b1r_d),
                   writes=[b1q], dma=True)
            sch.op("sync", lambda e: e.dma_start(out=b2q[:], in_=b2r_d),
                   writes=[b2q], dma=True)
            sch.op("sync", lambda e: e.dma_start(out=flg[:], in_=flg_d),
                   writes=[flg], dma=True)
            sch.op("vector", lambda e: e.memset(cnst[:, 0:1], EPS), writes=[cnst])
            sch.op("vector", lambda e: e.memset(cnst[:, 1:2], 1e-35), writes=[cnst])
            nc.const_aps.register(F32, EPS, cnst[:, 0:1])
            nc.const_aps.register(F32, 1e-35, cnst[:, 1:2])
            sch.op("vector", lambda e: e.memset(onesF[:], 1.0), writes=[onesF])
            sch.op("vector", lambda e: e.memset(onesB[:], 1.0), writes=[onesB])
            sch.op("vector", lambda e: e.memset(onesR[:], 1.0), writes=[onesR])
            def _derive(final):
                if final:
                    sch.op("vector", lambda e: e.tensor_copy(q8t[:, 0:32], b1q[:]),
                           reads=[b1q], writes=[q8t])
                    sch.op("vector", lambda e: e.tensor_copy(q8t[:, 32:40], b2q[:]),
                           reads=[b2q], writes=[q8t])
                sch.op("vector", lambda e: e.tensor_scalar(
                    cder[:, D1M:D1M + 8], craw[:, TM:TM + 8], -1.0, 1.0,
                    ALU.mult, ALU.add), reads=[craw], writes=[cder])
                sch.op("vector", lambda e: e.tensor_tensor(
                    cder[:, D1MC:D1MC + 8], cder[:, D1M:D1M + 8],
                    craw[:, CM:CM + 8], ALU.add), reads=[craw, cder], writes=[cder])
                sch.op("vector", lambda e: e.tensor_scalar(
                    cder[:, D2M:D2M + 8], craw[:, TMF:TMF + 8], -1.0, 1.0,
                    ALU.mult, ALU.add), reads=[craw], writes=[cder])
                sch.op("vector", lambda e: e.tensor_tensor(
                    cder[:, D2MC:D2MC + 8], cder[:, D2M:D2M + 8],
                    craw[:, CMF:CMF + 8], ALU.add), reads=[craw, cder], writes=[cder])
                if final:
                    sch.op("vector", lambda e: e.memset(cnst[:, 0:1], EPS),
                           writes=[cnst])
                    sch.op("vector", lambda e: e.memset(cnst[:, 1:2], 1e-35),
                           writes=[cnst])
                    sch.op("vector", lambda e: e.memset(onesF[:], 1.0),
                           writes=[onesF])
                    sch.op("vector", lambda e: e.memset(onesB[:], 1.0),
                           writes=[onesB])
                    sch.op("vector", lambda e: e.memset(onesR[:], 1.0),
                           writes=[onesR])
                    sch.op("vector", lambda e: e.tensor_copy(b1q[:], q8t[:, 0:32]),
                           reads=[q8t], writes=[b1q])
                    sch.op("vector", lambda e: e.tensor_copy(b2q[:], q8t[:, 32:40]),
                           reads=[q8t], writes=[b2q])
                sch.op("scalar", lambda e: e.activation(
                    cder[:, ED:ED + 8], ctd[:, 0:8], AF.Exp),
                    reads=[ctd], writes=[cder])
                sch.op("scalar", lambda e: e.activation(
                    cder[:, ED:ED + 8], cder[:, ED:ED + 8], AF.Exp, scale=-1.0),
                    reads=[cder], writes=[cder])
                sch.op("scalar", lambda e: e.activation(
                    cder[:, EU:EU + 8], ctd[:, 8:16], AF.Exp),
                    reads=[ctd], writes=[cder])

            _derive(final=False)
            sch.fence()
            _derive(final=True)
            sch.fence()

            wk_b = sb("wk_b", [128, 8192], BF16)
            wv_b = sb("wv_b", [128, 8192], BF16)
            wr_b = sb("wr_b", [128, 8192], BF16)
            wo_b = sb("wo_b", [128, 8192], BF16)
            xt0 = sb("xt0", [128, 8 * TA], F32)
            xt = [xt0, xt0]
            sqf = sb("sqf", [128, TA], F32)
            sqb2 = sb("sqb2", [128, 2 * TA], BF16)
            x1b = [sb(f"x1b{i}", [128, 8 * C514], F32) for i in range(2)]
            x2b = [sb(f"x2b{i}", [128, 8 * C514], F32) for i in range(2)]
            xma = sb("xma", [128, 8 * TA], BF16)
            tmin = sb("tmin", [128, 8 * TA], BF16)
            xmfb = sb("xmfb", [128, 8 * TA], BF16)
            mixt = sb("mixt", [128, TA], F32)
            mixt2 = sb("mixt2", [128, TA], F32)
            tmpf = sb("tmpf", [128, TA], F32)
            tmpg = tmpf
            kbuf = sb("kbuf", [128, TA], F32)
            ebuf = sb("ebuf", [128, TA], F32)
            ubuf = sb("ubuf", [128, TA], F32)
            sigb = sb("sigb", [128, TA], F32)
            srnb = sb("srnb", [128, TA], F32)
            sab = sb("sab", [128, 513], F32)
            sbb = sb("sbb", [128, 513], F32)
            aCar = sb("aCar", [128, 8], F32)
            bCar = sb("bCar", [128, 8], F32)
            rowsc = sb("rowsc", [64, 5 * TA], F32)
            rsave = sb("rsave", [64, 2 * TA], F32)

            for t_ in (aCar, bCar, sab, sbb, x1b[0], x1b[1], x2b[0], x2b[1]):
                sch.op("vector", lambda e, t_=t_: e.memset(t_[:], 0.0),
                       writes=[t_])

            for dst, src in ((wk_b, wkt), (wv_b, wvt), (wr_b, wrt),
                             (wo_b, wot)):
                sch.op("gpsimd",
                       lambda e, dst=dst, src=src: e.dma_start(
                           out=ch8(dst), in_=hbm8(src, 8)),
                       writes=[dst], dma=True)
            for dst, src in ((fr16, frt), (fk16, fkt), (fv16, fvt),
                             (w1q8, w1d), (w2q8, w2d)):
                sch.op("gpsimd", lambda e, dst=dst, src=src: e.dma_start(
                    out=dst, in_=src), writes=[dst], dma=True)

            def x1_stage(t):
                c0 = TA * t
                xbuf = xt[t % 2]
                sch.op("sync", lambda e, xbuf=xbuf, c0=c0: e.dma_start(
                    out=ch8(xbuf), in_=hbm8(xh, 8)[:, :, c0:c0 + TA]),
                    writes=[xbuf], dma=True)
                for c in range(8):
                    sl = 0
                    sch.op("scalar", lambda e, c=c, sl=sl, xbuf=xbuf:
                        e.activation(sqf[:, sl:sl + TA],
                                     xbuf[:, c * TA:(c + 1) * TA], AF.Square),
                        reads=[xbuf], writes=[sqf])
                    sch.op("tensor", lambda e, c=c, xbuf=xbuf: e.matmul(
                        pb[0].ap()[0:1, 0:TA], onesF[:, 0:1],
                        xbuf[:, c * TA:(c + 1) * TA],
                        start=(c == 0), stop=(c == 7)),
                        reads=[onesF, xbuf], writes=[pb[0]])
                    sch.op("tensor", lambda e, c=c, sl=sl: e.matmul(
                        pb[1].ap()[0:1, 0:TA], onesF[:, 0:1],
                        sqf[:, sl:sl + TA],
                        start=(c == 0), stop=(c == 7)),
                        reads=[onesF, sqf], writes=[pb[1]])
                P0 = 0
                R = lambda i: rowsc[P0:P0 + 1, i * TA:(i + 1) * TA]
                sch.op("vector", lambda e, R=R: e.tensor_scalar(
                    R(0), pb[0].ap()[0:1, 0:TA], 1.0 / 1024, None,
                    ALU.mult), reads=[pb[0]], writes=[rowsc])
                sch.op("vector", lambda e, R=R: e.tensor_scalar(
                    R(1), pb[1].ap()[0:1, 0:TA], 1.0 / 1024, None,
                    ALU.mult), reads=[pb[1]], writes=[rowsc])
                sch.op("vector", lambda e, R=R: e.tensor_tensor(
                    R(2), R(0), R(0), ALU.mult),
                    reads=[rowsc], writes=[rowsc])
                sch.op("vector", lambda e, R=R: e.tensor_tensor(
                    R(1), R(1), R(2), ALU.subtract),
                    reads=[rowsc], writes=[rowsc])
                sch.op("scalar", lambda e, R=R: e.activation(
                    R(2), R(1), AF.Ln, bias=EPS),
                    reads=[rowsc], sreads=[cnst], writes=[rowsc])
                sch.op("scalar", lambda e, R=R: e.activation(
                    R(3), R(2), AF.Exp, scale=-0.5),
                    reads=[rowsc], writes=[rowsc])
                if 1 <= t <= 8:
                    sch.op("vector", lambda e, R=R: e.tensor_copy(
                        rsave[0:1, 0:TA], R(3)),
                        reads=[rowsc], writes=[rsave])
                    sch.op("vector", lambda e, R=R: e.tensor_tensor(
                        rsave[0:1, TA:2 * TA], R(0), R(3), ALU.mult),
                        reads=[rowsc], writes=[rsave])
                    sch.op("sync", lambda e, t=t: e.dma_start(
                        out=stat_s[0:1, (t - 1) * TA:t * TA],
                        in_=rsave[0:1, 0:TA]), reads=[rsave], dma=True)
                    sch.op("sync", lambda e, t=t: e.dma_start(
                        out=stat_s[1:2, (t - 1) * TA:t * TA],
                        in_=rsave[0:1, TA:2 * TA]), reads=[rsave], dma=True)
                sch.op("vector", lambda e, R=R: e.tensor_tensor(
                    R(2), R(3), R(3), ALU.mult),
                    reads=[rowsc], writes=[rowsc])
                sch.op("vector", lambda e, R=R: e.tensor_tensor(
                    R(1), R(1), R(2), ALU.mult),
                    reads=[rowsc], writes=[rowsc])
                sch.op("scalar", lambda e, R=R: e.activation(
                    R(2), R(1), AF.Ln, bias=EPS),
                    reads=[rowsc], sreads=[cnst], writes=[rowsc])
                sch.op("scalar", lambda e, R=R: e.activation(
                    R(4), R(2), AF.Exp, scale=-0.5),
                    reads=[rowsc], writes=[rowsc])
                sch.op("vector", lambda e, R=R: e.tensor_tensor(
                    R(4), R(3), R(4), ALU.mult),
                    reads=[rowsc], writes=[rowsc])
                sch.op("vector", lambda e, R=R: e.tensor_tensor(
                    R(0), R(0), R(4), ALU.mult),
                    reads=[rowsc], writes=[rowsc])
                sch.op("tensor", lambda e, R=R: e.matmul(
                    pb[2].ap()[:, 0:TA], onesR[0:1, :], R(4),
                    start=True, stop=True),
                    reads=[onesR, rowsc], writes=[pb[2]])
                sch.op("tensor", lambda e, R=R: e.matmul(
                    pb[3].ap()[:, 0:TA], onesR[0:1, :], R(0),
                    start=True, stop=True),
                    reads=[onesR, rowsc], writes=[pb[3]])
                x1o = x1b[t % 2]
                for c in range(8):
                    sch.op("vector", lambda e, c=c, xbuf=xbuf: e.tensor_tensor(
                        tmpf[:], xbuf[:, c * TA:(c + 1) * TA],
                        pb[2].ap()[:, 0:TA], ALU.mult),
                        reads=[xbuf, pb[2]], writes=[tmpf])
                    sch.op("vector", lambda e: e.tensor_tensor(
                        tmpf[:], tmpf[:], pb[3].ap()[:, 0:TA], ALU.subtract),
                        reads=[tmpf, pb[3]], writes=[tmpf])
                    sch.op("scalar", lambda e, c=c, x1o=x1o: e.activation(
                        x1o[:, c * C514 + 1:c * C514 + 1 + TA], tmpf[:],
                        AF.Identity, bias=craw[:, B1 + c:B1 + c + 1],
                        scale=craw[:, G1 + c:G1 + c + 1]),
                        reads=[tmpf], sreads=[craw], writes=[x1o])
                if t == 0:
                    sch.op("vector", lambda e, x1o=x1o: e.tensor_scalar(
                        x1o[:], x1o[:], flg[:, 0:1], None, ALU.mult),
                        reads=[x1o], sreads=[flg], writes=[x1o])
                if t == NT - 1:
                    sch.op("vector", lambda e, x1o=x1o: e.tensor_scalar(
                        x1o[:], x1o[:], flg[:, 1:2], None, ALU.mult),
                        reads=[x1o], sreads=[flg], writes=[x1o])
                if t >= 1:
                    x1p = x1b[(t - 1) % 2]
                    sch.op("vector", lambda e, x1o=x1o, x1p=x1p: e.tensor_copy(
                        ch8(x1o, 8)[:, :, 0:1], ch8(x1p, 8)[:, :, 512:513]),
                        reads=[x1p], writes=[x1o])
                    sch.op("vector", lambda e, x1o=x1o, x1p=x1p: e.tensor_copy(
                        ch8(x1p, 8)[:, :, 513:514], ch8(x1o, 8)[:, :, 1:2]),
                        reads=[x1o], writes=[x1p])

            def mix(x_src, out_buf, tm_off, cm_off, d1m_off, d1mc_off):
                for c in range(8):
                    sl = 0
                    coef = d1mc_off if c < 4 else d1m_off
                    sch.op("scalar", lambda e, c=c, sl=sl, coef=coef,
                           x_src=x_src: e.activation(
                        mixt[:, sl:sl + TA],
                        x_src[:, c * C514:c * C514 + TA],
                        AF.Identity, bias=0.0,
                        scale=cder[:, coef + c:coef + c + 1]),
                        reads=[x_src], sreads=[cder], writes=[mixt])
                    if c >= 4:
                        sch.op("vector", lambda e, c=c, sl=sl, x_src=x_src:
                            e.scalar_tensor_tensor(
                                mixt2[:, 0:TA],
                                x_src[:, c * C514 + 2:c * C514 + 2 + TA],
                                craw[:, cm_off + c:cm_off + c + 1],
                                mixt[:, sl:sl + TA], ALU.mult, ALU.add),
                            reads=[x_src, mixt], sreads=[craw],
                            writes=[mixt2])
                    src, soff = (mixt, sl) if c < 4 else (mixt2, 0)
                    sch.op("vector", lambda e, c=c, soff=soff, src=src,
                           x_src=x_src: e.scalar_tensor_tensor(
                        out_buf[:, c * TA:(c + 1) * TA],
                        x_src[:, c * C514 + 1:c * C514 + 1 + TA],
                        craw[:, tm_off + c:tm_off + c + 1],
                        src[:, soff:soff + TA], ALU.mult, ALU.add),
                        reads=[x_src, src], sreads=[craw], writes=[out_buf])

            def chain_stage(t):
                x1cur = x1b[t % 2]
                mix(x1cur, xma, TM, CM, D1M, D1MC)
                for oc in range(8):
                    for i, wsb in ((4, wk_b), (5, wv_b), (6, wr_b)):
                        for kc in range(8):
                            sch.op("tensor", lambda e, i=i, wsb=wsb, kc=kc,
                                   oc=oc: e.matmul(
                                pb[i].ap()[:, 0:TA],
                                wsb[:, kc * 1024 + oc * 128:
                                    kc * 1024 + oc * 128 + 128],
                                xma[:, kc * TA:(kc + 1) * TA],
                                start=(kc == 0), stop=(kc == 7)),
                                reads=[wsb, xma], writes=[pb[i]])
                    sch.op("vector", lambda e: e.tensor_scalar(
                        kbuf[:], pb[4].ap()[:, 0:TA], KCLAMP, None, ALU.min),
                        reads=[pb[4]], writes=[kbuf])
                    sch.op("scalar", lambda e: e.activation(
                        ebuf[:], kbuf[:], AF.Exp),
                        reads=[kbuf], writes=[ebuf])
                    if t == 0:
                        sch.op("vector", lambda e: e.tensor_scalar(
                            ebuf[:], ebuf[:], flg[:, 0:1], None, ALU.mult),
                            reads=[ebuf], sreads=[flg], writes=[ebuf])
                    sch.op("vector", lambda e: e.tensor_tensor(
                        ubuf[:], ebuf[:], pb[5].ap()[:, 0:TA], ALU.mult),
                        reads=[ebuf, pb[5]], writes=[ubuf])
                    sch.op("scalar", lambda e: e.activation(
                        sigb[:], pb[6].ap()[:, 0:TA], AF.Sigmoid),
                        reads=[pb[6]], writes=[sigb])
                    edc = cder[:, ED + oc:ED + oc + 1].broadcast_to([128, TA])
                    sch.op("vector", lambda e, oc=oc: e.tensor_copy(
                        sab[:, 0:1], aCar[:, oc:oc + 1]),
                        reads=[aCar], writes=[sab])
                    sch.op("vector", lambda e, edc=edc, oc=oc:
                        e.tensor_tensor_scan(
                            sab[:, 1:1 + TA], edc, ubuf[:],
                            aCar[:, oc:oc + 1], ALU.mult, ALU.add),
                        reads=[cder, ubuf], sreads=[aCar], writes=[sab])
                    sch.op("vector", lambda e, oc=oc: e.tensor_copy(
                        sbb[:, 0:1], bCar[:, oc:oc + 1]),
                        reads=[bCar], writes=[sbb])
                    sch.op("vector", lambda e, edc=edc, oc=oc:
                        e.tensor_tensor_scan(
                            sbb[:, 1:1 + TA], edc, ebuf[:],
                            bCar[:, oc:oc + 1], ALU.mult, ALU.add),
                        reads=[cder, ebuf], sreads=[bCar], writes=[sbb])
                    sch.op("vector", lambda e, oc=oc: e.tensor_copy(
                        aCar[:, oc:oc + 1], sab[:, TA:TA + 1]),
                        reads=[sab], writes=[aCar])
                    sch.op("vector", lambda e, oc=oc: e.scalar_tensor_tensor(
                        ubuf[:], ubuf[:], cder[:, EU + oc:EU + oc + 1],
                        sab[:, 0:TA], ALU.mult, ALU.add),
                        reads=[ubuf, sab], sreads=[cder], writes=[ubuf])
                    sch.op("vector", lambda e, oc=oc: e.scalar_tensor_tensor(
                        ebuf[:], ebuf[:], cder[:, EU + oc:EU + oc + 1],
                        sbb[:, 0:TA], ALU.mult, ALU.add),
                        reads=[ebuf, sbb], sreads=[cder], writes=[ebuf])
                    sch.op("vector", lambda e, oc=oc: e.tensor_copy(
                        bCar[:, oc:oc + 1], sbb[:, TA:TA + 1]),
                        reads=[sbb], writes=[bCar])
                    sch.op("scalar", lambda e: e.activation(
                        kbuf[:], ebuf[:], AF.Ln, bias=1e-35),
                        reads=[ebuf], sreads=[cnst], writes=[kbuf])
                    sch.op("scalar", lambda e: e.activation(
                        srnb[:], kbuf[:], AF.Exp, scale=-1.0),
                        reads=[kbuf], writes=[srnb])
                    sch.op("vector", lambda e: e.tensor_tensor(
                        ubuf[:], ubuf[:], sigb[:], ALU.mult),
                        reads=[ubuf, sigb], writes=[ubuf])
                    sch.op("vector", lambda e, oc=oc: e.tensor_tensor(
                        tmin[:, oc * TA:(oc + 1) * TA], ubuf[:],
                        srnb[:], ALU.mult),
                        reads=[ubuf, srnb], writes=[tmin])
                for oc in range(8):
                    bk = 4 + (oc % 2)
                    for kc in range(8):
                        sch.op("tensor", lambda e, bk=bk, kc=kc, oc=oc:
                            e.matmul(
                                pb[bk].ap()[:, 0:TA],
                                wo_b[:, kc * 1024 + oc * 128:
                                     kc * 1024 + oc * 128 + 128],
                                tmin[:, kc * TA:(kc + 1) * TA],
                                start=(kc == 0), stop=(kc == 7)),
                            reads=[wo_b, tmin], writes=[pb[bk]])
                    sch.op("vector", lambda e, oc=oc, bk=bk, x1cur=x1cur,
                           x2o=x2b[t % 2]: e.tensor_tensor(
                            x2o[:, oc * C514 + 1:oc * C514 + 1 + TA],
                            x1cur[:, oc * C514 + 1:oc * C514 + 1 + TA],
                            pb[bk].ap()[:, 0:TA], ALU.add),
                        reads=[x1cur, pb[bk]], writes=[x2b[t % 2]])
                x2in = x2b[t % 2]
                for c in range(8):
                    sl = (c % 2) * TA
                    sch.op("scalar", lambda e, c=c, sl=sl, x2in=x2in:
                        e.activation(
                        sqb2[:, sl:sl + TA],
                        x2in[:, c * C514 + 1:c * C514 + 1 + TA],
                        AF.Square), reads=[x2in], writes=[sqb2])
                    sch.op("tensor", lambda e, c=c, x2in=x2in: e.matmul(
                        pb[0].ap()[32:33, 0:TA], onesF[:, 0:1],
                        x2in[:, c * C514 + 1:c * C514 + 1 + TA],
                        start=(c == 0), stop=(c == 7)),
                        reads=[onesF, x2in], writes=[pb[0]])
                    sch.op("tensor", lambda e, c=c, sl=sl: e.matmul(
                        pb[1].ap()[32:33, 0:TA], onesB[:, 0:1],
                        sqb2[:, sl:sl + TA],
                        start=(c == 0), stop=(c == 7)),
                        reads=[onesB, sqb2], writes=[pb[1]])
                P2 = 32
                R2 = lambda i: rowsc[P2:P2 + 1, i * TA:(i + 1) * TA]
                sch.op("vector", lambda e, R2=R2: e.tensor_scalar(
                    R2(0), pb[0].ap()[32:33, 0:TA], 1.0 / 1024, None,
                    ALU.mult), reads=[pb[0]], writes=[rowsc])
                sch.op("vector", lambda e, R2=R2: e.tensor_scalar(
                    R2(1), pb[1].ap()[32:33, 0:TA], 1.0 / 1024, None,
                    ALU.mult), reads=[pb[1]], writes=[rowsc])
                sch.op("vector", lambda e, R2=R2: e.tensor_tensor(
                    R2(2), R2(0), R2(0), ALU.mult),
                    reads=[rowsc], writes=[rowsc])
                sch.op("vector", lambda e, R2=R2: e.tensor_tensor(
                    R2(1), R2(1), R2(2), ALU.subtract),
                    reads=[rowsc], writes=[rowsc])
                sch.op("scalar", lambda e, R2=R2: e.activation(
                    R2(2), R2(1), AF.Ln, bias=EPS),
                    reads=[rowsc], sreads=[cnst], writes=[rowsc])
                sch.op("scalar", lambda e, R2=R2: e.activation(
                    R2(3), R2(2), AF.Exp, scale=-0.5),
                    reads=[rowsc], writes=[rowsc])
                sch.op("vector", lambda e, R2=R2: e.tensor_tensor(
                    R2(0), R2(0), R2(3), ALU.mult),
                    reads=[rowsc], writes=[rowsc])
                sch.op("tensor", lambda e, R2=R2: e.matmul(
                    pb[2].ap()[:, 0:TA], onesR[32:33, :], R2(3),
                    start=True, stop=True),
                    reads=[onesR, rowsc], writes=[pb[2]])
                sch.op("tensor", lambda e, R2=R2: e.matmul(
                    pb[3].ap()[:, 0:TA], onesR[32:33, :], R2(0),
                    start=True, stop=True),
                    reads=[onesR, rowsc], writes=[pb[3]])
                x2o = x2b[t % 2]
                for c in range(8):
                    sch.op("vector", lambda e, c=c, x2in=x2in: e.tensor_tensor(
                        tmpg[:], x2in[:, c * C514 + 1:c * C514 + 1 + TA],
                        pb[2].ap()[:, 0:TA], ALU.mult),
                        reads=[x2in, pb[2]], writes=[tmpg])
                    sch.op("vector", lambda e: e.tensor_tensor(
                        tmpg[:], tmpg[:], pb[3].ap()[:, 0:TA], ALU.subtract),
                        reads=[tmpg, pb[3]], writes=[tmpg])
                    sch.op("scalar", lambda e, c=c, x2o=x2o: e.activation(
                        x2o[:, c * C514 + 1:c * C514 + 1 + TA], tmpg[:],
                        AF.Identity, bias=craw[:, B2 + c:B2 + c + 1],
                        scale=craw[:, G2 + c:G2 + c + 1]),
                        reads=[tmpg], sreads=[craw], writes=[x2o])
                if t == 0:
                    sch.op("vector", lambda e, x2o=x2o: e.tensor_scalar(
                        x2o[:], x2o[:], flg[:, 0:1], None, ALU.mult),
                        reads=[x2o], sreads=[flg], writes=[x2o])
                if t == NT - 1:
                    sch.op("vector", lambda e, x2o=x2o: e.tensor_scalar(
                        x2o[:], x2o[:], flg[:, 1:2], None, ALU.mult),
                        reads=[x2o], sreads=[flg], writes=[x2o])
                if t >= 1:
                    x2p = x2b[(t - 1) % 2]
                    sch.op("vector", lambda e, x2o=x2o, x2p=x2p: e.tensor_copy(
                        ch8(x2o, 8)[:, :, 0:1], ch8(x2p, 8)[:, :, 512:513]),
                        reads=[x2p], writes=[x2o])
                    sch.op("vector", lambda e, x2o=x2o, x2p=x2p: e.tensor_copy(
                        ch8(x2p, 8)[:, :, 513:514], ch8(x2o, 8)[:, :, 1:2]),
                        reads=[x2o], writes=[x2p])
                if 1 <= t <= 8:
                    sch.op("gpsimd", lambda e, t=t, x2o=x2o: e.dma_start(
                        out=hbm8(x2_s, 8)[:, :, (t - 1) * TA:t * TA],
                        in_=ch8(x2o, 8)[:, :, 1:1 + TA]),
                        reads=[x2o], dma=True)

            def mix2_stage(t):
                x2cur = x2b[t % 2]
                mix(x2cur, xmfb, TMF, CMF, D2M, D2MC)
                sch.op("sync", lambda e, t=t: e.dma_start(
                    out=hbm8(xmf_s, 8)[:, :, (t - 1) * TA:t * TA],
                    in_=ch8(xmfb)), reads=[xmfb], dma=True)

            for k in range(NT + 2):
                if k <= NT - 1:
                    x1_stage(k)
                if 1 <= k <= NT:
                    chain_stage(k - 1)
                if 1 <= k - 2 <= 8:
                    mix2_stage(k - 2)

            sch.fence()
        with ExitStack() as eb:
            sb = lambda nme, sh, dt: eb.enter_context(
                nc.sbuf_tensor(nme, sh, dt))
            fk_b = sb("fk_b", [128, 8 * 4096], BF16)
            fv_b = sb("fv_b", [128, 32 * 1024], BF16)
            fr_b = sb("fr_b", [128, 8192], BF16)
            xfb_r = [sb(f"xfb{i}", [128, 8 * TB], BF16) for i in range(2)]
            x2rb_r = [sb(f"x2rb{i}", [128, 8 * TB], BF16) for i in range(2)]
            fkb = sb("fkb_t", [128, 32 * TB], BF16)
            sfb = sb("sfb", [128, 8 * TB], F32)
            tmpq = sb("tmpq", [128, TB], F32)
            rlf = sb("rlf", [128, TB], F32)

            sch.op("sync", lambda e: e.dma_start(
                out=ch8(fr_b, 8), in_=hbm8(fr16, 8)),
                writes=[fr_b], dma=True)
            sch.op("sync", lambda e: e.dma_start(
                out=ch8(fk_b, 8), in_=hbm8(fk16, 8)),
                writes=[fk_b], dma=True)
            sch.op("gpsimd", lambda e: e.dma_start(
                out=ch8(fv_b, 32), in_=hbm8(fv16, 32)),
                writes=[fv_b], dma=True)

            for t in range(16):
                c0 = TB * t
                xfb = xfb_r[t % 2]
                x2rb = x2rb_r[t % 2]
                sch.op("sync", lambda e, xfb=xfb, c0=c0: e.dma_start(
                    out=ch8(xfb), in_=hbm8(xmf_s, 8)[:, :, c0:c0 + TB]),
                    writes=[xfb], dma=True)
                sch.op("gpsimd", lambda e, x2rb=x2rb, c0=c0: e.dma_start(
                    out=ch8(x2rb), in_=hbm8(x2_s, 8)[:, :, c0:c0 + TB]),
                    writes=[x2rb], dma=True)
                for oc in range(8):
                    bk = oc % 2
                    for kc in range(8):
                        sch.op("tensor", lambda e, bk=bk, kc=kc, oc=oc,
                               xfb=xfb: e.matmul(
                                pb[bk].ap()[:, 0:TB],
                                fr_b[:, kc * 1024 + oc * 128:
                                     kc * 1024 + oc * 128 + 128],
                                xfb[:, kc * TB:(kc + 1) * TB],
                                start=(kc == 0), stop=(kc == 7)),
                            reads=[fr_b, xfb], writes=[pb[bk]])
                    sch.op("scalar", lambda e, oc=oc, bk=bk: e.activation(
                        sfb[:, oc * TB:(oc + 1) * TB], pb[bk].ap()[:, 0:TB],
                        AF.Sigmoid), reads=[pb[bk]], writes=[sfb])
                for hc in range(32):
                    bk = 2 + hc % 2
                    for kc in range(8):
                        sch.op("tensor", lambda e, bk=bk, kc=kc, hc=hc,
                               xfb=xfb: e.matmul(
                                pb[bk].ap()[:, 0:TB],
                                fk_b[:, kc * 4096 + hc * 128:
                                     kc * 4096 + hc * 128 + 128],
                                xfb[:, kc * TB:(kc + 1) * TB],
                                start=(kc == 0), stop=(kc == 7)),
                            reads=[fk_b, xfb], writes=[pb[bk]])
                    sch.op("scalar", lambda e, bk=bk: e.activation(
                        rlf[:], pb[bk].ap()[:, 0:TB], AF.Relu),
                        reads=[pb[bk]], writes=[rlf])
                    sch.op("vector", lambda e, hc=hc: e.tensor_tensor(
                        fkb[:, hc * TB:(hc + 1) * TB], rlf[:], rlf[:],
                        ALU.mult), reads=[rlf], writes=[fkb])
                for oc in range(8):
                    bk = 4 + oc % 2
                    for kc in range(32):
                        sch.op("tensor", lambda e, bk=bk, kc=kc, oc=oc:
                            e.matmul(
                                pb[bk].ap()[:, 0:TB],
                                fv_b[:, kc * 1024 + oc * 128:
                                     kc * 1024 + oc * 128 + 128],
                                fkb[:, kc * TB:(kc + 1) * TB],
                                start=(kc == 0), stop=(kc == 31)),
                            reads=[fv_b, fkb], writes=[pb[bk]])
                    sch.op("vector", lambda e, oc=oc, bk=bk: e.tensor_tensor(
                        tmpq[:], sfb[:, oc * TB:(oc + 1) * TB],
                        pb[bk].ap()[:, 0:TB], ALU.mult),
                        reads=[sfb, pb[bk]], writes=[tmpq])
                    sch.op("vector", lambda e, oc=oc, x2rb=x2rb:
                        e.tensor_tensor(
                            x2rb[:, oc * TB:(oc + 1) * TB], tmpq[:],
                            x2rb[:, oc * TB:(oc + 1) * TB], ALU.add),
                        reads=[tmpq, x2rb], writes=[x2rb])
                    sch.op("sync", lambda e, oc=oc, c0=c0, x2rb=x2rb:
                        e.dma_start(
                            out=hbm8(attn_s, 8)[:, oc, c0:c0 + TB],
                            in_=x2rb[:, oc * TB:(oc + 1) * TB]),
                        reads=[x2rb], dma=True)

            sch.fence()
        with ExitStack() as ec:
            sb = lambda nme, sh, dt: ec.enter_context(
                nc.sbuf_tensor(nme, sh, dt))
            w1_8 = sb("w1_8", [128, 32768], FP8)
            w2_8 = sb("w2_8", [128, 32768], FP8)
            stt_b = [sb(f"sttb{i}", [64, TC], F32) for i in range(2)]
            xcb = [sb(f"xcb{i}", [128, 8 * TC], F32) for i in range(2)]
            acb = [sb(f"acb{i}", [128, 8 * TC], BF16) for i in range(2)]
            mkb = [sb(f"mkb{i}", [128, 8 * TC], BF16) for i in range(2)]
            xqb = sb("xqb", [128, 8 * TC], FP8)
            hbb = sb("hbb", [128, 4 * TC], BF16)
            h8b = sb("h8b", [128, 32 * TC], FP8)
            y2b = sb("y2b", [128, 8 * TC], BF16)
            outb = [sb(f"outb{i}", [128, 8 * TC], F32) for i in range(2)]
            tmpc = sb("tmpc", [128, TC], F32)

            sch.op("sync", lambda e: e.dma_start(out=w1_8[:], in_=w1q8),
                   writes=[w1_8], dma=True)
            sch.op("sync", lambda e: e.dma_start(out=w2_8[:], in_=w2q8),
                   writes=[w2_8], dma=True)

            def wdr(wsb, base):
                return wsb[:, base:base + 256].rearrange(
                    "p (j n) -> p j n", j=2)

            def adr(asb, gk):
                return asb[:, 2 * gk * TC:(2 * gk + 2) * TC].rearrange(
                    "p (j m) -> p j m", j=2)

            for t in range(8):
                c0 = TC * t
                xc = xcb[t % 2]
                ac = acb[t % 2]
                mk = mkb[t % 2]
                ob = outb[t % 2]
                stt = stt_b[t % 2]
                sch.op("sync", lambda e, xc=xc, c0=c0: e.dma_start(
                    out=ch8(xc), in_=hbm8(xh, 8)[:, :, H + c0:H + c0 + TC]),
                    writes=[xc], dma=True)
                sch.op("gpsimd", lambda e, ac=ac, c0=c0: e.dma_start(
                    out=ch8(ac), in_=hbm8(attn_s, 8)[:, :, c0:c0 + TC]),
                    writes=[ac], dma=True)
                sch.op("sync", lambda e, mk=mk, c0=c0: e.dma_start(
                    out=ch8(mk), in_=hbm8(mask_d, 8)[:, :, c0:c0 + TC]),
                    writes=[mk], dma=True)
                sch.op("sync", lambda e, stt=stt, c0=c0: e.dma_start(
                    out=stt[0:1, :], in_=stat_s[0:1, c0:c0 + TC]),
                    writes=[stt], dma=True)
                sch.op("sync", lambda e, stt=stt, c0=c0: e.dma_start(
                    out=stt[32:33, :], in_=stat_s[1:2, c0:c0 + TC]),
                    writes=[stt], dma=True)
                sch.op("tensor", lambda e, stt=stt: e.matmul(
                    pb[2].ap()[:, 0:TC], onesR[0:1, :],
                    stt[0:1, :], start=True, stop=True),
                    reads=[onesR, stt], writes=[pb[2]])
                sch.op("tensor", lambda e, stt=stt: e.matmul(
                    pb[3].ap()[:, 0:TC], onesR[32:33, :],
                    stt[32:33, :], start=True, stop=True),
                    reads=[onesR, stt], writes=[pb[3]])
                for c in range(8):
                    sch.op("vector", lambda e, c=c, xc=xc: e.tensor_tensor(
                        tmpc[:], xc[:, c * TC:(c + 1) * TC],
                        pb[2].ap()[:, 0:TC], ALU.mult),
                        reads=[xc, pb[2]], writes=[tmpc])
                    sch.op("vector", lambda e, c=c: e.tensor_tensor(
                        tmpc[:], tmpc[:],
                        pb[3].ap()[:, 0:TC], ALU.subtract),
                        reads=[tmpc, pb[3]], writes=[tmpc])
                    sch.op("vector", lambda e, c=c: e.tensor_copy(
                        xqb[:, c * TC:(c + 1) * TC], tmpc[:]),
                        reads=[tmpc], writes=[xqb])
                for hc in range(32):
                    bk = 4 + hc % 2
                    for gk in range(4):
                        sch.op("tensor", lambda e, bk=bk, gk=gk, hc=hc:
                            e.matmul(
                                pb[bk].ap()[:, 0:TC],
                                wdr(w1_8, (gk * 32 + hc) * 256),
                                adr(xqb, gk),
                                start=(gk == 0), stop=(gk == 3),
                                perf_mode=PM.DoubleRow),
                            reads=[w1_8, xqb], writes=[pb[bk]])
                    hs = (hc % 4) * TC
                    sch.op("scalar", lambda e, hc=hc, bk=bk, hs=hs:
                        e.activation(
                            hbb[:, hs:hs + TC], pb[bk].ap()[:, 0:TC],
                            AF.Relu, bias=b1q[:, hc:hc + 1], scale=1.0),
                        reads=[pb[bk]], sreads=[b1q], writes=[hbb])
                    sch.op("vector", lambda e, hc=hc, hs=hs: e.tensor_copy(
                        h8b[:, hc * TC:(hc + 1) * TC],
                        hbb[:, hs:hs + TC]),
                        reads=[hbb], writes=[h8b])
                for oc in range(8):
                    bk = 6 + oc % 2
                    for gk in range(16):
                        sch.op("tensor", lambda e, bk=bk, gk=gk, oc=oc:
                            e.matmul(
                                pb[bk].ap()[:, 0:TC],
                                wdr(w2_8, (gk * 8 + oc) * 256),
                                adr(h8b, gk),
                                start=(gk == 0), stop=(gk == 15),
                                perf_mode=PM.DoubleRow),
                            reads=[w2_8, h8b], writes=[pb[bk]])
                    sch.op("scalar", lambda e, oc=oc, bk=bk: e.activation(
                        y2b[:, oc * TC:(oc + 1) * TC], pb[bk].ap()[:, 0:TC],
                        AF.Identity, bias=b2q[:, oc:oc + 1], scale=1.0),
                        reads=[pb[bk]], sreads=[b2q], writes=[y2b])
                    sch.op("vector", lambda e, oc=oc, xc=xc, ac=ac, ob=ob:
                        e.tensor_tensor(
                            ob[:, oc * TC:(oc + 1) * TC],
                            xc[:, oc * TC:(oc + 1) * TC],
                            ac[:, oc * TC:(oc + 1) * TC], ALU.add),
                        reads=[xc, ac], writes=[ob])
                    sch.op("vector", lambda e, oc=oc, mk=mk: e.tensor_tensor(
                        tmpc[:], y2b[:, oc * TC:(oc + 1) * TC],
                        mk[:, oc * TC:(oc + 1) * TC], ALU.mult),
                        reads=[y2b, mk], writes=[tmpc])
                    sch.op("vector", lambda e, oc=oc, ob=ob: e.tensor_tensor(
                        ob[:, oc * TC:(oc + 1) * TC],
                        ob[:, oc * TC:(oc + 1) * TC], tmpc[:], ALU.add),
                        reads=[ob, tmpc], writes=[ob])
                sch.op("gpsimd", lambda e, ob=ob, c0=c0: e.dma_start(
                    out=hbm8(out_d, 8)[:, :, c0:c0 + TC], in_=ch8(ob)),
                    reads=[ob], dma=True)

            sch.fence()
            sch.op("gpsimd", lambda e: e.nop())

        sch.finalize()
        with nc.Block() as block:
            sch.flush(block, sems)
    return nc


_NC_CACHE = [None]


def _stage(inputs):
    x = np.asarray(inputs["L0_attn_x"], dtype=np.float32)
    B, T, C = x.shape
    assert (B, T, C) == (2, 16384, 1024)

    def wt(a):
        return np.ascontiguousarray(np.asarray(a, np.float32).T)

    def drpack(wT, ngk, noc):
        A = wT.reshape(ngk, 2, 128, noc, 128)
        return np.ascontiguousarray(
            A.transpose(2, 0, 3, 1, 4).reshape(128, -1))

    weights = dict(
        wkt=wt(inputs["L0_attn_Wk"]), wvt=wt(inputs["L0_attn_Wv"]),
        wrt=wt(inputs["L0_attn_Wr"]), wot=wt(inputs["L0_attn_Wo"]),
        frt=wt(inputs["L0_attn_Fr"]), fkt=wt(inputs["L0_attn_Fk"]),
        fvt=wt(inputs["L0_attn_Fv"]),
        w1d=drpack(wt(inputs["L0_ffn_W1"]), 4, 32),
        w2d=drpack(wt(inputs["L0_ffn_W2"]), 16, 8))

    def pack8(v, ncol):
        return np.ascontiguousarray(
            np.asarray(v, np.float32).reshape(ncol, 128).T)

    craw = np.zeros((128, 64), np.float32)
    for off, nme in ((TM, "L0_attn_att_time_mix"),
                     (CM, "L0_attn_att_combined_mix"),
                     (TMF, "L0_attn_ffn_time_mix"),
                     (CMF, "L0_attn_ffn_combined_mix"),
                     (G1, "L0_attn_ln1_g"), (B1, "L0_attn_ln1_b"),
                     (G2, "L0_attn_ln2_g"), (B2, "L0_attn_ln2_b")):
        craw[:, off:off + 8] = pack8(np.asarray(inputs[nme]).reshape(-1), 8)
    ctd = np.zeros((128, 16), np.float32)
    ctd[:, 0:8] = pack8(inputs["L0_attn_time_decay"], 8)
    ctd[:, 8:16] = pack8(inputs["L0_attn_time_first"], 8)
    b1r = pack8(inputs["L0_ffn_b1"], 32)
    b2r = pack8(inputs["L0_ffn_b2"], 8)

    mask = _bernoulli_mask(42, 0.8, (B, T, C))
    maskv = np.where(mask, np.float32(1.25),
                     np.float32(0.0)).astype(ml_dtypes.bfloat16)

    in_maps = []
    for core in range(NCORES):
        b, j = core // 4, core % 4
        t0 = j * CH
        lo, hi = t0 - H, t0 + CH + 2
        xr = np.zeros((NXC, 1024), np.float32)
        s0, s1 = max(lo, 0), min(hi, T)
        xr[s0 - lo:s1 - lo] = x[b, s0:s1]
        flgv = np.zeros((128, 2), np.float32)
        flgv[:, 0] = 0.0 if j == 0 else 1.0
        flgv[:, 1] = 0.0 if j == 3 else 1.0
        m = dict(weights)
        m.update(xh=np.ascontiguousarray(xr.T), craw=craw, ctd=ctd,
                 b1r=b1r, b2r=b2r, flg=flgv,
                 maskv=np.ascontiguousarray(maskv[b, t0:t0 + CH].T))
        in_maps.append(m)
    return in_maps


def kernel(**inputs):
    x = np.asarray(inputs["L0_attn_x"], dtype=np.float32)
    B, T, C = x.shape
    in_maps = _stage(inputs)
    if _NC_CACHE[0] is None:
        _NC_CACHE[0] = _build()
    nc = _NC_CACHE[0]
    res = run_bass_kernel_spmd(nc, in_maps, core_ids=list(range(NCORES)))

    out = np.empty((B, T, C), np.float32)
    for core in range(NCORES):
        b, j = core // 4, core % 4
        out[b, j * CH:(j + 1) * CH] = np.asarray(res.results[core]["out"]).T
    return out


# version 07
# speedup vs baseline: 1.0924x; 1.0246x over previous
import os
import numpy as np
import ml_dtypes
from contextlib import ExitStack

_PARANOID = os.environ.get("K_PARANOID", "") == "1"

import concourse.bass as bass
import concourse.mybir as mybir
from concourse.bass_utils import run_bass_kernel_spmd

F32 = mybir.dt.float32
BF16 = mybir.dt.bfloat16
FP8 = mybir.dt.float8e5
AF = mybir.ActivationFunctionType
ALU = mybir.AluOpType
PM = mybir.MatmulPerfMode

H = 512
CH = 4096
NXC = 5120
TA = 512
NT = 10
TB = 256
TC = 512
EPS = 1e-5
KCLAMP = 60.0
NCORES = 8
C514 = 514

TM, CM, TMF, CMF, G1, B1, G2, B2 = (i * 8 for i in range(8))
D1M, D1MC, D2M, D2MC, ED, EU = (i * 8 for i in range(6))


def _threefry2x32(k1, k2, x1, x2):
    x = [x1.astype(np.uint32).copy(), x2.astype(np.uint32).copy()]
    rot = [(13, 15, 26, 6), (17, 29, 16, 24)]
    ks = [np.uint32(k1), np.uint32(k2),
          np.uint32(k1) ^ np.uint32(k2) ^ np.uint32(0x1BD11BDA)]

    def rotl(v, r):
        return (v << np.uint32(r)) | (v >> np.uint32(32 - r))

    with np.errstate(over="ignore"):
        x[0] = x[0] + ks[0]
        x[1] = x[1] + ks[1]
        for i in range(5):
            for r in rot[i % 2]:
                x[0] = x[0] + x[1]
                x[1] = x[0] ^ rotl(x[1], r)
            x[0] = x[0] + ks[(i + 1) % 3]
            x[1] = x[1] + ks[(i + 2) % 3] + np.uint32(i + 1)
    return x[0], x[1]


def _bernoulli_mask(seed, p, shape):
    n = int(np.prod(shape))
    idx = np.arange(n, dtype=np.uint64)
    c1 = (idx >> np.uint64(32)).astype(np.uint32)
    c2 = (idx & np.uint64(0xFFFFFFFF)).astype(np.uint32)
    b1, b2 = _threefry2x32(np.uint32(seed >> 32), np.uint32(seed & 0xFFFFFFFF),
                           c1, c2)
    bits = b1 ^ b2
    fb = (bits >> np.uint32(9)) | np.uint32(0x3F800000)
    u = fb.view(np.float32) - np.float32(1.0)
    return (u < np.float32(p)).reshape(shape)


def _tname(t):
    if hasattr(t, "tensor"):
        t = t.tensor
    return t.name


class Sched:
    ENGS = ("sync", "tensor", "vector", "scalar", "gpsimd")
    COMPUTE = {"tensor", "vector", "scalar"}
    SEM_OF = {"sync": "LX", "tensor": "T", "vector": "V", "scalar": "S",
              "gpsimd": "G"}
    NROT = {"sync": 6, "gpsimd": 8}

    def __init__(self):
        self.ops = []
        self.state = {}
        self.epos = {e: 0 for e in self.ENGS}
        self.fence_deps = []

    def op(self, eng, fn, reads=(), writes=(), sreads=(), dma=False):
        idx = len(self.ops)
        deps = set(self.fence_deps)
        if _PARANOID and idx > 0:
            deps.add(idx - 1)
        sdeps = set()
        for t in reads:
            st = self.state.get(_tname(t))
            if st and st["w"] is not None:
                deps.add(st["w"])
        for t in sreads:
            st = self.state.get(_tname(t))
            if st and st["w"] is not None:
                sdeps.add(st["w"])
        for t in writes:
            st = self.state.get(_tname(t))
            if st:
                if st["w"] is not None:
                    deps.add(st["w"])
                deps.update(st["r"])
        o = dict(idx=idx, eng=eng, fn=fn, deps=deps, sdeps=sdeps, dma=dma,
                 pos=self.epos[eng], inc_needed=dma, lm=None)
        self.epos[eng] += 1
        self.ops.append(o)
        for t in list(reads) + list(sreads):
            self.state.setdefault(_tname(t), {"w": None, "r": []})["r"].append(idx)
        for t in writes:
            self.state[_tname(t)] = {"w": idx, "r": []}
        return idx

    def fence(self):
        last = {}
        for o in self.ops:
            last[o["eng"]] = o["idx"]
        self.fence_deps = list(last.values())

    def finalize(self):
        rot = {"sync": 0, "gpsimd": 0}
        for o in self.ops:
            if o["dma"]:
                i = rot[o["eng"]]
                o["sem"] = self.SEM_OF[o["eng"]] + str(i)
                rot[o["eng"]] = (i + 1) % self.NROT[o["eng"]]
            else:
                o["sem"] = self.SEM_OF[o["eng"]]
        for o in self.ops:
            waits = {}
            for d in o["deps"]:
                dop = self.ops[d]
                if (dop["eng"] == o["eng"] and o["eng"] in self.COMPUTE
                        and not dop["dma"]):
                    continue
                waits[d] = True
            for d in o["sdeps"]:
                dop = self.ops[d]
                same = (dop["eng"] == o["eng"] and o["eng"] in self.COMPUTE
                        and not dop["dma"])
                if same and (o["pos"] - dop["pos"]) >= 8:
                    continue
                waits[d] = True
            o["waitset"] = list(waits)
            for d in waits:
                self.ops[d]["inc_needed"] = True
        cnt = {}
        for o in self.ops:
            if o["dma"]:
                o["inc_needed"] = True
            if o["inc_needed"]:
                cnt[o["sem"]] = cnt.get(o["sem"], 0) + (16 if o["dma"] else 1)
                o["lm"] = cnt[o["sem"]]
        self.final_cnt = cnt

    def flush(self, block, sems):
        lastw = {e: {} for e in self.ENGS}
        seqs = {e: [] for e in self.ENGS}
        for o in self.ops:
            seqs[o["eng"]].append(o)
        for ename in self.ENGS:
            ops = seqs[ename]
            if not ops:
                continue

            def body(e, ops=ops, ename=ename):
                lw = lastw[ename]
                for o in ops:
                    wmax = {}
                    if o["dma"] and o["lm"] > 16:
                        wmax[o["sem"]] = o["lm"] - 16
                    for d in o["waitset"]:
                        dop = self.ops[d]
                        assert dop["lm"] is not None
                        wmax[dop["sem"]] = max(wmax.get(dop["sem"], 0),
                                               dop["lm"])
                    twait = False
                    for sn, v in wmax.items():
                        if lw.get(sn, 0) < v:
                            e.wait_ge(sems[sn], v)
                            lw[sn] = v
                            if sn == "T":
                                twait = True
                    ins = o["fn"](e)
                    if o["inc_needed"]:
                        ins.then_inc(sems[o["sem"]], 16 if o["dma"] else 1)

            getattr(block, ename)(body)


def _build():
    nc = bass.Bass("TRN2", target_bir_lowering=False,
                   detect_race_conditions=False)

    xh = nc.dram_tensor("xh", [1024, NXC], F32, kind="ExternalInput").ap()
    wkt = nc.dram_tensor("wkt", [1024, 1024], F32, kind="ExternalInput").ap()
    wvt = nc.dram_tensor("wvt", [1024, 1024], F32, kind="ExternalInput").ap()
    wrt = nc.dram_tensor("wrt", [1024, 1024], F32, kind="ExternalInput").ap()
    wot = nc.dram_tensor("wot", [1024, 1024], F32, kind="ExternalInput").ap()
    frt = nc.dram_tensor("frt", [1024, 1024], F32, kind="ExternalInput").ap()
    fkt = nc.dram_tensor("fkt", [1024, 4096], F32, kind="ExternalInput").ap()
    fvt = nc.dram_tensor("fvt", [4096, 1024], F32, kind="ExternalInput").ap()
    w1d = nc.dram_tensor("w1d", [128, 32768], F32, kind="ExternalInput").ap()
    w2d = nc.dram_tensor("w2d", [128, 32768], F32, kind="ExternalInput").ap()
    craw_d = nc.dram_tensor("craw", [128, 64], F32, kind="ExternalInput").ap()
    ctd_d = nc.dram_tensor("ctd", [128, 16], F32, kind="ExternalInput").ap()
    b1r_d = nc.dram_tensor("b1r", [128, 32], F32, kind="ExternalInput").ap()
    b2r_d = nc.dram_tensor("b2r", [128, 8], F32, kind="ExternalInput").ap()
    flg_d = nc.dram_tensor("flg", [128, 2], F32, kind="ExternalInput").ap()
    mask_d = nc.dram_tensor("maskv", [1024, CH], BF16, kind="ExternalInput").ap()
    out_d = nc.dram_tensor("out", [1024, CH], F32, kind="ExternalOutput").ap()
    xmf_s = nc.dram_tensor("xmf_s", [1024, CH], BF16).ap()
    x2_s = nc.dram_tensor("x2_s", [1024, CH], BF16).ap()
    attn_s = nc.dram_tensor("attn_s", [1024, CH], BF16).ap()
    stat_s = nc.dram_tensor("stat_s", [2, CH], F32).ap()
    fk16 = nc.dram_tensor("fk16", [1024, 4096], BF16).ap()
    fv16 = nc.dram_tensor("fv16", [4096, 1024], BF16).ap()
    fr16 = nc.dram_tensor("fr16", [1024, 1024], BF16).ap()
    w1q8 = nc.dram_tensor("w1q8", [128, 32768], FP8).ap()
    w2q8 = nc.dram_tensor("w2q8", [128, 32768], FP8).ap()

    def hbm8(ap, nchunk):
        return ap.rearrange("(c p) m -> p c m", p=128)

    def ch8(buf, nc_=8):
        return buf[:].rearrange("p (c m) -> p c m", c=nc_)

    sch = Sched()
    top = ExitStack()
    with top:
        semnames = (["V", "S", "T"]
                    + ["LX%d" % i for i in range(Sched.NROT["sync"])]
                    + ["G%d" % i for i in range(Sched.NROT["gpsimd"])])
        sems = {nm: top.enter_context(nc.semaphore(nm)) for nm in semnames}
        pb = [top.enter_context(nc.psum_tensor(f"pb{i}", [128, 512], F32))
              for i in range(8)]

        b1q = top.enter_context(nc.sbuf_tensor("b1q", [128, 32], F32))
        b2q = top.enter_context(nc.sbuf_tensor("b2q", [128, 8], F32))
        onesR = top.enter_context(nc.sbuf_tensor("onesR", [64, 128], F32))

        with ExitStack() as ea:
            sb = lambda nme, sh, dt: ea.enter_context(
                nc.sbuf_tensor(nme, sh, dt))
            craw = sb("craw_s", [128, 64], F32)
            cder = sb("cder_s", [128, 64], F32)
            ctd = sb("ctd_s", [128, 16], F32)
            q8t = sb("q8t", [128, 40], FP8)
            flg = sb("flg_s", [128, 2], F32)
            onesF = sb("onesF", [128, 1], F32)
            onesB = sb("onesB", [128, 1], BF16)
            cnst = sb("cnst", [128, 3], F32)
            sch.op("sync", lambda e: e.dma_start(out=craw[:], in_=craw_d),
                   writes=[craw], dma=True)
            sch.op("sync", lambda e: e.dma_start(out=ctd[:], in_=ctd_d),
                   writes=[ctd], dma=True)
            sch.op("sync", lambda e: e.dma_start(out=b1q[:], in_=b1r_d),
                   writes=[b1q], dma=True)
            sch.op("sync", lambda e: e.dma_start(out=b2q[:], in_=b2r_d),
                   writes=[b2q], dma=True)
            sch.op("sync", lambda e: e.dma_start(out=flg[:], in_=flg_d),
                   writes=[flg], dma=True)
            sch.op("vector", lambda e: e.memset(cnst[:, 0:1], EPS), writes=[cnst])
            sch.op("vector", lambda e: e.memset(cnst[:, 1:2], 1e-35), writes=[cnst])
            sch.op("vector", lambda e: e.memset(cnst[:, 2:3], 1.0), writes=[cnst])
            nc.const_aps.register(F32, EPS, cnst[:, 0:1])
            nc.const_aps.register(F32, 1e-35, cnst[:, 1:2])
            nc.const_aps.register(F32, 1.0, cnst[:, 2:3])
            sch.op("vector", lambda e: e.memset(onesF[:], 1.0), writes=[onesF])
            sch.op("vector", lambda e: e.memset(onesB[:], 1.0), writes=[onesB])
            sch.op("vector", lambda e: e.memset(onesR[:], 1.0), writes=[onesR])
            def _derive(final):
                if final:
                    sch.op("vector", lambda e: e.tensor_copy(q8t[:, 0:32], b1q[:]),
                           reads=[b1q], writes=[q8t])
                    sch.op("vector", lambda e: e.tensor_copy(q8t[:, 32:40], b2q[:]),
                           reads=[b2q], writes=[q8t])
                sch.op("vector", lambda e: e.tensor_scalar(
                    cder[:, D1M:D1M + 8], craw[:, TM:TM + 8], -1.0, 1.0,
                    ALU.mult, ALU.add), reads=[craw], writes=[cder])
                sch.op("vector", lambda e: e.tensor_tensor(
                    cder[:, D1MC:D1MC + 8], cder[:, D1M:D1M + 8],
                    craw[:, CM:CM + 8], ALU.add), reads=[craw, cder], writes=[cder])
                sch.op("vector", lambda e: e.tensor_scalar(
                    cder[:, D2M:D2M + 8], craw[:, TMF:TMF + 8], -1.0, 1.0,
                    ALU.mult, ALU.add), reads=[craw], writes=[cder])
                sch.op("vector", lambda e: e.tensor_tensor(
                    cder[:, D2MC:D2MC + 8], cder[:, D2M:D2M + 8],
                    craw[:, CMF:CMF + 8], ALU.add), reads=[craw, cder], writes=[cder])
                if final:
                    sch.op("vector", lambda e: e.memset(cnst[:, 0:1], EPS),
                           writes=[cnst])
                    sch.op("vector", lambda e: e.memset(cnst[:, 1:2], 1e-35),
                           writes=[cnst])
                    sch.op("vector", lambda e: e.memset(onesF[:], 1.0),
                           writes=[onesF])
                    sch.op("vector", lambda e: e.memset(onesB[:], 1.0),
                           writes=[onesB])
                    sch.op("vector", lambda e: e.memset(onesR[:], 1.0),
                           writes=[onesR])
                    sch.op("vector", lambda e: e.tensor_copy(b1q[:], q8t[:, 0:32]),
                           reads=[q8t], writes=[b1q])
                    sch.op("vector", lambda e: e.tensor_copy(b2q[:], q8t[:, 32:40]),
                           reads=[q8t], writes=[b2q])
                sch.op("scalar", lambda e: e.activation(
                    cder[:, ED:ED + 8], ctd[:, 0:8], AF.Exp),
                    reads=[ctd], writes=[cder])
                sch.op("scalar", lambda e: e.activation(
                    cder[:, ED:ED + 8], cder[:, ED:ED + 8], AF.Exp, scale=-1.0),
                    reads=[cder], writes=[cder])
                sch.op("scalar", lambda e: e.activation(
                    cder[:, EU:EU + 8], ctd[:, 8:16], AF.Exp),
                    reads=[ctd], writes=[cder])

            _derive(final=False)
            sch.fence()
            _derive(final=True)
            sch.fence()

            wk_b = sb("wk_b", [128, 8192], BF16)
            wv_b = sb("wv_b", [128, 8192], BF16)
            wr_b = sb("wr_b", [128, 8192], BF16)
            wo_b = sb("wo_b", [128, 8192], BF16)
            xt0 = sb("xt0", [128, 8 * TA], F32)
            xt = [xt0, xt0]
            sqf = sb("sqf", [128, TA], F32)
            sqb2 = sb("sqb2", [128, 2 * TA], BF16)
            x1b = [sb(f"x1b{i}", [128, 8 * C514], F32) for i in range(2)]
            x2b = [sb(f"x2b{i}", [128, 8 * C514], F32) for i in range(2)]
            xma = sb("xma", [128, 8 * TA], BF16)
            tmin = sb("tmin", [128, 8 * TA], BF16)
            xmfb = sb("xmfb", [128, 8 * TA], BF16)
            mixt = sb("mixt", [128, TA], F32)
            mixt2 = sb("mixt2", [128, TA], F32)
            tmpf = sb("tmpf", [128, TA], F32)
            tmpg = tmpf
            kbuf = sb("kbuf", [128, TA], F32)
            ebuf = sb("ebuf", [128, TA], F32)
            ubuf = sb("ubuf", [128, TA], F32)
            sigb = sb("sigb", [128, TA], F32)
            srnb = sb("srnb", [128, TA], F32)
            sab = sb("sab", [128, 513], F32)
            sbb = sb("sbb", [128, 513], F32)
            aCar = sb("aCar", [128, 8], F32)
            bCar = sb("bCar", [128, 8], F32)
            rowsc = sb("rowsc", [64, 5 * TA], F32)
            rsave = sb("rsave", [64, 2 * TA], F32)

            for t_ in (aCar, bCar, sab, sbb, x1b[0], x1b[1], x2b[0], x2b[1]):
                sch.op("vector", lambda e, t_=t_: e.memset(t_[:], 0.0),
                       writes=[t_])

            for dst, src in ((wk_b, wkt), (wv_b, wvt), (wr_b, wrt),
                             (wo_b, wot)):
                sch.op("gpsimd",
                       lambda e, dst=dst, src=src: e.dma_start(
                           out=ch8(dst), in_=hbm8(src, 8)),
                       writes=[dst], dma=True)
            precast_chunks = []
            for dst, src in ((fr16, frt), (fk16, fkt), (fv16, fvt)):
                nrow = dst.tensor.shape[0]
                step = max(nrow // 8, 512)
                for r0 in range(0, nrow, step):
                    precast_chunks.append((dst[r0:r0 + step, :],
                                           src[r0:r0 + step, :]))
            for dst, src in ((w1q8, w1d), (w2q8, w2d)):
                for c0 in range(0, 32768, 4096):
                    precast_chunks.append((dst[:, c0:c0 + 4096],
                                           src[:, c0:c0 + 4096]))

            def emit_precast(n):
                for _ in range(n):
                    if not precast_chunks:
                        return
                    dst, src = precast_chunks.pop(0)
                    sch.op("gpsimd", lambda e, dst=dst, src=src: e.dma_start(
                        out=dst, in_=src), dma=True)

            def x1_stage(t):
                c0 = TA * t
                xbuf = xt[t % 2]
                sch.op("sync", lambda e, xbuf=xbuf, c0=c0: e.dma_start(
                    out=ch8(xbuf), in_=hbm8(xh, 8)[:, :, c0:c0 + TA]),
                    writes=[xbuf], dma=True)
                for c in range(8):
                    sl = 0
                    sch.op("scalar", lambda e, c=c, sl=sl, xbuf=xbuf:
                        e.activation(sqf[:, sl:sl + TA],
                                     xbuf[:, c * TA:(c + 1) * TA], AF.Square),
                        reads=[xbuf], writes=[sqf])
                    sch.op("tensor", lambda e, c=c, xbuf=xbuf: e.matmul(
                        pb[0].ap()[0:1, 0:TA], onesF[:, 0:1],
                        xbuf[:, c * TA:(c + 1) * TA],
                        start=(c == 0), stop=(c == 7)),
                        reads=[onesF, xbuf], writes=[pb[0]])
                    sch.op("tensor", lambda e, c=c, sl=sl: e.matmul(
                        pb[1].ap()[0:1, 0:TA], onesF[:, 0:1],
                        sqf[:, sl:sl + TA],
                        start=(c == 0), stop=(c == 7)),
                        reads=[onesF, sqf], writes=[pb[1]])
                P0 = 0
                R = lambda i: rowsc[P0:P0 + 1, i * TA:(i + 1) * TA]
                sch.op("vector", lambda e, R=R: e.tensor_scalar(
                    R(0), pb[0].ap()[0:1, 0:TA], 1.0 / 1024, None,
                    ALU.mult), reads=[pb[0]], writes=[rowsc])
                sch.op("vector", lambda e, R=R: e.tensor_scalar(
                    R(1), pb[1].ap()[0:1, 0:TA], 1.0 / 1024, None,
                    ALU.mult), reads=[pb[1]], writes=[rowsc])
                sch.op("vector", lambda e, R=R: e.tensor_tensor(
                    R(2), R(0), R(0), ALU.mult),
                    reads=[rowsc], writes=[rowsc])
                sch.op("vector", lambda e, R=R: e.tensor_tensor(
                    R(1), R(1), R(2), ALU.subtract),
                    reads=[rowsc], writes=[rowsc])
                sch.op("scalar", lambda e, R=R: e.activation(
                    R(2), R(1), AF.Ln, bias=EPS),
                    reads=[rowsc], sreads=[cnst], writes=[rowsc])
                sch.op("scalar", lambda e, R=R: e.activation(
                    R(3), R(2), AF.Exp, scale=-0.5),
                    reads=[rowsc], writes=[rowsc])
                if 1 <= t <= 8:
                    sch.op("vector", lambda e, R=R: e.tensor_copy(
                        rsave[0:1, 0:TA], R(3)),
                        reads=[rowsc], writes=[rsave])
                    sch.op("vector", lambda e, R=R: e.tensor_tensor(
                        rsave[0:1, TA:2 * TA], R(0), R(3), ALU.mult),
                        reads=[rowsc], writes=[rsave])
                    sch.op("sync", lambda e, t=t: e.dma_start(
                        out=stat_s[0:1, (t - 1) * TA:t * TA],
                        in_=rsave[0:1, 0:TA]), reads=[rsave], dma=True)
                    sch.op("sync", lambda e, t=t: e.dma_start(
                        out=stat_s[1:2, (t - 1) * TA:t * TA],
                        in_=rsave[0:1, TA:2 * TA]), reads=[rsave], dma=True)
                sch.op("vector", lambda e, R=R: e.tensor_tensor(
                    R(2), R(3), R(3), ALU.mult),
                    reads=[rowsc], writes=[rowsc])
                sch.op("vector", lambda e, R=R: e.tensor_tensor(
                    R(1), R(1), R(2), ALU.mult),
                    reads=[rowsc], writes=[rowsc])
                sch.op("scalar", lambda e, R=R: e.activation(
                    R(2), R(1), AF.Ln, bias=EPS),
                    reads=[rowsc], sreads=[cnst], writes=[rowsc])
                sch.op("scalar", lambda e, R=R: e.activation(
                    R(4), R(2), AF.Exp, scale=-0.5),
                    reads=[rowsc], writes=[rowsc])
                sch.op("vector", lambda e, R=R: e.tensor_tensor(
                    R(4), R(3), R(4), ALU.mult),
                    reads=[rowsc], writes=[rowsc])
                sch.op("vector", lambda e, R=R: e.tensor_tensor(
                    R(0), R(0), R(4), ALU.mult),
                    reads=[rowsc], writes=[rowsc])
                sch.op("tensor", lambda e, R=R: e.matmul(
                    pb[2].ap()[:, 0:TA], onesR[0:1, :], R(4),
                    start=True, stop=True),
                    reads=[onesR, rowsc], writes=[pb[2]])
                sch.op("tensor", lambda e, R=R: e.matmul(
                    pb[3].ap()[:, 0:TA], onesR[0:1, :], R(0),
                    start=True, stop=True),
                    reads=[onesR, rowsc], writes=[pb[3]])
                x1o = x1b[t % 2]
                for c in range(8):
                    sch.op("vector", lambda e, c=c, xbuf=xbuf: e.tensor_tensor(
                        tmpf[:], xbuf[:, c * TA:(c + 1) * TA],
                        pb[2].ap()[:, 0:TA], ALU.mult),
                        reads=[xbuf, pb[2]], writes=[tmpf])
                    sch.op("vector", lambda e: e.tensor_tensor(
                        tmpf[:], tmpf[:], pb[3].ap()[:, 0:TA], ALU.subtract),
                        reads=[tmpf, pb[3]], writes=[tmpf])
                    sch.op("scalar", lambda e, c=c, x1o=x1o: e.activation(
                        x1o[:, c * C514 + 1:c * C514 + 1 + TA], tmpf[:],
                        AF.Identity, bias=craw[:, B1 + c:B1 + c + 1],
                        scale=craw[:, G1 + c:G1 + c + 1]),
                        reads=[tmpf], sreads=[craw], writes=[x1o])
                if t == 0:
                    sch.op("vector", lambda e, x1o=x1o: e.tensor_scalar(
                        x1o[:], x1o[:], flg[:, 0:1], None, ALU.mult),
                        reads=[x1o], sreads=[flg], writes=[x1o])
                if t == NT - 1:
                    sch.op("vector", lambda e, x1o=x1o: e.tensor_scalar(
                        x1o[:], x1o[:], flg[:, 1:2], None, ALU.mult),
                        reads=[x1o], sreads=[flg], writes=[x1o])
                if t >= 1:
                    x1p = x1b[(t - 1) % 2]
                    sch.op("vector", lambda e, x1o=x1o, x1p=x1p: e.tensor_copy(
                        ch8(x1o, 8)[:, :, 0:1], ch8(x1p, 8)[:, :, 512:513]),
                        reads=[x1p], writes=[x1o])
                    sch.op("vector", lambda e, x1o=x1o, x1p=x1p: e.tensor_copy(
                        ch8(x1p, 8)[:, :, 513:514], ch8(x1o, 8)[:, :, 1:2]),
                        reads=[x1o], writes=[x1p])

            def mix(x_src, out_buf, tm_off, cm_off, d1m_off, d1mc_off):
                for c in range(8):
                    sl = 0
                    coef = d1mc_off if c < 4 else d1m_off
                    sch.op("scalar", lambda e, c=c, sl=sl, coef=coef,
                           x_src=x_src: e.activation(
                        mixt[:, sl:sl + TA],
                        x_src[:, c * C514:c * C514 + TA],
                        AF.Identity, bias=0.0,
                        scale=cder[:, coef + c:coef + c + 1]),
                        reads=[x_src], sreads=[cder], writes=[mixt])
                    if c >= 4:
                        sch.op("vector", lambda e, c=c, sl=sl, x_src=x_src:
                            e.scalar_tensor_tensor(
                                mixt2[:, 0:TA],
                                x_src[:, c * C514 + 2:c * C514 + 2 + TA],
                                craw[:, cm_off + c:cm_off + c + 1],
                                mixt[:, sl:sl + TA], ALU.mult, ALU.add),
                            reads=[x_src, mixt], sreads=[craw],
                            writes=[mixt2])
                    src, soff = (mixt, sl) if c < 4 else (mixt2, 0)
                    sch.op("vector", lambda e, c=c, soff=soff, src=src,
                           x_src=x_src: e.scalar_tensor_tensor(
                        out_buf[:, c * TA:(c + 1) * TA],
                        x_src[:, c * C514 + 1:c * C514 + 1 + TA],
                        craw[:, tm_off + c:tm_off + c + 1],
                        src[:, soff:soff + TA], ALU.mult, ALU.add),
                        reads=[x_src, src], sreads=[craw], writes=[out_buf])

            def chain_stage(t):
                x1cur = x1b[t % 2]
                mix(x1cur, xma, TM, CM, D1M, D1MC)
                for oc in range(8):
                    for i, wsb in ((4, wk_b), (5, wv_b), (6, wr_b)):
                        for kc in range(8):
                            sch.op("tensor", lambda e, i=i, wsb=wsb, kc=kc,
                                   oc=oc: e.matmul(
                                pb[i].ap()[:, 0:TA],
                                wsb[:, kc * 1024 + oc * 128:
                                    kc * 1024 + oc * 128 + 128],
                                xma[:, kc * TA:(kc + 1) * TA],
                                start=(kc == 0), stop=(kc == 7)),
                                reads=[wsb, xma], writes=[pb[i]])
                    sch.op("vector", lambda e: e.tensor_scalar(
                        kbuf[:], pb[4].ap()[:, 0:TA], KCLAMP, None, ALU.min),
                        reads=[pb[4]], writes=[kbuf])
                    sch.op("scalar", lambda e: e.activation(
                        ebuf[:], kbuf[:], AF.Exp),
                        reads=[kbuf], writes=[ebuf])
                    if t == 0:
                        sch.op("vector", lambda e: e.tensor_scalar(
                            ebuf[:], ebuf[:], flg[:, 0:1], None, ALU.mult),
                            reads=[ebuf], sreads=[flg], writes=[ebuf])
                    sch.op("vector", lambda e: e.tensor_tensor(
                        ubuf[:], ebuf[:], pb[5].ap()[:, 0:TA], ALU.mult),
                        reads=[ebuf, pb[5]], writes=[ubuf])
                    sch.op("scalar", lambda e: e.activation(
                        sigb[:], pb[6].ap()[:, 0:TA], AF.Exp, scale=-1.0),
                        reads=[pb[6]], writes=[sigb])
                    edc = cder[:, ED + oc:ED + oc + 1].broadcast_to([128, TA])
                    sch.op("vector", lambda e, oc=oc: e.tensor_copy(
                        sab[:, 0:1], aCar[:, oc:oc + 1]),
                        reads=[aCar], writes=[sab])
                    sch.op("vector", lambda e, edc=edc, oc=oc:
                        e.tensor_tensor_scan(
                            sab[:, 1:1 + TA], edc, ubuf[:],
                            aCar[:, oc:oc + 1], ALU.mult, ALU.add),
                        reads=[cder, ubuf], sreads=[aCar], writes=[sab])
                    sch.op("vector", lambda e, oc=oc: e.tensor_copy(
                        sbb[:, 0:1], bCar[:, oc:oc + 1]),
                        reads=[bCar], writes=[sbb])
                    sch.op("vector", lambda e, edc=edc, oc=oc:
                        e.tensor_tensor_scan(
                            sbb[:, 1:1 + TA], edc, ebuf[:],
                            bCar[:, oc:oc + 1], ALU.mult, ALU.add),
                        reads=[cder, ebuf], sreads=[bCar], writes=[sbb])
                    sch.op("vector", lambda e, oc=oc: e.tensor_copy(
                        aCar[:, oc:oc + 1], sab[:, TA:TA + 1]),
                        reads=[sab], writes=[aCar])
                    sch.op("vector", lambda e, oc=oc: e.scalar_tensor_tensor(
                        ubuf[:], ubuf[:], cder[:, EU + oc:EU + oc + 1],
                        sab[:, 0:TA], ALU.mult, ALU.add),
                        reads=[ubuf, sab], sreads=[cder], writes=[ubuf])
                    sch.op("vector", lambda e, oc=oc: e.scalar_tensor_tensor(
                        ebuf[:], ebuf[:], cder[:, EU + oc:EU + oc + 1],
                        sbb[:, 0:TA], ALU.mult, ALU.add),
                        reads=[ebuf, sbb], sreads=[cder], writes=[ebuf])
                    sch.op("vector", lambda e, oc=oc: e.tensor_copy(
                        bCar[:, oc:oc + 1], sbb[:, TA:TA + 1]),
                        reads=[sbb], writes=[bCar])
                    sch.op("scalar", lambda e: e.activation(
                        kbuf[:], ebuf[:], AF.Ln, bias=1e-35),
                        reads=[ebuf], sreads=[cnst], writes=[kbuf])
                    sch.op("scalar", lambda e: e.activation(
                        srnb[:], sigb[:], AF.Ln, bias=1.0),
                        reads=[sigb], sreads=[cnst], writes=[srnb])
                    sch.op("vector", lambda e: e.tensor_tensor(
                        kbuf[:], kbuf[:], srnb[:], ALU.add),
                        reads=[kbuf, srnb], writes=[kbuf])
                    sch.op("scalar", lambda e: e.activation(
                        srnb[:], kbuf[:], AF.Exp, scale=-1.0),
                        reads=[kbuf], writes=[srnb])
                    sch.op("vector", lambda e, oc=oc: e.tensor_tensor(
                        tmin[:, oc * TA:(oc + 1) * TA], ubuf[:],
                        srnb[:], ALU.mult),
                        reads=[ubuf, srnb], writes=[tmin])
                for oc in range(8):
                    bk = 4 + (oc % 2)
                    for kc in range(8):
                        sch.op("tensor", lambda e, bk=bk, kc=kc, oc=oc:
                            e.matmul(
                                pb[bk].ap()[:, 0:TA],
                                wo_b[:, kc * 1024 + oc * 128:
                                     kc * 1024 + oc * 128 + 128],
                                tmin[:, kc * TA:(kc + 1) * TA],
                                start=(kc == 0), stop=(kc == 7)),
                            reads=[wo_b, tmin], writes=[pb[bk]])
                    sch.op("vector", lambda e, oc=oc, bk=bk, x1cur=x1cur,
                           x2o=x2b[t % 2]: e.tensor_tensor(
                            x2o[:, oc * C514 + 1:oc * C514 + 1 + TA],
                            x1cur[:, oc * C514 + 1:oc * C514 + 1 + TA],
                            pb[bk].ap()[:, 0:TA], ALU.add),
                        reads=[x1cur, pb[bk]], writes=[x2b[t % 2]])
                x2in = x2b[t % 2]
                for c in range(8):
                    sl = (c % 2) * TA
                    sch.op("scalar", lambda e, c=c, sl=sl, x2in=x2in:
                        e.activation(
                        sqb2[:, sl:sl + TA],
                        x2in[:, c * C514 + 1:c * C514 + 1 + TA],
                        AF.Square), reads=[x2in], writes=[sqb2])
                    sch.op("tensor", lambda e, c=c, x2in=x2in: e.matmul(
                        pb[0].ap()[32:33, 0:TA], onesF[:, 0:1],
                        x2in[:, c * C514 + 1:c * C514 + 1 + TA],
                        start=(c == 0), stop=(c == 7)),
                        reads=[onesF, x2in], writes=[pb[0]])
                    sch.op("tensor", lambda e, c=c, sl=sl: e.matmul(
                        pb[1].ap()[32:33, 0:TA], onesB[:, 0:1],
                        sqb2[:, sl:sl + TA],
                        start=(c == 0), stop=(c == 7)),
                        reads=[onesB, sqb2], writes=[pb[1]])
                P2 = 32
                R2 = lambda i: rowsc[P2:P2 + 1, i * TA:(i + 1) * TA]
                sch.op("vector", lambda e, R2=R2: e.tensor_scalar(
                    R2(0), pb[0].ap()[32:33, 0:TA], 1.0 / 1024, None,
                    ALU.mult), reads=[pb[0]], writes=[rowsc])
                sch.op("vector", lambda e, R2=R2: e.tensor_scalar(
                    R2(1), pb[1].ap()[32:33, 0:TA], 1.0 / 1024, None,
                    ALU.mult), reads=[pb[1]], writes=[rowsc])
                sch.op("vector", lambda e, R2=R2: e.tensor_tensor(
                    R2(2), R2(0), R2(0), ALU.mult),
                    reads=[rowsc], writes=[rowsc])
                sch.op("vector", lambda e, R2=R2: e.tensor_tensor(
                    R2(1), R2(1), R2(2), ALU.subtract),
                    reads=[rowsc], writes=[rowsc])
                sch.op("scalar", lambda e, R2=R2: e.activation(
                    R2(2), R2(1), AF.Ln, bias=EPS),
                    reads=[rowsc], sreads=[cnst], writes=[rowsc])
                sch.op("scalar", lambda e, R2=R2: e.activation(
                    R2(3), R2(2), AF.Exp, scale=-0.5),
                    reads=[rowsc], writes=[rowsc])
                sch.op("vector", lambda e, R2=R2: e.tensor_tensor(
                    R2(0), R2(0), R2(3), ALU.mult),
                    reads=[rowsc], writes=[rowsc])
                sch.op("tensor", lambda e, R2=R2: e.matmul(
                    pb[2].ap()[:, 0:TA], onesR[32:33, :], R2(3),
                    start=True, stop=True),
                    reads=[onesR, rowsc], writes=[pb[2]])
                sch.op("tensor", lambda e, R2=R2: e.matmul(
                    pb[3].ap()[:, 0:TA], onesR[32:33, :], R2(0),
                    start=True, stop=True),
                    reads=[onesR, rowsc], writes=[pb[3]])
                x2o = x2b[t % 2]
                for c in range(8):
                    sch.op("vector", lambda e, c=c, x2in=x2in: e.tensor_tensor(
                        tmpg[:], x2in[:, c * C514 + 1:c * C514 + 1 + TA],
                        pb[2].ap()[:, 0:TA], ALU.mult),
                        reads=[x2in, pb[2]], writes=[tmpg])
                    sch.op("vector", lambda e: e.tensor_tensor(
                        tmpg[:], tmpg[:], pb[3].ap()[:, 0:TA], ALU.subtract),
                        reads=[tmpg, pb[3]], writes=[tmpg])
                    sch.op("scalar", lambda e, c=c, x2o=x2o: e.activation(
                        x2o[:, c * C514 + 1:c * C514 + 1 + TA], tmpg[:],
                        AF.Identity, bias=craw[:, B2 + c:B2 + c + 1],
                        scale=craw[:, G2 + c:G2 + c + 1]),
                        reads=[tmpg], sreads=[craw], writes=[x2o])
                if t == 0:
                    sch.op("vector", lambda e, x2o=x2o: e.tensor_scalar(
                        x2o[:], x2o[:], flg[:, 0:1], None, ALU.mult),
                        reads=[x2o], sreads=[flg], writes=[x2o])
                if t == NT - 1:
                    sch.op("vector", lambda e, x2o=x2o: e.tensor_scalar(
                        x2o[:], x2o[:], flg[:, 1:2], None, ALU.mult),
                        reads=[x2o], sreads=[flg], writes=[x2o])
                if t >= 1:
                    x2p = x2b[(t - 1) % 2]
                    sch.op("vector", lambda e, x2o=x2o, x2p=x2p: e.tensor_copy(
                        ch8(x2o, 8)[:, :, 0:1], ch8(x2p, 8)[:, :, 512:513]),
                        reads=[x2p], writes=[x2o])
                    sch.op("vector", lambda e, x2o=x2o, x2p=x2p: e.tensor_copy(
                        ch8(x2p, 8)[:, :, 513:514], ch8(x2o, 8)[:, :, 1:2]),
                        reads=[x2o], writes=[x2p])
                if 1 <= t <= 8:
                    sch.op("gpsimd", lambda e, t=t, x2o=x2o: e.dma_start(
                        out=hbm8(x2_s, 8)[:, :, (t - 1) * TA:t * TA],
                        in_=ch8(x2o, 8)[:, :, 1:1 + TA]),
                        reads=[x2o], dma=True)

            def mix2_stage(t):
                x2cur = x2b[t % 2]
                mix(x2cur, xmfb, TMF, CMF, D2M, D2MC)
                sch.op("sync", lambda e, t=t: e.dma_start(
                    out=hbm8(xmf_s, 8)[:, :, (t - 1) * TA:t * TA],
                    in_=ch8(xmfb)), reads=[xmfb], dma=True)

            for k in range(NT + 2):
                if k <= NT - 1:
                    x1_stage(k)
                if 1 <= k <= NT:
                    chain_stage(k - 1)
                if 1 <= k - 2 <= 8:
                    mix2_stage(k - 2)
                emit_precast(3 if k > 0 else 0)
            emit_precast(len(precast_chunks))

            sch.fence()
        with ExitStack() as eb:
            sb = lambda nme, sh, dt: eb.enter_context(
                nc.sbuf_tensor(nme, sh, dt))
            fk_b = sb("fk_b", [128, 8 * 4096], BF16)
            fv_b = sb("fv_b", [128, 32 * 1024], BF16)
            fr_b = sb("fr_b", [128, 8192], BF16)
            xfb_r = [sb(f"xfb{i}", [128, 8 * TB], BF16) for i in range(2)]
            x2rb_r = [sb(f"x2rb{i}", [128, 8 * TB], BF16) for i in range(2)]
            fkb = sb("fkb_t", [128, 32 * TB], BF16)
            sfb = sb("sfb", [128, 8 * TB], F32)
            tmpq = sb("tmpq", [128, TB], F32)
            rlf = sb("rlf", [128, TB], F32)

            sch.op("sync", lambda e: e.dma_start(
                out=ch8(fr_b, 8), in_=hbm8(fr16, 8)),
                writes=[fr_b], dma=True)
            sch.op("sync", lambda e: e.dma_start(
                out=ch8(fk_b, 8), in_=hbm8(fk16, 8)),
                writes=[fk_b], dma=True)
            sch.op("gpsimd", lambda e: e.dma_start(
                out=ch8(fv_b, 32), in_=hbm8(fv16, 32)),
                writes=[fv_b], dma=True)

            for t in range(16):
                c0 = TB * t
                xfb = xfb_r[t % 2]
                x2rb = x2rb_r[t % 2]
                sch.op("sync", lambda e, xfb=xfb, c0=c0: e.dma_start(
                    out=ch8(xfb), in_=hbm8(xmf_s, 8)[:, :, c0:c0 + TB]),
                    writes=[xfb], dma=True)
                sch.op("gpsimd", lambda e, x2rb=x2rb, c0=c0: e.dma_start(
                    out=ch8(x2rb), in_=hbm8(x2_s, 8)[:, :, c0:c0 + TB]),
                    writes=[x2rb], dma=True)
                for oc in range(8):
                    bk = oc % 2
                    for kc in range(8):
                        sch.op("tensor", lambda e, bk=bk, kc=kc, oc=oc,
                               xfb=xfb: e.matmul(
                                pb[bk].ap()[:, 0:TB],
                                fr_b[:, kc * 1024 + oc * 128:
                                     kc * 1024 + oc * 128 + 128],
                                xfb[:, kc * TB:(kc + 1) * TB],
                                start=(kc == 0), stop=(kc == 7)),
                            reads=[fr_b, xfb], writes=[pb[bk]])
                    sch.op("scalar", lambda e, oc=oc, bk=bk: e.activation(
                        sfb[:, oc * TB:(oc + 1) * TB], pb[bk].ap()[:, 0:TB],
                        AF.Sigmoid), reads=[pb[bk]], writes=[sfb])
                for hc in range(32):
                    bk = 2 + hc % 2
                    for kc in range(8):
                        sch.op("tensor", lambda e, bk=bk, kc=kc, hc=hc,
                               xfb=xfb: e.matmul(
                                pb[bk].ap()[:, 0:TB],
                                fk_b[:, kc * 4096 + hc * 128:
                                     kc * 4096 + hc * 128 + 128],
                                xfb[:, kc * TB:(kc + 1) * TB],
                                start=(kc == 0), stop=(kc == 7)),
                            reads=[fk_b, xfb], writes=[pb[bk]])
                    sch.op("scalar", lambda e, bk=bk: e.activation(
                        rlf[:], pb[bk].ap()[:, 0:TB], AF.Relu),
                        reads=[pb[bk]], writes=[rlf])
                    sch.op("vector", lambda e, hc=hc: e.tensor_tensor(
                        fkb[:, hc * TB:(hc + 1) * TB], rlf[:], rlf[:],
                        ALU.mult), reads=[rlf], writes=[fkb])
                for oc in range(8):
                    bk = 4 + oc % 2
                    for kc in range(32):
                        sch.op("tensor", lambda e, bk=bk, kc=kc, oc=oc:
                            e.matmul(
                                pb[bk].ap()[:, 0:TB],
                                fv_b[:, kc * 1024 + oc * 128:
                                     kc * 1024 + oc * 128 + 128],
                                fkb[:, kc * TB:(kc + 1) * TB],
                                start=(kc == 0), stop=(kc == 31)),
                            reads=[fv_b, fkb], writes=[pb[bk]])
                    sch.op("vector", lambda e, oc=oc, bk=bk: e.tensor_tensor(
                        tmpq[:], sfb[:, oc * TB:(oc + 1) * TB],
                        pb[bk].ap()[:, 0:TB], ALU.mult),
                        reads=[sfb, pb[bk]], writes=[tmpq])
                    sch.op("vector", lambda e, oc=oc, x2rb=x2rb:
                        e.tensor_tensor(
                            x2rb[:, oc * TB:(oc + 1) * TB], tmpq[:],
                            x2rb[:, oc * TB:(oc + 1) * TB], ALU.add),
                        reads=[tmpq, x2rb], writes=[x2rb])
                    sch.op("sync", lambda e, oc=oc, c0=c0, x2rb=x2rb:
                        e.dma_start(
                            out=hbm8(attn_s, 8)[:, oc, c0:c0 + TB],
                            in_=x2rb[:, oc * TB:(oc + 1) * TB]),
                        reads=[x2rb], dma=True)

            sch.fence()
        with ExitStack() as ec:
            sb = lambda nme, sh, dt: ec.enter_context(
                nc.sbuf_tensor(nme, sh, dt))
            w1_8 = sb("w1_8", [128, 32768], FP8)
            w2_8 = sb("w2_8", [128, 32768], FP8)
            stt_b = [sb(f"sttb{i}", [64, TC], F32) for i in range(2)]
            xcb = [sb(f"xcb{i}", [128, 8 * TC], F32) for i in range(2)]
            acb = [sb(f"acb{i}", [128, 8 * TC], BF16) for i in range(2)]
            mkb = [sb(f"mkb{i}", [128, 8 * TC], BF16) for i in range(2)]
            xqb = sb("xqb", [128, 8 * TC], FP8)
            hbb = sb("hbb", [128, 4 * TC], BF16)
            h8b = sb("h8b", [128, 32 * TC], FP8)
            y2b = sb("y2b", [128, 8 * TC], BF16)
            outb = [sb(f"outb{i}", [128, 8 * TC], F32) for i in range(2)]
            tmpc = sb("tmpc", [128, TC], F32)

            sch.op("sync", lambda e: e.dma_start(out=w1_8[:], in_=w1q8),
                   writes=[w1_8], dma=True)
            sch.op("sync", lambda e: e.dma_start(out=w2_8[:], in_=w2q8),
                   writes=[w2_8], dma=True)

            def wdr(wsb, base):
                return wsb[:, base:base + 256].rearrange(
                    "p (j n) -> p j n", j=2)

            def adr(asb, gk):
                return asb[:, 2 * gk * TC:(2 * gk + 2) * TC].rearrange(
                    "p (j m) -> p j m", j=2)

            for t in range(8):
                c0 = TC * t
                xc = xcb[t % 2]
                ac = acb[t % 2]
                mk = mkb[t % 2]
                ob = outb[t % 2]
                stt = stt_b[t % 2]
                sch.op("sync", lambda e, xc=xc, c0=c0: e.dma_start(
                    out=ch8(xc), in_=hbm8(xh, 8)[:, :, H + c0:H + c0 + TC]),
                    writes=[xc], dma=True)
                sch.op("gpsimd", lambda e, ac=ac, c0=c0: e.dma_start(
                    out=ch8(ac), in_=hbm8(attn_s, 8)[:, :, c0:c0 + TC]),
                    writes=[ac], dma=True)
                sch.op("sync", lambda e, mk=mk, c0=c0: e.dma_start(
                    out=ch8(mk), in_=hbm8(mask_d, 8)[:, :, c0:c0 + TC]),
                    writes=[mk], dma=True)
                sch.op("sync", lambda e, stt=stt, c0=c0: e.dma_start(
                    out=stt[0:1, :], in_=stat_s[0:1, c0:c0 + TC]),
                    writes=[stt], dma=True)
                sch.op("sync", lambda e, stt=stt, c0=c0: e.dma_start(
                    out=stt[32:33, :], in_=stat_s[1:2, c0:c0 + TC]),
                    writes=[stt], dma=True)
                sch.op("tensor", lambda e, stt=stt: e.matmul(
                    pb[2].ap()[:, 0:TC], onesR[0:1, :],
                    stt[0:1, :], start=True, stop=True),
                    reads=[onesR, stt], writes=[pb[2]])
                sch.op("tensor", lambda e, stt=stt: e.matmul(
                    pb[3].ap()[:, 0:TC], onesR[32:33, :],
                    stt[32:33, :], start=True, stop=True),
                    reads=[onesR, stt], writes=[pb[3]])
                for c in range(8):
                    sch.op("vector", lambda e, c=c, xc=xc: e.tensor_tensor(
                        tmpc[:], xc[:, c * TC:(c + 1) * TC],
                        pb[2].ap()[:, 0:TC], ALU.mult),
                        reads=[xc, pb[2]], writes=[tmpc])
                    sch.op("vector", lambda e, c=c: e.tensor_tensor(
                        tmpc[:], tmpc[:],
                        pb[3].ap()[:, 0:TC], ALU.subtract),
                        reads=[tmpc, pb[3]], writes=[tmpc])
                    sch.op("vector", lambda e, c=c: e.tensor_copy(
                        xqb[:, c * TC:(c + 1) * TC], tmpc[:]),
                        reads=[tmpc], writes=[xqb])
                for hc in range(32):
                    bk = 4 + hc % 2
                    for gk in range(4):
                        sch.op("tensor", lambda e, bk=bk, gk=gk, hc=hc:
                            e.matmul(
                                pb[bk].ap()[:, 0:TC],
                                wdr(w1_8, (gk * 32 + hc) * 256),
                                adr(xqb, gk),
                                start=(gk == 0), stop=(gk == 3),
                                perf_mode=PM.DoubleRow),
                            reads=[w1_8, xqb], writes=[pb[bk]])
                    hs = (hc % 4) * TC
                    sch.op("scalar", lambda e, hc=hc, bk=bk, hs=hs:
                        e.activation(
                            hbb[:, hs:hs + TC], pb[bk].ap()[:, 0:TC],
                            AF.Relu, bias=b1q[:, hc:hc + 1], scale=1.0),
                        reads=[pb[bk]], sreads=[b1q], writes=[hbb])
                    sch.op("vector", lambda e, hc=hc, hs=hs: e.tensor_copy(
                        h8b[:, hc * TC:(hc + 1) * TC],
                        hbb[:, hs:hs + TC]),
                        reads=[hbb], writes=[h8b])
                for oc in range(8):
                    bk = 6 + oc % 2
                    for gk in range(16):
                        sch.op("tensor", lambda e, bk=bk, gk=gk, oc=oc:
                            e.matmul(
                                pb[bk].ap()[:, 0:TC],
                                wdr(w2_8, (gk * 8 + oc) * 256),
                                adr(h8b, gk),
                                start=(gk == 0), stop=(gk == 15),
                                perf_mode=PM.DoubleRow),
                            reads=[w2_8, h8b], writes=[pb[bk]])
                    sch.op("scalar", lambda e, oc=oc, bk=bk: e.activation(
                        y2b[:, oc * TC:(oc + 1) * TC], pb[bk].ap()[:, 0:TC],
                        AF.Identity, bias=b2q[:, oc:oc + 1], scale=1.0),
                        reads=[pb[bk]], sreads=[b2q], writes=[y2b])
                    sch.op("vector", lambda e, oc=oc, xc=xc, ac=ac, ob=ob:
                        e.tensor_tensor(
                            ob[:, oc * TC:(oc + 1) * TC],
                            xc[:, oc * TC:(oc + 1) * TC],
                            ac[:, oc * TC:(oc + 1) * TC], ALU.add),
                        reads=[xc, ac], writes=[ob])
                    sch.op("vector", lambda e, oc=oc, mk=mk: e.tensor_tensor(
                        tmpc[:], y2b[:, oc * TC:(oc + 1) * TC],
                        mk[:, oc * TC:(oc + 1) * TC], ALU.mult),
                        reads=[y2b, mk], writes=[tmpc])
                    sch.op("vector", lambda e, oc=oc, ob=ob: e.tensor_tensor(
                        ob[:, oc * TC:(oc + 1) * TC],
                        ob[:, oc * TC:(oc + 1) * TC], tmpc[:], ALU.add),
                        reads=[ob, tmpc], writes=[ob])
                sch.op("gpsimd", lambda e, ob=ob, c0=c0: e.dma_start(
                    out=hbm8(out_d, 8)[:, :, c0:c0 + TC], in_=ch8(ob)),
                    reads=[ob], dma=True)

            sch.fence()
            sch.op("gpsimd", lambda e: e.nop())

        sch.finalize()
        with nc.Block() as block:
            sch.flush(block, sems)
    return nc


_NC_CACHE = [None]


def _stage(inputs):
    x = np.asarray(inputs["L0_attn_x"], dtype=np.float32)
    B, T, C = x.shape
    assert (B, T, C) == (2, 16384, 1024)

    def wt(a):
        return np.ascontiguousarray(np.asarray(a, np.float32).T)

    def drpack(wT, ngk, noc):
        A = wT.reshape(ngk, 2, 128, noc, 128)
        return np.ascontiguousarray(
            A.transpose(2, 0, 3, 1, 4).reshape(128, -1))

    weights = dict(
        wkt=wt(inputs["L0_attn_Wk"]), wvt=wt(inputs["L0_attn_Wv"]),
        wrt=wt(inputs["L0_attn_Wr"]), wot=wt(inputs["L0_attn_Wo"]),
        frt=wt(inputs["L0_attn_Fr"]), fkt=wt(inputs["L0_attn_Fk"]),
        fvt=wt(inputs["L0_attn_Fv"]),
        w1d=drpack(wt(inputs["L0_ffn_W1"]), 4, 32),
        w2d=drpack(wt(inputs["L0_ffn_W2"]), 16, 8))

    def pack8(v, ncol):
        return np.ascontiguousarray(
            np.asarray(v, np.float32).reshape(ncol, 128).T)

    craw = np.zeros((128, 64), np.float32)
    for off, nme in ((TM, "L0_attn_att_time_mix"),
                     (CM, "L0_attn_att_combined_mix"),
                     (TMF, "L0_attn_ffn_time_mix"),
                     (CMF, "L0_attn_ffn_combined_mix"),
                     (G1, "L0_attn_ln1_g"), (B1, "L0_attn_ln1_b"),
                     (G2, "L0_attn_ln2_g"), (B2, "L0_attn_ln2_b")):
        craw[:, off:off + 8] = pack8(np.asarray(inputs[nme]).reshape(-1), 8)
    ctd = np.zeros((128, 16), np.float32)
    ctd[:, 0:8] = pack8(inputs["L0_attn_time_decay"], 8)
    ctd[:, 8:16] = pack8(inputs["L0_attn_time_first"], 8)
    b1r = pack8(inputs["L0_ffn_b1"], 32)
    b2r = pack8(inputs["L0_ffn_b2"], 8)

    mask = _bernoulli_mask(42, 0.8, (B, T, C))
    maskv = np.where(mask, np.float32(1.25),
                     np.float32(0.0)).astype(ml_dtypes.bfloat16)

    in_maps = []
    for core in range(NCORES):
        b, j = core // 4, core % 4
        t0 = j * CH
        lo, hi = t0 - H, t0 + CH + 2
        xr = np.zeros((NXC, 1024), np.float32)
        s0, s1 = max(lo, 0), min(hi, T)
        xr[s0 - lo:s1 - lo] = x[b, s0:s1]
        flgv = np.zeros((128, 2), np.float32)
        flgv[:, 0] = 0.0 if j == 0 else 1.0
        flgv[:, 1] = 0.0 if j == 3 else 1.0
        m = dict(weights)
        m.update(xh=np.ascontiguousarray(xr.T), craw=craw, ctd=ctd,
                 b1r=b1r, b2r=b2r, flg=flgv,
                 maskv=np.ascontiguousarray(maskv[b, t0:t0 + CH].T))
        in_maps.append(m)
    return in_maps


def kernel(**inputs):
    x = np.asarray(inputs["L0_attn_x"], dtype=np.float32)
    B, T, C = x.shape
    in_maps = _stage(inputs)
    if _NC_CACHE[0] is None:
        _NC_CACHE[0] = _build()
    nc = _NC_CACHE[0]
    res = run_bass_kernel_spmd(nc, in_maps, core_ids=list(range(NCORES)))

    out = np.empty((B, T, C), np.float32)
    for core in range(NCORES):
        b, j = core // 4, core % 4
        out[b, j * CH:(j + 1) * CH] = np.asarray(res.results[core]["out"]).T
    return out


# version 09
# speedup vs baseline: 1.1348x; 1.0387x over previous
import os
import numpy as np
import ml_dtypes
from contextlib import ExitStack

_PARANOID = os.environ.get("K_PARANOID", "") == "1"

import concourse.bass as bass
import concourse.mybir as mybir
from concourse.bass_utils import run_bass_kernel_spmd

F32 = mybir.dt.float32
BF16 = mybir.dt.bfloat16
FP8 = mybir.dt.float8e5
AF = mybir.ActivationFunctionType
ALU = mybir.AluOpType
PM = mybir.MatmulPerfMode

H = 512
CH = 4096
NXC = 5120
TA = 512
NT = 10
TB = 256
TC = 512
EPS = 1e-5
KCLAMP = 60.0
NCORES = 8
C514 = 514

TM, CM, TMF, CMF, G1, B1, G2, B2 = (i * 8 for i in range(8))
D1M, D1MC, D2M, D2MC, ED, EU = (i * 8 for i in range(6))


def _threefry2x32(k1, k2, x1, x2):
    x = [x1.astype(np.uint32).copy(), x2.astype(np.uint32).copy()]
    rot = [(13, 15, 26, 6), (17, 29, 16, 24)]
    ks = [np.uint32(k1), np.uint32(k2),
          np.uint32(k1) ^ np.uint32(k2) ^ np.uint32(0x1BD11BDA)]

    def rotl(v, r):
        return (v << np.uint32(r)) | (v >> np.uint32(32 - r))

    with np.errstate(over="ignore"):
        x[0] = x[0] + ks[0]
        x[1] = x[1] + ks[1]
        for i in range(5):
            for r in rot[i % 2]:
                x[0] = x[0] + x[1]
                x[1] = x[0] ^ rotl(x[1], r)
            x[0] = x[0] + ks[(i + 1) % 3]
            x[1] = x[1] + ks[(i + 2) % 3] + np.uint32(i + 1)
    return x[0], x[1]


def _bernoulli_mask(seed, p, shape):
    n = int(np.prod(shape))
    idx = np.arange(n, dtype=np.uint64)
    c1 = (idx >> np.uint64(32)).astype(np.uint32)
    c2 = (idx & np.uint64(0xFFFFFFFF)).astype(np.uint32)
    b1, b2 = _threefry2x32(np.uint32(seed >> 32), np.uint32(seed & 0xFFFFFFFF),
                           c1, c2)
    bits = b1 ^ b2
    fb = (bits >> np.uint32(9)) | np.uint32(0x3F800000)
    u = fb.view(np.float32) - np.float32(1.0)
    return (u < np.float32(p)).reshape(shape)


def _tname(t):
    if hasattr(t, "tensor"):
        t = t.tensor
    return t.name


class Sched:
    ENGS = ("sync", "tensor", "vector", "scalar", "gpsimd")
    COMPUTE = {"tensor", "vector", "scalar"}
    SEM_OF = {"sync": "LX", "tensor": "T", "vector": "V", "scalar": "S",
              "gpsimd": "G"}
    NROT = {"sync": 6, "gpsimd": 8}

    def __init__(self):
        self.ops = []
        self.state = {}
        self.epos = {e: 0 for e in self.ENGS}
        self.fence_deps = []

    def op(self, eng, fn, reads=(), writes=(), sreads=(), dma=False):
        idx = len(self.ops)
        deps = set(self.fence_deps)
        if _PARANOID and idx > 0:
            deps.add(idx - 1)
        sdeps = set()
        for t in reads:
            st = self.state.get(_tname(t))
            if st and st["w"] is not None:
                deps.add(st["w"])
        for t in sreads:
            st = self.state.get(_tname(t))
            if st and st["w"] is not None:
                sdeps.add(st["w"])
        for t in writes:
            st = self.state.get(_tname(t))
            if st:
                if st["w"] is not None:
                    deps.add(st["w"])
                deps.update(st["r"])
        o = dict(idx=idx, eng=eng, fn=fn, deps=deps, sdeps=sdeps, dma=dma,
                 pos=self.epos[eng], inc_needed=dma, lm=None)
        self.epos[eng] += 1
        self.ops.append(o)
        for t in list(reads) + list(sreads):
            self.state.setdefault(_tname(t), {"w": None, "r": []})["r"].append(idx)
        for t in writes:
            self.state[_tname(t)] = {"w": idx, "r": []}
        return idx

    def fence(self):
        last = {}
        for o in self.ops:
            last[o["eng"]] = o["idx"]
        self.fence_deps = list(last.values())

    def finalize(self):
        rot = {"sync": 0, "gpsimd": 0}
        for o in self.ops:
            if o["dma"]:
                i = rot[o["eng"]]
                o["sem"] = self.SEM_OF[o["eng"]] + str(i)
                rot[o["eng"]] = (i + 1) % self.NROT[o["eng"]]
            else:
                o["sem"] = self.SEM_OF[o["eng"]]
        for o in self.ops:
            waits = {}
            for d in o["deps"]:
                dop = self.ops[d]
                if (dop["eng"] == o["eng"] and o["eng"] in self.COMPUTE
                        and not dop["dma"]):
                    continue
                waits[d] = True
            for d in o["sdeps"]:
                dop = self.ops[d]
                same = (dop["eng"] == o["eng"] and o["eng"] in self.COMPUTE
                        and not dop["dma"])
                if same and (o["pos"] - dop["pos"]) >= 8:
                    continue
                waits[d] = True
            o["waitset"] = list(waits)
            for d in waits:
                self.ops[d]["inc_needed"] = True
        cnt = {}
        for o in self.ops:
            if o["dma"]:
                o["inc_needed"] = True
            if o["inc_needed"]:
                cnt[o["sem"]] = cnt.get(o["sem"], 0) + (16 if o["dma"] else 1)
                o["lm"] = cnt[o["sem"]]
        self.final_cnt = cnt

    def flush(self, block, sems):
        lastw = {e: {} for e in self.ENGS}
        seqs = {e: [] for e in self.ENGS}
        for o in self.ops:
            seqs[o["eng"]].append(o)
        for ename in self.ENGS:
            ops = seqs[ename]
            if not ops:
                continue

            def body(e, ops=ops, ename=ename):
                lw = lastw[ename]
                for o in ops:
                    wmax = {}
                    if o["dma"] and o["lm"] > 16:
                        wmax[o["sem"]] = o["lm"] - 16
                    for d in o["waitset"]:
                        dop = self.ops[d]
                        assert dop["lm"] is not None
                        wmax[dop["sem"]] = max(wmax.get(dop["sem"], 0),
                                               dop["lm"])
                    twait = False
                    for sn, v in wmax.items():
                        if lw.get(sn, 0) < v:
                            e.wait_ge(sems[sn], v)
                            lw[sn] = v
                            if sn == "T":
                                twait = True
                    ins = o["fn"](e)
                    if o["inc_needed"]:
                        ins.then_inc(sems[o["sem"]], 16 if o["dma"] else 1)

            getattr(block, ename)(body)


def _build():
    nc = bass.Bass("TRN2", target_bir_lowering=False,
                   detect_race_conditions=False)

    xh = nc.dram_tensor("xh", [1024, NXC], F32, kind="ExternalInput").ap()
    wkt = nc.dram_tensor("wkt", [1024, 1024], F32, kind="ExternalInput").ap()
    wvt = nc.dram_tensor("wvt", [1024, 1024], F32, kind="ExternalInput").ap()
    wrt = nc.dram_tensor("wrt", [1024, 1024], F32, kind="ExternalInput").ap()
    wot = nc.dram_tensor("wot", [1024, 1024], F32, kind="ExternalInput").ap()
    frt = nc.dram_tensor("frt", [1024, 1024], F32, kind="ExternalInput").ap()
    fkt = nc.dram_tensor("fkt", [1024, 4096], F32, kind="ExternalInput").ap()
    fvt = nc.dram_tensor("fvt", [4096, 1024], F32, kind="ExternalInput").ap()
    w1d = nc.dram_tensor("w1d", [128, 32768], F32, kind="ExternalInput").ap()
    w2d = nc.dram_tensor("w2d", [128, 32768], F32, kind="ExternalInput").ap()
    craw_d = nc.dram_tensor("craw", [128, 64], F32, kind="ExternalInput").ap()
    ctd_d = nc.dram_tensor("ctd", [128, 16], F32, kind="ExternalInput").ap()
    b1r_d = nc.dram_tensor("b1r", [128, 32], F32, kind="ExternalInput").ap()
    b2r_d = nc.dram_tensor("b2r", [128, 8], F32, kind="ExternalInput").ap()
    flg_d = nc.dram_tensor("flg", [128, 2], F32, kind="ExternalInput").ap()
    mask_d = nc.dram_tensor("maskv", [1024, CH], BF16, kind="ExternalInput").ap()
    out_d = nc.dram_tensor("out", [1024, CH], F32, kind="ExternalOutput").ap()
    xmf_s = nc.dram_tensor("xmf_s", [1024, CH], BF16).ap()
    x2_s = nc.dram_tensor("x2_s", [1024, CH], BF16).ap()
    attn_s = nc.dram_tensor("attn_s", [1024, CH], BF16).ap()
    stat_s = nc.dram_tensor("stat_s", [2, CH], F32).ap()
    fk16 = nc.dram_tensor("fk16", [1024, 4096], BF16).ap()
    fv16 = nc.dram_tensor("fv16", [4096, 1024], BF16).ap()
    fr16 = nc.dram_tensor("fr16", [1024, 1024], BF16).ap()
    w1q8 = nc.dram_tensor("w1q8", [128, 32768], FP8).ap()
    w2q8 = nc.dram_tensor("w2q8", [128, 32768], FP8).ap()

    def hbm8(ap, nchunk):
        return ap.rearrange("(c p) m -> p c m", p=128)

    def ch8(buf, nc_=8):
        return buf[:].rearrange("p (c m) -> p c m", c=nc_)

    sch = Sched()
    top = ExitStack()
    with top:
        semnames = (["V", "S", "T"]
                    + ["LX%d" % i for i in range(Sched.NROT["sync"])]
                    + ["G%d" % i for i in range(Sched.NROT["gpsimd"])])
        sems = {nm: top.enter_context(nc.semaphore(nm)) for nm in semnames}
        pb = [top.enter_context(nc.psum_tensor(f"pb{i}", [128, 512], F32))
              for i in range(8)]

        b1q = top.enter_context(nc.sbuf_tensor("b1q", [128, 32], F32))
        b2q = top.enter_context(nc.sbuf_tensor("b2q", [128, 8], F32))
        onesR = top.enter_context(nc.sbuf_tensor("onesR", [64, 128], F32))

        with ExitStack() as ea:
            sb = lambda nme, sh, dt: ea.enter_context(
                nc.sbuf_tensor(nme, sh, dt))
            craw = sb("craw_s", [128, 64], F32)
            cder = sb("cder_s", [128, 64], F32)
            ctd = sb("ctd_s", [128, 16], F32)
            q8t = sb("q8t", [128, 40], FP8)
            flg = sb("flg_s", [128, 2], F32)
            onesF = sb("onesF", [128, 1], F32)
            onesB = sb("onesB", [128, 1], BF16)
            cnst = sb("cnst", [128, 3], F32)
            sch.op("sync", lambda e: e.dma_start(out=craw[:], in_=craw_d),
                   writes=[craw], dma=True)
            sch.op("sync", lambda e: e.dma_start(out=ctd[:], in_=ctd_d),
                   writes=[ctd], dma=True)
            sch.op("sync", lambda e: e.dma_start(out=b1q[:], in_=b1r_d),
                   writes=[b1q], dma=True)
            sch.op("sync", lambda e: e.dma_start(out=b2q[:], in_=b2r_d),
                   writes=[b2q], dma=True)
            sch.op("sync", lambda e: e.dma_start(out=flg[:], in_=flg_d),
                   writes=[flg], dma=True)
            sch.op("vector", lambda e: e.memset(cnst[:, 0:1], EPS), writes=[cnst])
            sch.op("vector", lambda e: e.memset(cnst[:, 1:2], 1e-35), writes=[cnst])
            sch.op("vector", lambda e: e.memset(cnst[:, 2:3], 1.0), writes=[cnst])
            nc.const_aps.register(F32, EPS, cnst[:, 0:1])
            nc.const_aps.register(F32, 1e-35, cnst[:, 1:2])
            nc.const_aps.register(F32, 1.0, cnst[:, 2:3])
            sch.op("vector", lambda e: e.memset(onesF[:], 1.0), writes=[onesF])
            sch.op("vector", lambda e: e.memset(onesB[:], 1.0), writes=[onesB])
            sch.op("vector", lambda e: e.memset(onesR[:], 1.0), writes=[onesR])
            def _derive(final):
                if final:
                    sch.op("vector", lambda e: e.tensor_copy(q8t[:, 0:32], b1q[:]),
                           reads=[b1q], writes=[q8t])
                    sch.op("vector", lambda e: e.tensor_copy(q8t[:, 32:40], b2q[:]),
                           reads=[b2q], writes=[q8t])
                sch.op("vector", lambda e: e.tensor_scalar(
                    cder[:, D1M:D1M + 8], craw[:, TM:TM + 8], -1.0, 1.0,
                    ALU.mult, ALU.add), reads=[craw], writes=[cder])
                sch.op("vector", lambda e: e.tensor_tensor(
                    cder[:, D1MC:D1MC + 8], cder[:, D1M:D1M + 8],
                    craw[:, CM:CM + 8], ALU.add), reads=[craw, cder], writes=[cder])
                sch.op("vector", lambda e: e.tensor_scalar(
                    cder[:, D2M:D2M + 8], craw[:, TMF:TMF + 8], -1.0, 1.0,
                    ALU.mult, ALU.add), reads=[craw], writes=[cder])
                sch.op("vector", lambda e: e.tensor_tensor(
                    cder[:, D2MC:D2MC + 8], cder[:, D2M:D2M + 8],
                    craw[:, CMF:CMF + 8], ALU.add), reads=[craw, cder], writes=[cder])
                if final:
                    sch.op("vector", lambda e: e.memset(cnst[:, 0:1], EPS),
                           writes=[cnst])
                    sch.op("vector", lambda e: e.memset(cnst[:, 1:2], 1e-35),
                           writes=[cnst])
                    sch.op("vector", lambda e: e.memset(onesF[:], 1.0),
                           writes=[onesF])
                    sch.op("vector", lambda e: e.memset(onesB[:], 1.0),
                           writes=[onesB])
                    sch.op("vector", lambda e: e.memset(onesR[:], 1.0),
                           writes=[onesR])
                    sch.op("vector", lambda e: e.tensor_copy(b1q[:], q8t[:, 0:32]),
                           reads=[q8t], writes=[b1q])
                    sch.op("vector", lambda e: e.tensor_copy(b2q[:], q8t[:, 32:40]),
                           reads=[q8t], writes=[b2q])
                sch.op("scalar", lambda e: e.activation(
                    cder[:, ED:ED + 8], ctd[:, 0:8], AF.Exp),
                    reads=[ctd], writes=[cder])
                sch.op("scalar", lambda e: e.activation(
                    cder[:, ED:ED + 8], cder[:, ED:ED + 8], AF.Exp, scale=-1.0),
                    reads=[cder], writes=[cder])
                sch.op("scalar", lambda e: e.activation(
                    cder[:, EU:EU + 8], ctd[:, 8:16], AF.Exp),
                    reads=[ctd], writes=[cder])

            _derive(final=False)
            sch.fence()
            _derive(final=True)
            sch.fence()

            wk_b = sb("wk_b", [128, 8192], BF16)
            wv_b = sb("wv_b", [128, 8192], BF16)
            wr_b = sb("wr_b", [128, 8192], BF16)
            wo_b = sb("wo_b", [128, 8192], BF16)
            xt0 = sb("xt0", [128, 8 * TA], F32)
            xt = [xt0, xt0]
            sqf = sb("sqf", [128, TA], F32)
            sqb2 = sb("sqb2", [128, 2 * TA], BF16)
            x1b = [sb(f"x1b{i}", [128, 8 * C514], F32) for i in range(2)]
            x2b = [sb(f"x2b{i}", [128, 8 * C514], F32) for i in range(2)]
            xma = sb("xma", [128, 8 * TA], BF16)
            tmin = sb("tmin", [128, 8 * TA], BF16)
            xmfb = sb("xmfb", [128, 8 * TA], BF16)
            mixt = sb("mixt", [128, TA], F32)
            mixt2 = sb("mixt2", [128, TA], F32)
            tmpf = sb("tmpf", [128, TA], F32)
            tmpg = tmpf
            kbuf = sb("kbuf", [128, TA], F32)
            ebuf = sb("ebuf", [128, TA], F32)
            ubuf = sb("ubuf", [128, TA], F32)
            sigb = sb("sigb", [128, TA], F32)
            srnb = sb("srnb", [128, TA], F32)
            sab = sb("sab", [128, 513], F32)
            sbb = sb("sbb", [128, 513], F32)
            aCar = sb("aCar", [128, 8], F32)
            bCar = sb("bCar", [128, 8], F32)
            rowsc = sb("rowsc", [64, 5 * TA], F32)
            rsave = sb("rsave", [64, 2 * TA], F32)

            for t_ in (aCar, bCar, sab, sbb, x1b[0], x1b[1], x2b[0], x2b[1]):
                sch.op("vector", lambda e, t_=t_: e.memset(t_[:], 0.0),
                       writes=[t_])

            for dst, src in ((wk_b, wkt), (wv_b, wvt), (wr_b, wrt),
                             (wo_b, wot)):
                sch.op("gpsimd",
                       lambda e, dst=dst, src=src: e.dma_start(
                           out=ch8(dst), in_=hbm8(src, 8)),
                       writes=[dst], dma=True)
            precast_chunks = []
            for dst, src in ((fr16, frt), (fk16, fkt), (fv16, fvt)):
                nrow = dst.tensor.shape[0]
                step = max(nrow // 8, 512)
                for r0 in range(0, nrow, step):
                    precast_chunks.append((dst[r0:r0 + step, :],
                                           src[r0:r0 + step, :]))
            for dst, src in ((w1q8, w1d), (w2q8, w2d)):
                for c0 in range(0, 32768, 4096):
                    precast_chunks.append((dst[:, c0:c0 + 4096],
                                           src[:, c0:c0 + 4096]))

            def emit_precast(n):
                for _ in range(n):
                    if not precast_chunks:
                        return
                    dst, src = precast_chunks.pop(0)
                    sch.op("gpsimd", lambda e, dst=dst, src=src: e.dma_start(
                        out=dst, in_=src), dma=True)

            def x1_stage(t, W=TA):
                c0 = TA * t
                xbuf = xt[t % 2]
                sch.op("sync", lambda e, xbuf=xbuf, c0=c0, W=W: e.dma_start(
                    out=ch8(xbuf)[:, :, 0:W],
                    in_=hbm8(xh, 8)[:, :, c0:c0 + W]),
                    writes=[xbuf], dma=True)
                for c in range(8):
                    sl = 0
                    sch.op("scalar", lambda e, c=c, sl=sl, xbuf=xbuf, W=W:
                        e.activation(sqf[:, sl:sl + W],
                                     xbuf[:, c * TA:c * TA + W], AF.Square),
                        reads=[xbuf], writes=[sqf])
                    sch.op("tensor", lambda e, c=c, xbuf=xbuf, W=W: e.matmul(
                        pb[0].ap()[0:1, 0:W], onesF[:, 0:1],
                        xbuf[:, c * TA:c * TA + W],
                        start=(c == 0), stop=(c == 7)),
                        reads=[onesF, xbuf], writes=[pb[0]])
                    sch.op("tensor", lambda e, c=c, sl=sl, W=W: e.matmul(
                        pb[1].ap()[0:1, 0:W], onesF[:, 0:1],
                        sqf[:, sl:sl + W],
                        start=(c == 0), stop=(c == 7)),
                        reads=[onesF, sqf], writes=[pb[1]])
                P0 = 0
                R = lambda i: rowsc[P0:P0 + 1, i * TA:i * TA + W]
                sch.op("vector", lambda e, R=R, W=W: e.tensor_scalar(
                    R(0), pb[0].ap()[0:1, 0:W], 1.0 / 1024, None,
                    ALU.mult), reads=[pb[0]], writes=[rowsc])
                sch.op("vector", lambda e, R=R, W=W: e.tensor_scalar(
                    R(1), pb[1].ap()[0:1, 0:W], 1.0 / 1024, None,
                    ALU.mult), reads=[pb[1]], writes=[rowsc])
                sch.op("vector", lambda e, R=R: e.tensor_tensor(
                    R(2), R(0), R(0), ALU.mult),
                    reads=[rowsc], writes=[rowsc])
                sch.op("vector", lambda e, R=R: e.tensor_tensor(
                    R(1), R(1), R(2), ALU.subtract),
                    reads=[rowsc], writes=[rowsc])
                sch.op("scalar", lambda e, R=R: e.activation(
                    R(2), R(1), AF.Ln, bias=EPS),
                    reads=[rowsc], sreads=[cnst], writes=[rowsc])
                sch.op("scalar", lambda e, R=R: e.activation(
                    R(3), R(2), AF.Exp, scale=-0.5),
                    reads=[rowsc], writes=[rowsc])
                if 1 <= t <= 8:
                    sch.op("vector", lambda e, R=R: e.tensor_copy(
                        rsave[0:1, 0:TA], R(3)),
                        reads=[rowsc], writes=[rsave])
                    sch.op("vector", lambda e, R=R: e.tensor_tensor(
                        rsave[0:1, TA:2 * TA], R(0), R(3), ALU.mult),
                        reads=[rowsc], writes=[rsave])
                    sch.op("sync", lambda e, t=t: e.dma_start(
                        out=stat_s[0:1, (t - 1) * TA:t * TA],
                        in_=rsave[0:1, 0:TA]), reads=[rsave], dma=True)
                    sch.op("sync", lambda e, t=t: e.dma_start(
                        out=stat_s[1:2, (t - 1) * TA:t * TA],
                        in_=rsave[0:1, TA:2 * TA]), reads=[rsave], dma=True)
                sch.op("vector", lambda e, R=R: e.tensor_tensor(
                    R(2), R(3), R(3), ALU.mult),
                    reads=[rowsc], writes=[rowsc])
                sch.op("vector", lambda e, R=R: e.tensor_tensor(
                    R(1), R(1), R(2), ALU.mult),
                    reads=[rowsc], writes=[rowsc])
                sch.op("scalar", lambda e, R=R: e.activation(
                    R(2), R(1), AF.Ln, bias=EPS),
                    reads=[rowsc], sreads=[cnst], writes=[rowsc])
                sch.op("scalar", lambda e, R=R: e.activation(
                    R(4), R(2), AF.Exp, scale=-0.5),
                    reads=[rowsc], writes=[rowsc])
                sch.op("vector", lambda e, R=R: e.tensor_tensor(
                    R(4), R(3), R(4), ALU.mult),
                    reads=[rowsc], writes=[rowsc])
                sch.op("vector", lambda e, R=R: e.tensor_tensor(
                    R(0), R(0), R(4), ALU.mult),
                    reads=[rowsc], writes=[rowsc])
                sch.op("tensor", lambda e, R=R, W=W: e.matmul(
                    pb[2].ap()[:, 0:W], onesR[0:1, :], R(4),
                    start=True, stop=True),
                    reads=[onesR, rowsc], writes=[pb[2]])
                sch.op("tensor", lambda e, R=R, W=W: e.matmul(
                    pb[3].ap()[:, 0:W], onesR[0:1, :], R(0),
                    start=True, stop=True),
                    reads=[onesR, rowsc], writes=[pb[3]])
                x1o = x1b[t % 2]
                for c in range(8):
                    sch.op("vector", lambda e, c=c, xbuf=xbuf, W=W:
                        e.tensor_tensor(
                            tmpf[:, 0:W], xbuf[:, c * TA:c * TA + W],
                            pb[2].ap()[:, 0:W], ALU.mult),
                        reads=[xbuf, pb[2]], writes=[tmpf])
                    sch.op("vector", lambda e, W=W: e.tensor_tensor(
                        tmpf[:, 0:W], tmpf[:, 0:W],
                        pb[3].ap()[:, 0:W], ALU.subtract),
                        reads=[tmpf, pb[3]], writes=[tmpf])
                    sch.op("scalar", lambda e, c=c, x1o=x1o, W=W: e.activation(
                        x1o[:, c * C514 + 1:c * C514 + 1 + W], tmpf[:, 0:W],
                        AF.Identity, bias=craw[:, B1 + c:B1 + c + 1],
                        scale=craw[:, G1 + c:G1 + c + 1]),
                        reads=[tmpf], sreads=[craw], writes=[x1o])
                if t == 0:
                    sch.op("vector", lambda e, x1o=x1o: e.tensor_scalar(
                        x1o[:], x1o[:], flg[:, 0:1], None, ALU.mult),
                        reads=[x1o], sreads=[flg], writes=[x1o])
                if t == NT - 1:
                    sch.op("vector", lambda e, x1o=x1o: e.tensor_scalar(
                        x1o[:], x1o[:], flg[:, 1:2], None, ALU.mult),
                        reads=[x1o], sreads=[flg], writes=[x1o])
                if t >= 1:
                    x1p = x1b[(t - 1) % 2]
                    sch.op("vector", lambda e, x1o=x1o, x1p=x1p: e.tensor_copy(
                        ch8(x1o, 8)[:, :, 0:1], ch8(x1p, 8)[:, :, 512:513]),
                        reads=[x1p], writes=[x1o])
                    sch.op("vector", lambda e, x1o=x1o, x1p=x1p: e.tensor_copy(
                        ch8(x1p, 8)[:, :, 513:514], ch8(x1o, 8)[:, :, 1:2]),
                        reads=[x1o], writes=[x1p])

            def mix(x_src, out_buf, tm_off, cm_off, d1m_off, d1mc_off,
                    W=TA):
                for c in range(8):
                    sl = 0
                    coef = d1mc_off if c < 4 else d1m_off
                    sch.op("scalar", lambda e, c=c, sl=sl, coef=coef,
                           x_src=x_src, W=W: e.activation(
                        mixt[:, sl:sl + W],
                        x_src[:, c * C514:c * C514 + W],
                        AF.Identity, bias=0.0,
                        scale=cder[:, coef + c:coef + c + 1]),
                        reads=[x_src], sreads=[cder], writes=[mixt])
                    if c >= 4:
                        sch.op("vector", lambda e, c=c, sl=sl, x_src=x_src,
                               W=W: e.scalar_tensor_tensor(
                                mixt2[:, 0:W],
                                x_src[:, c * C514 + 2:c * C514 + 2 + W],
                                craw[:, cm_off + c:cm_off + c + 1],
                                mixt[:, sl:sl + W], ALU.mult, ALU.add),
                            reads=[x_src, mixt], sreads=[craw],
                            writes=[mixt2])
                    src, soff = (mixt, sl) if c < 4 else (mixt2, 0)
                    sch.op("vector", lambda e, c=c, soff=soff, src=src,
                           x_src=x_src, W=W: e.scalar_tensor_tensor(
                        out_buf[:, c * TA:c * TA + W],
                        x_src[:, c * C514 + 1:c * C514 + 1 + W],
                        craw[:, tm_off + c:tm_off + c + 1],
                        src[:, soff:soff + W], ALU.mult, ALU.add),
                        reads=[x_src, src], sreads=[craw], writes=[out_buf])

            def chain_stage(t, W=TA):
                x1cur = x1b[t % 2]
                mix(x1cur, xma, TM, CM, D1M, D1MC, W=W)
                for oc in range(8):
                    for i, wsb in ((4, wk_b), (5, wv_b), (6, wr_b)):
                        for kc in range(8):
                            sch.op("tensor", lambda e, i=i, wsb=wsb, kc=kc,
                                   oc=oc, W=W: e.matmul(
                                pb[i].ap()[:, 0:W],
                                wsb[:, kc * 1024 + oc * 128:
                                    kc * 1024 + oc * 128 + 128],
                                xma[:, kc * TA:kc * TA + W],
                                start=(kc == 0), stop=(kc == 7)),
                                reads=[wsb, xma], writes=[pb[i]])
                    sch.op("vector", lambda e, W=W: e.tensor_scalar(
                        kbuf[:, 0:W], pb[4].ap()[:, 0:W], KCLAMP, None,
                        ALU.min), reads=[pb[4]], writes=[kbuf])
                    sch.op("scalar", lambda e, W=W: e.activation(
                        ebuf[:, 0:W], kbuf[:, 0:W], AF.Exp),
                        reads=[kbuf], writes=[ebuf])
                    if t == 0:
                        sch.op("vector", lambda e, W=W: e.tensor_scalar(
                            ebuf[:, 0:W], ebuf[:, 0:W], flg[:, 0:1], None,
                            ALU.mult), reads=[ebuf], sreads=[flg],
                            writes=[ebuf])
                    sch.op("vector", lambda e, W=W: e.tensor_tensor(
                        ubuf[:, 0:W], ebuf[:, 0:W], pb[5].ap()[:, 0:W],
                        ALU.mult), reads=[ebuf, pb[5]], writes=[ubuf])
                    sch.op("scalar", lambda e, W=W: e.activation(
                        sigb[:, 0:W], pb[6].ap()[:, 0:W], AF.Exp, scale=-1.0),
                        reads=[pb[6]], writes=[sigb])
                    edc = cder[:, ED + oc:ED + oc + 1].broadcast_to([128, W])
                    sch.op("vector", lambda e, oc=oc: e.tensor_copy(
                        sab[:, 0:1], aCar[:, oc:oc + 1]),
                        reads=[aCar], writes=[sab])
                    sch.op("vector", lambda e, edc=edc, oc=oc, W=W:
                        e.tensor_tensor_scan(
                            sab[:, 1:1 + W], edc, ubuf[:, 0:W],
                            aCar[:, oc:oc + 1], ALU.mult, ALU.add),
                        reads=[cder, ubuf], sreads=[aCar], writes=[sab])
                    sch.op("vector", lambda e, oc=oc: e.tensor_copy(
                        sbb[:, 0:1], bCar[:, oc:oc + 1]),
                        reads=[bCar], writes=[sbb])
                    sch.op("vector", lambda e, edc=edc, oc=oc, W=W:
                        e.tensor_tensor_scan(
                            sbb[:, 1:1 + W], edc, ebuf[:, 0:W],
                            bCar[:, oc:oc + 1], ALU.mult, ALU.add),
                        reads=[cder, ebuf], sreads=[bCar], writes=[sbb])
                    sch.op("vector", lambda e, oc=oc, W=W: e.tensor_copy(
                        aCar[:, oc:oc + 1], sab[:, W:W + 1]),
                        reads=[sab], writes=[aCar])
                    sch.op("vector", lambda e, oc=oc, W=W:
                        e.scalar_tensor_tensor(
                            ubuf[:, 0:W], ubuf[:, 0:W],
                            cder[:, EU + oc:EU + oc + 1],
                            sab[:, 0:W], ALU.mult, ALU.add),
                        reads=[ubuf, sab], sreads=[cder], writes=[ubuf])
                    sch.op("vector", lambda e, oc=oc, W=W:
                        e.scalar_tensor_tensor(
                            ebuf[:, 0:W], ebuf[:, 0:W],
                            cder[:, EU + oc:EU + oc + 1],
                            sbb[:, 0:W], ALU.mult, ALU.add),
                        reads=[ebuf, sbb], sreads=[cder], writes=[ebuf])
                    sch.op("vector", lambda e, oc=oc, W=W: e.tensor_copy(
                        bCar[:, oc:oc + 1], sbb[:, W:W + 1]),
                        reads=[sbb], writes=[bCar])
                    sch.op("scalar", lambda e, W=W: e.activation(
                        kbuf[:, 0:W], ebuf[:, 0:W], AF.Ln, bias=1e-35),
                        reads=[ebuf], sreads=[cnst], writes=[kbuf])
                    sch.op("scalar", lambda e, W=W: e.activation(
                        srnb[:, 0:W], sigb[:, 0:W], AF.Ln, bias=1.0),
                        reads=[sigb], sreads=[cnst], writes=[srnb])
                    sch.op("vector", lambda e, W=W: e.tensor_tensor(
                        kbuf[:, 0:W], kbuf[:, 0:W], srnb[:, 0:W], ALU.add),
                        reads=[kbuf, srnb], writes=[kbuf])
                    sch.op("scalar", lambda e, W=W: e.activation(
                        srnb[:, 0:W], kbuf[:, 0:W], AF.Exp, scale=-1.0),
                        reads=[kbuf], writes=[srnb])
                    sch.op("vector", lambda e, oc=oc, W=W: e.tensor_tensor(
                        tmin[:, oc * TA:oc * TA + W], ubuf[:, 0:W],
                        srnb[:, 0:W], ALU.mult),
                        reads=[ubuf, srnb], writes=[tmin])
                for oc in range(8):
                    bk = 4 + (oc % 2)
                    for kc in range(8):
                        sch.op("tensor", lambda e, bk=bk, kc=kc, oc=oc, W=W:
                            e.matmul(
                                pb[bk].ap()[:, 0:W],
                                wo_b[:, kc * 1024 + oc * 128:
                                     kc * 1024 + oc * 128 + 128],
                                tmin[:, kc * TA:kc * TA + W],
                                start=(kc == 0), stop=(kc == 7)),
                            reads=[wo_b, tmin], writes=[pb[bk]])
                    sch.op("vector", lambda e, oc=oc, bk=bk, x1cur=x1cur,
                           x2o=x2b[t % 2], W=W: e.tensor_tensor(
                            x2o[:, oc * C514 + 1:oc * C514 + 1 + W],
                            x1cur[:, oc * C514 + 1:oc * C514 + 1 + W],
                            pb[bk].ap()[:, 0:W], ALU.add),
                        reads=[x1cur, pb[bk]], writes=[x2b[t % 2]])
                x2in = x2b[t % 2]
                for c in range(8):
                    sl = (c % 2) * TA
                    sch.op("scalar", lambda e, c=c, sl=sl, x2in=x2in, W=W:
                        e.activation(
                        sqb2[:, sl:sl + W],
                        x2in[:, c * C514 + 1:c * C514 + 1 + W],
                        AF.Square), reads=[x2in], writes=[sqb2])
                    sch.op("tensor", lambda e, c=c, x2in=x2in, W=W: e.matmul(
                        pb[0].ap()[32:33, 0:W], onesF[:, 0:1],
                        x2in[:, c * C514 + 1:c * C514 + 1 + W],
                        start=(c == 0), stop=(c == 7)),
                        reads=[onesF, x2in], writes=[pb[0]])
                    sch.op("tensor", lambda e, c=c, sl=sl, W=W: e.matmul(
                        pb[1].ap()[32:33, 0:W], onesB[:, 0:1],
                        sqb2[:, sl:sl + W],
                        start=(c == 0), stop=(c == 7)),
                        reads=[onesB, sqb2], writes=[pb[1]])
                P2 = 32
                R2 = lambda i: rowsc[P2:P2 + 1, i * TA:i * TA + W]
                sch.op("vector", lambda e, R2=R2, W=W: e.tensor_scalar(
                    R2(0), pb[0].ap()[32:33, 0:W], 1.0 / 1024, None,
                    ALU.mult), reads=[pb[0]], writes=[rowsc])
                sch.op("vector", lambda e, R2=R2, W=W: e.tensor_scalar(
                    R2(1), pb[1].ap()[32:33, 0:W], 1.0 / 1024, None,
                    ALU.mult), reads=[pb[1]], writes=[rowsc])
                sch.op("vector", lambda e, R2=R2: e.tensor_tensor(
                    R2(2), R2(0), R2(0), ALU.mult),
                    reads=[rowsc], writes=[rowsc])
                sch.op("vector", lambda e, R2=R2: e.tensor_tensor(
                    R2(1), R2(1), R2(2), ALU.subtract),
                    reads=[rowsc], writes=[rowsc])
                sch.op("scalar", lambda e, R2=R2: e.activation(
                    R2(2), R2(1), AF.Ln, bias=EPS),
                    reads=[rowsc], sreads=[cnst], writes=[rowsc])
                sch.op("scalar", lambda e, R2=R2: e.activation(
                    R2(3), R2(2), AF.Exp, scale=-0.5),
                    reads=[rowsc], writes=[rowsc])
                sch.op("vector", lambda e, R2=R2: e.tensor_tensor(
                    R2(0), R2(0), R2(3), ALU.mult),
                    reads=[rowsc], writes=[rowsc])
                sch.op("tensor", lambda e, R2=R2, W=W: e.matmul(
                    pb[2].ap()[:, 0:W], onesR[32:33, :], R2(3),
                    start=True, stop=True),
                    reads=[onesR, rowsc], writes=[pb[2]])
                sch.op("tensor", lambda e, R2=R2, W=W: e.matmul(
                    pb[3].ap()[:, 0:W], onesR[32:33, :], R2(0),
                    start=True, stop=True),
                    reads=[onesR, rowsc], writes=[pb[3]])
                x2o = x2b[t % 2]
                for c in range(8):
                    sch.op("vector", lambda e, c=c, x2in=x2in, W=W:
                        e.tensor_tensor(
                            tmpg[:, 0:W],
                            x2in[:, c * C514 + 1:c * C514 + 1 + W],
                            pb[2].ap()[:, 0:W], ALU.mult),
                        reads=[x2in, pb[2]], writes=[tmpg])
                    sch.op("vector", lambda e, W=W: e.tensor_tensor(
                        tmpg[:, 0:W], tmpg[:, 0:W],
                        pb[3].ap()[:, 0:W], ALU.subtract),
                        reads=[tmpg, pb[3]], writes=[tmpg])
                    sch.op("scalar", lambda e, c=c, x2o=x2o, W=W: e.activation(
                        x2o[:, c * C514 + 1:c * C514 + 1 + W], tmpg[:, 0:W],
                        AF.Identity, bias=craw[:, B2 + c:B2 + c + 1],
                        scale=craw[:, G2 + c:G2 + c + 1]),
                        reads=[tmpg], sreads=[craw], writes=[x2o])
                if t == 0:
                    sch.op("vector", lambda e, x2o=x2o: e.tensor_scalar(
                        x2o[:], x2o[:], flg[:, 0:1], None, ALU.mult),
                        reads=[x2o], sreads=[flg], writes=[x2o])
                if t == NT - 1:
                    sch.op("vector", lambda e, x2o=x2o: e.tensor_scalar(
                        x2o[:], x2o[:], flg[:, 1:2], None, ALU.mult),
                        reads=[x2o], sreads=[flg], writes=[x2o])
                if t >= 1:
                    x2p = x2b[(t - 1) % 2]
                    sch.op("vector", lambda e, x2o=x2o, x2p=x2p: e.tensor_copy(
                        ch8(x2o, 8)[:, :, 0:1], ch8(x2p, 8)[:, :, 512:513]),
                        reads=[x2p], writes=[x2o])
                    sch.op("vector", lambda e, x2o=x2o, x2p=x2p: e.tensor_copy(
                        ch8(x2p, 8)[:, :, 513:514], ch8(x2o, 8)[:, :, 1:2]),
                        reads=[x2o], writes=[x2p])
                if 1 <= t <= 8:
                    sch.op("gpsimd", lambda e, t=t, x2o=x2o: e.dma_start(
                        out=hbm8(x2_s, 8)[:, :, (t - 1) * TA:t * TA],
                        in_=ch8(x2o, 8)[:, :, 1:1 + TA]),
                        reads=[x2o], dma=True)

            def mix2_stage(t):
                x2cur = x2b[t % 2]
                mix(x2cur, xmfb, TMF, CMF, D2M, D2MC)
                sch.op("sync", lambda e, t=t: e.dma_start(
                    out=hbm8(xmf_s, 8)[:, :, (t - 1) * TA:t * TA],
                    in_=ch8(xmfb)), reads=[xmfb], dma=True)

            WSLIM = 32
            for k in range(NT + 2):
                if k <= NT - 1:
                    x1_stage(k, W=WSLIM if k == NT - 1 else TA)
                if 1 <= k <= NT:
                    chain_stage(k - 1, W=WSLIM if k - 1 == NT - 1 else TA)
                if 1 <= k - 2 <= 8:
                    mix2_stage(k - 2)
                emit_precast(3 if k > 0 else 0)
            emit_precast(len(precast_chunks))

            sch.fence()
        with ExitStack() as eb:
            sb = lambda nme, sh, dt: eb.enter_context(
                nc.sbuf_tensor(nme, sh, dt))
            fk_b = sb("fk_b", [128, 8 * 4096], BF16)
            fv_b = sb("fv_b", [128, 32 * 1024], BF16)
            fr_b = sb("fr_b", [128, 8192], BF16)
            xfb_r = [sb(f"xfb{i}", [128, 8 * TB], BF16) for i in range(2)]
            x2rb_r = [sb(f"x2rb{i}", [128, 8 * TB], BF16) for i in range(2)]
            fkb = sb("fkb_t", [128, 32 * TB], BF16)
            sfb = sb("sfb", [128, 8 * TB], F32)
            tmpq = sb("tmpq", [128, TB], F32)
            rlf = sb("rlf", [128, TB], F32)

            sch.op("sync", lambda e: e.dma_start(
                out=ch8(fr_b, 8), in_=hbm8(fr16, 8)),
                writes=[fr_b], dma=True)
            sch.op("sync", lambda e: e.dma_start(
                out=ch8(fk_b, 8), in_=hbm8(fk16, 8)),
                writes=[fk_b], dma=True)
            sch.op("gpsimd", lambda e: e.dma_start(
                out=ch8(fv_b, 32), in_=hbm8(fv16, 32)),
                writes=[fv_b], dma=True)

            for t in range(16):
                c0 = TB * t
                xfb = xfb_r[t % 2]
                x2rb = x2rb_r[t % 2]
                sch.op("sync", lambda e, xfb=xfb, c0=c0: e.dma_start(
                    out=ch8(xfb), in_=hbm8(xmf_s, 8)[:, :, c0:c0 + TB]),
                    writes=[xfb], dma=True)
                sch.op("gpsimd", lambda e, x2rb=x2rb, c0=c0: e.dma_start(
                    out=ch8(x2rb), in_=hbm8(x2_s, 8)[:, :, c0:c0 + TB]),
                    writes=[x2rb], dma=True)
                for oc in range(8):
                    bk = oc % 2
                    for kc in range(8):
                        sch.op("tensor", lambda e, bk=bk, kc=kc, oc=oc,
                               xfb=xfb: e.matmul(
                                pb[bk].ap()[:, 0:TB],
                                fr_b[:, kc * 1024 + oc * 128:
                                     kc * 1024 + oc * 128 + 128],
                                xfb[:, kc * TB:(kc + 1) * TB],
                                start=(kc == 0), stop=(kc == 7)),
                            reads=[fr_b, xfb], writes=[pb[bk]])
                    sch.op("scalar", lambda e, oc=oc, bk=bk: e.activation(
                        sfb[:, oc * TB:(oc + 1) * TB], pb[bk].ap()[:, 0:TB],
                        AF.Sigmoid), reads=[pb[bk]], writes=[sfb])
                for hc in range(32):
                    bk = 2 + hc % 2
                    for kc in range(8):
                        sch.op("tensor", lambda e, bk=bk, kc=kc, hc=hc,
                               xfb=xfb: e.matmul(
                                pb[bk].ap()[:, 0:TB],
                                fk_b[:, kc * 4096 + hc * 128:
                                     kc * 4096 + hc * 128 + 128],
                                xfb[:, kc * TB:(kc + 1) * TB],
                                start=(kc == 0), stop=(kc == 7)),
                            reads=[fk_b, xfb], writes=[pb[bk]])
                    sch.op("scalar", lambda e, bk=bk: e.activation(
                        rlf[:], pb[bk].ap()[:, 0:TB], AF.Relu),
                        reads=[pb[bk]], writes=[rlf])
                    sch.op("vector", lambda e, hc=hc: e.tensor_tensor(
                        fkb[:, hc * TB:(hc + 1) * TB], rlf[:], rlf[:],
                        ALU.mult), reads=[rlf], writes=[fkb])
                for oc in range(8):
                    bk = 4 + oc % 2
                    for kc in range(32):
                        sch.op("tensor", lambda e, bk=bk, kc=kc, oc=oc:
                            e.matmul(
                                pb[bk].ap()[:, 0:TB],
                                fv_b[:, kc * 1024 + oc * 128:
                                     kc * 1024 + oc * 128 + 128],
                                fkb[:, kc * TB:(kc + 1) * TB],
                                start=(kc == 0), stop=(kc == 31)),
                            reads=[fv_b, fkb], writes=[pb[bk]])
                    sch.op("vector", lambda e, oc=oc, bk=bk: e.tensor_tensor(
                        tmpq[:], sfb[:, oc * TB:(oc + 1) * TB],
                        pb[bk].ap()[:, 0:TB], ALU.mult),
                        reads=[sfb, pb[bk]], writes=[tmpq])
                    sch.op("vector", lambda e, oc=oc, x2rb=x2rb:
                        e.tensor_tensor(
                            x2rb[:, oc * TB:(oc + 1) * TB], tmpq[:],
                            x2rb[:, oc * TB:(oc + 1) * TB], ALU.add),
                        reads=[tmpq, x2rb], writes=[x2rb])
                    sch.op("sync", lambda e, oc=oc, c0=c0, x2rb=x2rb:
                        e.dma_start(
                            out=hbm8(attn_s, 8)[:, oc, c0:c0 + TB],
                            in_=x2rb[:, oc * TB:(oc + 1) * TB]),
                        reads=[x2rb], dma=True)

            sch.fence()
        with ExitStack() as ec:
            sb = lambda nme, sh, dt: ec.enter_context(
                nc.sbuf_tensor(nme, sh, dt))
            w1_8 = sb("w1_8", [128, 32768], FP8)
            w2_8 = sb("w2_8", [128, 32768], FP8)
            stt_b = [sb(f"sttb{i}", [64, TC], F32) for i in range(2)]
            xcb = [sb(f"xcb{i}", [128, 8 * TC], F32) for i in range(2)]
            acb = [sb(f"acb{i}", [128, 8 * TC], BF16) for i in range(2)]
            mkb = [sb(f"mkb{i}", [128, 8 * TC], BF16) for i in range(2)]
            xqb = sb("xqb", [128, 8 * TC], FP8)
            hbb = sb("hbb", [128, 4 * TC], BF16)
            h8b = sb("h8b", [128, 32 * TC], FP8)
            y2b = sb("y2b", [128, 8 * TC], BF16)
            outb = [sb(f"outb{i}", [128, 8 * TC], F32) for i in range(2)]
            tmpc = sb("tmpc", [128, TC], F32)

            sch.op("sync", lambda e: e.dma_start(out=w1_8[:], in_=w1q8),
                   writes=[w1_8], dma=True)
            sch.op("sync", lambda e: e.dma_start(out=w2_8[:], in_=w2q8),
                   writes=[w2_8], dma=True)

            def wdr(wsb, base):
                return wsb[:, base:base + 256].rearrange(
                    "p (j n) -> p j n", j=2)

            def adr(asb, gk):
                return asb[:, 2 * gk * TC:(2 * gk + 2) * TC].rearrange(
                    "p (j m) -> p j m", j=2)

            for t in range(8):
                c0 = TC * t
                xc = xcb[t % 2]
                ac = acb[t % 2]
                mk = mkb[t % 2]
                ob = outb[t % 2]
                stt = stt_b[t % 2]
                sch.op("sync", lambda e, xc=xc, c0=c0: e.dma_start(
                    out=ch8(xc), in_=hbm8(xh, 8)[:, :, H + c0:H + c0 + TC]),
                    writes=[xc], dma=True)
                sch.op("gpsimd", lambda e, ac=ac, c0=c0: e.dma_start(
                    out=ch8(ac), in_=hbm8(attn_s, 8)[:, :, c0:c0 + TC]),
                    writes=[ac], dma=True)
                sch.op("sync", lambda e, mk=mk, c0=c0: e.dma_start(
                    out=ch8(mk), in_=hbm8(mask_d, 8)[:, :, c0:c0 + TC]),
                    writes=[mk], dma=True)
                sch.op("sync", lambda e, stt=stt, c0=c0: e.dma_start(
                    out=stt[0:1, :], in_=stat_s[0:1, c0:c0 + TC]),
                    writes=[stt], dma=True)
                sch.op("sync", lambda e, stt=stt, c0=c0: e.dma_start(
                    out=stt[32:33, :], in_=stat_s[1:2, c0:c0 + TC]),
                    writes=[stt], dma=True)
                sch.op("tensor", lambda e, stt=stt: e.matmul(
                    pb[2].ap()[:, 0:TC], onesR[0:1, :],
                    stt[0:1, :], start=True, stop=True),
                    reads=[onesR, stt], writes=[pb[2]])
                sch.op("tensor", lambda e, stt=stt: e.matmul(
                    pb[3].ap()[:, 0:TC], onesR[32:33, :],
                    stt[32:33, :], start=True, stop=True),
                    reads=[onesR, stt], writes=[pb[3]])
                for c in range(8):
                    sch.op("vector", lambda e, c=c, xc=xc: e.tensor_tensor(
                        tmpc[:], xc[:, c * TC:(c + 1) * TC],
                        pb[2].ap()[:, 0:TC], ALU.mult),
                        reads=[xc, pb[2]], writes=[tmpc])
                    sch.op("vector", lambda e, c=c: e.tensor_tensor(
                        tmpc[:], tmpc[:],
                        pb[3].ap()[:, 0:TC], ALU.subtract),
                        reads=[tmpc, pb[3]], writes=[tmpc])
                    sch.op("vector", lambda e, c=c: e.tensor_copy(
                        xqb[:, c * TC:(c + 1) * TC], tmpc[:]),
                        reads=[tmpc], writes=[xqb])
                for hc in range(32):
                    bk = 4 + hc % 2
                    for gk in range(4):
                        sch.op("tensor", lambda e, bk=bk, gk=gk, hc=hc:
                            e.matmul(
                                pb[bk].ap()[:, 0:TC],
                                wdr(w1_8, (gk * 32 + hc) * 256),
                                adr(xqb, gk),
                                start=(gk == 0), stop=(gk == 3),
                                perf_mode=PM.DoubleRow),
                            reads=[w1_8, xqb], writes=[pb[bk]])
                    hs = (hc % 4) * TC
                    sch.op("scalar", lambda e, hc=hc, bk=bk, hs=hs:
                        e.activation(
                            hbb[:, hs:hs + TC], pb[bk].ap()[:, 0:TC],
                            AF.Relu, bias=b1q[:, hc:hc + 1], scale=1.0),
                        reads=[pb[bk]], sreads=[b1q], writes=[hbb])
                    sch.op("vector", lambda e, hc=hc, hs=hs: e.tensor_copy(
                        h8b[:, hc * TC:(hc + 1) * TC],
                        hbb[:, hs:hs + TC]),
                        reads=[hbb], writes=[h8b])
                for oc in range(8):
                    bk = 6 + oc % 2
                    for gk in range(16):
                        sch.op("tensor", lambda e, bk=bk, gk=gk, oc=oc:
                            e.matmul(
                                pb[bk].ap()[:, 0:TC],
                                wdr(w2_8, (gk * 8 + oc) * 256),
                                adr(h8b, gk),
                                start=(gk == 0), stop=(gk == 15),
                                perf_mode=PM.DoubleRow),
                            reads=[w2_8, h8b], writes=[pb[bk]])
                    sch.op("scalar", lambda e, oc=oc, bk=bk: e.activation(
                        y2b[:, oc * TC:(oc + 1) * TC], pb[bk].ap()[:, 0:TC],
                        AF.Identity, bias=b2q[:, oc:oc + 1], scale=1.0),
                        reads=[pb[bk]], sreads=[b2q], writes=[y2b])
                    sch.op("vector", lambda e, oc=oc, xc=xc, ac=ac, ob=ob:
                        e.tensor_tensor(
                            ob[:, oc * TC:(oc + 1) * TC],
                            xc[:, oc * TC:(oc + 1) * TC],
                            ac[:, oc * TC:(oc + 1) * TC], ALU.add),
                        reads=[xc, ac], writes=[ob])
                    sch.op("vector", lambda e, oc=oc, mk=mk: e.tensor_tensor(
                        tmpc[:], y2b[:, oc * TC:(oc + 1) * TC],
                        mk[:, oc * TC:(oc + 1) * TC], ALU.mult),
                        reads=[y2b, mk], writes=[tmpc])
                    sch.op("vector", lambda e, oc=oc, ob=ob: e.tensor_tensor(
                        ob[:, oc * TC:(oc + 1) * TC],
                        ob[:, oc * TC:(oc + 1) * TC], tmpc[:], ALU.add),
                        reads=[ob, tmpc], writes=[ob])
                sch.op("gpsimd", lambda e, ob=ob, c0=c0: e.dma_start(
                    out=hbm8(out_d, 8)[:, :, c0:c0 + TC], in_=ch8(ob)),
                    reads=[ob], dma=True)

            sch.fence()
            sch.op("gpsimd", lambda e: e.nop())

        sch.finalize()
        with nc.Block() as block:
            sch.flush(block, sems)
    return nc


_NC_CACHE = [None]


def _stage(inputs):
    x = np.asarray(inputs["L0_attn_x"], dtype=np.float32)
    B, T, C = x.shape
    assert (B, T, C) == (2, 16384, 1024)

    def wt(a):
        return np.ascontiguousarray(np.asarray(a, np.float32).T)

    def drpack(wT, ngk, noc):
        A = wT.reshape(ngk, 2, 128, noc, 128)
        return np.ascontiguousarray(
            A.transpose(2, 0, 3, 1, 4).reshape(128, -1))

    weights = dict(
        wkt=wt(inputs["L0_attn_Wk"]), wvt=wt(inputs["L0_attn_Wv"]),
        wrt=wt(inputs["L0_attn_Wr"]), wot=wt(inputs["L0_attn_Wo"]),
        frt=wt(inputs["L0_attn_Fr"]), fkt=wt(inputs["L0_attn_Fk"]),
        fvt=wt(inputs["L0_attn_Fv"]),
        w1d=drpack(wt(inputs["L0_ffn_W1"]), 4, 32),
        w2d=drpack(wt(inputs["L0_ffn_W2"]), 16, 8))

    def pack8(v, ncol):
        return np.ascontiguousarray(
            np.asarray(v, np.float32).reshape(ncol, 128).T)

    craw = np.zeros((128, 64), np.float32)
    for off, nme in ((TM, "L0_attn_att_time_mix"),
                     (CM, "L0_attn_att_combined_mix"),
                     (TMF, "L0_attn_ffn_time_mix"),
                     (CMF, "L0_attn_ffn_combined_mix"),
                     (G1, "L0_attn_ln1_g"), (B1, "L0_attn_ln1_b"),
                     (G2, "L0_attn_ln2_g"), (B2, "L0_attn_ln2_b")):
        craw[:, off:off + 8] = pack8(np.asarray(inputs[nme]).reshape(-1), 8)
    ctd = np.zeros((128, 16), np.float32)
    ctd[:, 0:8] = pack8(inputs["L0_attn_time_decay"], 8)
    ctd[:, 8:16] = pack8(inputs["L0_attn_time_first"], 8)
    b1r = pack8(inputs["L0_ffn_b1"], 32)
    b2r = pack8(inputs["L0_ffn_b2"], 8)

    mask = _bernoulli_mask(42, 0.8, (B, T, C))
    maskv = np.where(mask, np.float32(1.25),
                     np.float32(0.0)).astype(ml_dtypes.bfloat16)

    in_maps = []
    for core in range(NCORES):
        b, j = core // 4, core % 4
        t0 = j * CH
        lo, hi = t0 - H, t0 + CH + 2
        xr = np.zeros((NXC, 1024), np.float32)
        s0, s1 = max(lo, 0), min(hi, T)
        xr[s0 - lo:s1 - lo] = x[b, s0:s1]
        flgv = np.zeros((128, 2), np.float32)
        flgv[:, 0] = 0.0 if j == 0 else 1.0
        flgv[:, 1] = 0.0 if j == 3 else 1.0
        m = dict(weights)
        m.update(xh=np.ascontiguousarray(xr.T), craw=craw, ctd=ctd,
                 b1r=b1r, b2r=b2r, flg=flgv,
                 maskv=np.ascontiguousarray(maskv[b, t0:t0 + CH].T))
        in_maps.append(m)
    return in_maps


def kernel(**inputs):
    x = np.asarray(inputs["L0_attn_x"], dtype=np.float32)
    B, T, C = x.shape
    in_maps = _stage(inputs)
    if _NC_CACHE[0] is None:
        _NC_CACHE[0] = _build()
    nc = _NC_CACHE[0]
    res = run_bass_kernel_spmd(nc, in_maps, core_ids=list(range(NCORES)))

    out = np.empty((B, T, C), np.float32)
    for core in range(NCORES):
        b, j = core // 4, core % 4
        out[b, j * CH:(j + 1) * CH] = np.asarray(res.results[core]["out"]).T
    return out
